# version 15
# baseline (speedup 1.0000x reference)
import contextlib

import numpy as np
import ml_dtypes
import concourse.bass as bass
import concourse.mybir as mybir
from concourse.bass_utils import run_bass_kernel_spmd

F32 = mybir.dt.float32
BF16 = mybir.dt.bfloat16
AF = mybir.ActivationFunctionType
ALU = mybir.AluOpType
AX = mybir.AxisListType

NCORES = 8
CDIM = 1024
EDIM = 2048
NEXP, PSLOT = 64, 32
NPAIR = 8
EPS = 1e-5

C_INVC, C_EPS, C_NHALF, C_FFNB, C_ATTNS, C_ATTNB = 0, 1, 2, 3, 4, 5
C_CLIPLO, C_CLIPHI, C_ONE, C_ZERO, C_NEG1 = 6, 7, 8, 9, 10
NCONST = 11
CONST_VALS = [1.0 / CDIM, EPS, -0.5, -2.0, 0.125, -4.0,
              -33000.0, 65000.0, 1.0, 0.0, -1.0]


class Prog:

    ENGS = ("pe", "act", "dve", "pop", "sp")
    QDEPTH = {"sp": 8, "act": 6, "pop": 4}

    def __init__(self, nc):
        self.nc = nc
        self.ops = {e: [] for e in self.ENGS}
        self.cnt = {s: 0 for s in self.ENGS}
        for e, d in self.QDEPTH.items():
            for j in range(d):
                self.cnt[f"q_{e}{j}"] = 0
        self.qn = {e: 0 for e in self.QDEPTH}

    def op(self, eng, builder, waits=(), inc=1, sem=None):
        sem = sem or eng
        self.cnt[sem] += inc
        tok = (sem, self.cnt[sem])
        self.ops[eng].append((builder, inc, sem, [w for w in waits if w]))
        return tok

    def dma(self, eng, builder, waits=()):
        j = self.qn[eng] % self.QDEPTH[eng]
        self.qn[eng] += 1
        sem = f"q_{eng}{j}"
        auto = [(sem, self.cnt[sem])] if self.cnt[sem] else []
        return self.op(eng, builder, waits=list(waits) + auto, inc=16, sem=sem)

    def last(self, sem):
        return (sem, self.cnt[sem])

    def qlast(self, eng):
        return [(f"q_{eng}{j}", self.cnt[f"q_{eng}{j}"])
                for j in range(self.QDEPTH[eng])
                if self.cnt[f"q_{eng}{j}"]]

    def emit(self):
        nc = self.nc
        es = contextlib.ExitStack()
        sems = {s: es.enter_context(nc.semaphore("sem_" + s))
                for s in self.cnt}
        with nc.Block() as blk:
            sect = {"pe": blk.tensor, "act": blk.scalar, "dve": blk.vector,
                    "pop": blk.gpsimd, "sp": blk.sync}

            def mk(ename):
                ops = self.ops[ename]

                def run(eng):
                    waited = {}
                    for builder, inc, sem, waits in ops:
                        for (s, v) in waits:
                            if waited.get(s, 0) < v:
                                eng.wait_ge(sems[s], v)
                                waited[s] = v
                        inst = builder(eng)
                        if inst is not None and inc > 0:
                            inst.then_inc(sems[sem], inc)
                return run

            for e in self.ENGS:
                sect[e](mk(e))
        es.close()


def _alloc(es, nc, name, shape, dt):
    return es.enter_context(nc.sbuf_tensor(name, shape, dt))


def _psum(es, nc, name, shape, dt=F32):
    return es.enter_context(nc.psum_tensor(name, shape, dt))


def _ln_chain(p, CST, src, st, sqs, dst_act, dst_fn, waits):
    s1, sq, m, msq, t1, var, lnv, rstd, nmr = (st[:, j:j + 1] for j in range(9))
    ta = p.op("dve", lambda e: e.reduce_sum(s1, src, AX.X), waits=waits)
    tb = p.op("act", lambda e: e.activation(sqs, src, AF.Square, accum_out=sq),
              waits=waits)
    tm = p.op("dve", lambda e: e.tensor_scalar_mul(m, s1, CST(C_INVC)), waits=[ta])
    tmsq = p.op("dve", lambda e: e.tensor_scalar_mul(msq, sq, CST(C_INVC)), waits=[tb])
    tt1 = p.op("dve", lambda e: e.tensor_mul(t1, m, m), waits=[tm])
    tvar = p.op("dve", lambda e: e.tensor_sub(var, msq, t1), waits=[tmsq, tt1])
    tlnv = p.op("act", lambda e: e.activation(lnv, var, AF.Ln, bias=CST(C_EPS)),
                waits=[tvar])
    trstd = p.op("act", lambda e: e.activation(rstd, lnv, AF.Exp, scale=CST(C_NHALF)),
                 waits=[tlnv])
    tnmr = p.op("dve", lambda e: e.tensor_scalar(nmr, m, rstd, CST(C_NEG1),
                                                 ALU.mult, ALU.mult), waits=[trstd])
    return p.op("act", lambda e: e.activation(dst_act, src, dst_fn,
                                              scale=rstd, bias=nmr),
                waits=[tnmr] + list(waits))


def build_l1(T=4096):
    TC = T // 128
    NQT = T // 512
    spc = min(8, TC)
    XN_RING = spc + 4
    LAG = 4

    nc = bass.Bass("TRN2", target_bir_lowering=False)
    x_i = nc.dram_tensor("x", [T, CDIM], F32, kind="ExternalInput").ap()
    phiw_i = nc.dram_tensor("phiw", [EDIM, CDIM], F32, kind="ExternalInput").ap()
    phib_i = nc.dram_tensor("phib", [1, EDIM], F32, kind="ExternalInput").ap()
    wq_i = nc.dram_tensor("wq", [CDIM, CDIM], F32, kind="ExternalInput").ap()
    wk_i = nc.dram_tensor("wk", [CDIM, CDIM], F32, kind="ExternalInput").ap()
    wv_i = nc.dram_tensor("wv", [CDIM, CDIM], F32, kind="ExternalInput").ap()
    bqc_i = nc.dram_tensor("bqc", [CDIM, 1], F32, kind="ExternalInput").ap()
    bkc_i = nc.dram_tensor("bkc", [CDIM, 1], F32, kind="ExternalInput").ap()
    bvr_i = nc.dram_tensor("bvr", [1, CDIM], F32, kind="ExternalInput").ap()
    tri_i = nc.dram_tensor("tri", [128, 128], BF16, kind="ExternalInput").ap()
    cst_i = nc.dram_tensor("consts", [128, NCONST], F32, kind="ExternalInput").ap()
    xt_o = nc.dram_tensor("xt_part", [EDIM, CDIM], F32, kind="ExternalOutput").ap()
    csp_o = nc.dram_tensor("c_sp", [T, EDIM], BF16, kind="ExternalOutput").ap()
    yt_o = nc.dram_tensor("yt_sp", [CDIM, T], BF16, kind="ExternalOutput").ap()
    xnsp = nc.dram_tensor("xn_sp", [T, CDIM], BF16).ap()
    phibf = nc.dram_tensor("phi_bf", [EDIM, CDIM], BF16).ap()

    p = Prog(nc)
    es = contextlib.ExitStack()
    cst = _alloc(es, nc, "cst", [128, NCONST], F32)
    ones_r = _alloc(es, nc, "ones_r", [1, 128], BF16)
    tri = _alloc(es, nc, "tri_sb", [128, 128], BF16)

    def CST(j):
        return cst[:, j:j + 1]

    esA = contextlib.ExitStack()
    phiT = _alloc(esA, nc, "phiT", [128, 8 * EDIM], BF16)
    phibb = _alloc(esA, nc, "phibb", [1, EDIM], BF16)
    esStg = contextlib.ExitStack()
    phibf32 = _alloc(esStg, nc, "phibf32", [1, EDIM], F32)
    pstg = _alloc(esStg, nc, "pstg", [128, 4 * CDIM], F32)
    pstgb = _alloc(esStg, nc, "pstgb", [128, 4 * CDIM], BF16)
    esStg.close()
    xb = _alloc(esA, nc, "xb", [128, 3 * CDIM], F32)
    stt = _alloc(esA, nc, "stt", [128, 3 * 12], F32)
    xnb = _alloc(esA, nc, "xnb", [128, XN_RING * CDIM], BF16)
    xnTs = _alloc(esA, nc, "xnTs", [128, 2 * 512 * 8], BF16)
    Et = _alloc(esA, nc, "Et", [128, 3 * EDIM], BF16)
    dsum = _alloc(esA, nc, "dsum", [128, 2 * 64], F32)
    csum = _alloc(esA, nc, "csum", [128, 2 * 32], F32)
    drec = _alloc(esA, nc, "drec", [128, 2 * 64], F32)
    crec = _alloc(esA, nc, "crec", [128, 2 * 32], F32)
    Dbuf = _alloc(esA, nc, "Dbuf", [128, spc * EDIM], BF16)
    Cbuf = _alloc(esA, nc, "Cbuf", [128, 2 * EDIM], BF16)
    xtacc = _alloc(esA, nc, "xtacc", [128, 16 * CDIM], F32)

    esP = contextlib.ExitStack()
    P1 = _psum(esP, nc, "P1", [128, 2048])
    P2a = _psum(esP, nc, "P2a", [128, 512])
    P2b = _psum(esP, nc, "P2b", [128, 512])
    P4 = _psum(esP, nc, "P4", [128, 1024])

    t_cst = p.dma("sp", lambda e: e.dma_start(out=cst[:, :], in_=cst_i))
    t_tri = p.dma("sp", lambda e: e.dma_start(out=tri[:, :], in_=tri_i))
    t_ones = p.op("dve", lambda e: e.memset(ones_r[:, :], 1.0))
    t_phibld = p.dma("sp", lambda e: e.dma_start(out=phibf32[:, :], in_=phib_i))
    t_phibb = p.op("dve", lambda e: e.tensor_copy(phibb[:, :], phibf32[:, :]),
                   waits=[t_phibld])

    t_sp_prev = None
    t_cv_prev = None
    for r in range(4):
        rows = phiw_i[512 * r: 512 * (r + 1), :]
        tl = p.dma("pop", lambda e, rows=rows: e.dma_start(
            out=pstg[:, :].rearrange("p (c m) -> p c m", c=4),
            in_=rows.rearrange("(c p) m -> p c m", p=128)),
            waits=[t_cv_prev] if t_cv_prev else [])
        tc = p.op("dve", lambda e: e.tensor_copy(pstgb[:, :], pstg[:, :]),
                  waits=[tl, t_sp_prev] if t_sp_prev else [tl])
        t_sp_prev = p.dma("pop", lambda e, r=r: e.dma_start(
            out=phibf[512 * r: 512 * (r + 1), :].rearrange("(c p) m -> p c m", p=128),
            in_=pstgb[:, :].rearrange("p (c m) -> p c m", c=4)),
            waits=[tc])
        t_cv_prev = tc
    t_phiT = []
    for c in range(8):
        t_phiT.append(p.dma("act", lambda e, c=c: e.dma_start(
            out=phiT[:, EDIM * c: EDIM * (c + 1)],
            in_=phibf[:, 128 * c: 128 * (c + 1)], transpose=True),
            waits=[t_sp_prev]))
    t_stg_free = [t_cv_prev, t_sp_prev, t_phibb]

    tok_x = [None] * TC
    tok_xn = [None] * TC
    tok_xnsp = [None] * TC
    tok_sup = [None] * ((TC + 3) // 4)
    tok_E = [None] * TC
    tok_D = [None] * TC
    tok_C = [None] * TC
    tok_csp = [None] * TC
    tok_logits = [None] * TC
    tok_xtmm = {}
    tok_xtadd = {}
    xt_bank_last = {0: None, 1: None, 2: None, 3: None}
    n_grp = (TC + spc - 1) // spc


    def x_load(t):
        sl = t % 3
        xv = xb[:, CDIM * sl: CDIM * (sl + 1)]
        w = [t_cst] + ([tok_xn[t - 3]] if t >= 3 else [])
        if t == 0:
            w += t_stg_free
        tok_x[t] = p.dma("sp", lambda e, xv=xv, t=t: e.dma_start(
            out=xv, in_=x_i[128 * t: 128 * (t + 1), :]), waits=w)

    def ln_part(t):
        sl = t % 3
        xv = xb[:, CDIM * sl: CDIM * (sl + 1)]
        st = stt[:, 12 * sl: 12 * (sl + 1)]
        xnv = xnb[:, CDIM * (t % XN_RING): CDIM * (t % XN_RING + 1)]
        w_xn = [tok_x[t]] + ([tok_xn[t - 3]] if t >= 3 else [])
        if t >= XN_RING:
            gprev = (t - XN_RING) // spc
            while (gprev, 15, 1) not in tok_xtmm:
                drain_xt(1)
            w_xn.append(tok_xtmm[(gprev, 15, 1)])
        tok_xn[t] = _ln_chain(p, CST, xv, st, xnv, xnv, AF.Identity, w_xn)
        tok_xnsp[t] = p.dma("sp", lambda e: e.dma_start(
            out=xnsp[128 * t: 128 * (t + 1), :], in_=xnv), waits=[tok_xn[t]])
        if t % 4 == 3:
            s = t // 4
            sv = xnTs[:, 4096 * (s % 2): 4096 * (s % 2 + 1)]
            w_sup = [tok_xnsp[4 * (t // 4) + j] for j in range(4)]
            if s >= 2:
                w_sup.append(tok_logits[4 * (s - 2) + 3])
            tks = []
            for c in range(8):
                tks.append(p.dma("act", lambda e, sv=sv, s=s, c=c: e.dma_start(
                    out=sv[:, 512 * c: 512 * (c + 1)],
                    in_=xnsp[512 * s: 512 * (s + 1), 128 * c: 128 * (c + 1)],
                    transpose=True), waits=w_sup))
            tok_sup[s] = tks

    def ffn_part(t):
        s = t // 4
        sv = xnTs[:, 4096 * (s % 2): 4096 * (s % 2 + 1)]
        esl = Et[:, EDIM * (t % 3): EDIM * (t % 3 + 1)]
        t_eh = []
        for h in range(2):
            def mk_lg(e, sv=sv, t=t, h=h):
                last = None
                for c in range(8):
                    lh = sv[:, 512 * c + 128 * (t % 4): 512 * c + 128 * (t % 4) + 128]
                    for g in (2 * h, 2 * h + 1):
                        last = e.matmul(
                            P1[:, 512 * g: 512 * (g + 1)], lh,
                            phiT[:, EDIM * c + 512 * g: EDIM * c + 512 * (g + 1)],
                            start=(c == 0), stop=False)
                for g in (2 * h, 2 * h + 1):
                    last = e.matmul(P1[:, 512 * g: 512 * (g + 1)], ones_r[:, :],
                                    phibb[:, 512 * g: 512 * (g + 1)],
                                    start=False, stop=True)
                return last
            w_lg = tok_sup[s] + t_phiT + [t_phibb, t_ones]
            if t >= 1:
                w_lg.append(ffn_part.exp_h[h])
            t_lg = p.op("pe", mk_lg, waits=w_lg)
            w_E = [t_lg] + ([tok_C[t - 3], tok_D[t - 3]] if t >= 3 else [])
            te = p.op("act", lambda e, esl=esl, h=h: e.activation(
                esl[:, 1024 * h: 1024 * (h + 1)], P1[:, 1024 * h: 1024 * (h + 1)],
                AF.Exp, bias=CST(C_FFNB)), waits=w_E)
            t_eh.append(te)
            ffn_part.exp_h[h] = te
        tok_logits[t] = t_lg
        tok_E[t] = t_eh[1]

        dsl = dsum[:, 64 * (t % 2): 64 * (t % 2 + 1)]
        csl = csum[:, 32 * (t % 2): 32 * (t % 2 + 1)]
        drl = drec[:, 64 * (t % 2): 64 * (t % 2 + 1)]
        crl = crec[:, 32 * (t % 2): 32 * (t % 2 + 1)]
        w_slot = [tok_D[t - 2], tok_C[t - 2]] if t >= 2 else []
        td1 = p.op("dve", lambda e, dsl=dsl, esl=esl: e.reduce_sum(
            dsl, esl.rearrange("p (n w) -> p n w", w=PSLOT), AX.X),
            waits=[t_eh[0], t_eh[1]] + w_slot)
        tc1 = p.op("dve", lambda e, csl=csl, esl=esl: e.reduce_sum(
            csl, esl.rearrange("p (n w) -> p w n", w=PSLOT), AX.X),
            waits=[t_eh[0], t_eh[1]] + w_slot)
        td2 = p.op("dve", lambda e, drl=drl, dsl=dsl: e.reciprocal(drl, dsl),
                   waits=[td1])
        tc2 = p.op("dve", lambda e, crl=crl, csl=csl: e.reciprocal(crl, csl),
                   waits=[tc1])
        dv = Dbuf[:, EDIM * (t % spc): EDIM * (t % spc + 1)]
        w_D = [td2]
        if t >= spc:
            gp = (t - spc) // spc
            while (gp, 15, 1) not in tok_xtmm:
                drain_xt(1)
            w_D.append(tok_xtmm[(gp, 15, 1)])
        tok_D[t] = p.op("dve", lambda e, dv=dv, esl=esl, drl=drl: e.tensor_tensor(
            dv.rearrange("p (n w) -> p n w", w=PSLOT),
            esl.rearrange("p (n w) -> p n w", w=PSLOT),
            drl.unsqueeze(2).broadcast_to([128, NEXP, PSLOT]), ALU.mult), waits=w_D)
        cv = Cbuf[:, EDIM * (t % 2): EDIM * (t % 2 + 1)]
        w_C = [tc2] + ([tok_csp[t - 2]] if t >= 2 else [])
        tok_C[t] = p.op("dve", lambda e, cv=cv, esl=esl, crl=crl: e.tensor_tensor(
            cv.rearrange("p (n w) -> p n w", w=PSLOT),
            esl.rearrange("p (n w) -> p n w", w=PSLOT),
            crl.unsqueeze(1).broadcast_to([128, NEXP, PSLOT]), ALU.mult), waits=w_C)
        tok_csp[t] = p.dma("sp", lambda e, cv=cv, t=t: e.dma_start(
            out=csp_o[128 * t: 128 * (t + 1), :], in_=cv), waits=[tok_C[t]])

        if (t + 1) % spc == 0 or t == TC - 1:
            g = t // spc
            g_lo = g * spc
            g_n = t - g_lo + 1
            for et in range(16):
                for dh in range(2):
                    xt_tasks.append((g, g_lo, g_n, et, dh, tok_D[t]))

    XT_BANKS = (P2a[:, 0:512], P2b[:, 0:512], P4[:, 0:512], P4[:, 512:1024])

    def drain_xt(k):
        while xt_tasks and k > 0:
            k -= 1
            g, g_lo, g_n, et, dh, t_d = xt_tasks.pop(0)
            bk = drain_xt.n % 4
            drain_xt.n += 1
            pt = XT_BANKS[bk]
            w_mm = [t_d, xt_bank_last[bk]]

            def mk_xt(e, g_lo=g_lo, g_n=g_n, et=et, dh=dh, pt=pt):
                last = None
                for j in range(g_n):
                    tch = g_lo + j
                    dvj = Dbuf[:, EDIM * (tch % spc) + 128 * et:
                               EDIM * (tch % spc) + 128 * (et + 1)]
                    xnj = xnb[:, CDIM * (tch % XN_RING) + 512 * dh:
                              CDIM * (tch % XN_RING) + 512 * (dh + 1)]
                    last = e.matmul(pt, dvj, xnj,
                                    start=(j == 0), stop=(j == g_n - 1))
                return last
            tok_xtmm[(g, et, dh)] = p.op("pe", mk_xt, waits=w_mm)
            dst = xtacc[:, CDIM * et + 512 * dh: CDIM * et + 512 * (dh + 1)]
            if g == 0:
                tk = p.op("dve", lambda e, dst=dst, pt=pt: e.tensor_copy(
                    dst, pt), waits=[tok_xtmm[(g, et, dh)]])
            else:
                tk = p.op("dve", lambda e, dst=dst, pt=pt: e.tensor_add(
                    dst, dst, pt),
                    waits=[tok_xtmm[(g, et, dh)], tok_xtadd[(g - 1, et, dh)]])
            tok_xtadd[(g, et, dh)] = tk
            xt_bank_last[bk] = tk

    drain_xt.n = 0
    xt_tasks = []
    ffn_part.exp_h = {0: None, 1: None}
    ffn_part.exp_h = {0: None, 1: None}
    for it in range(TC + LAG):
        if it == 0:
            x_load(0)
            x_load(1)
            x_load(2)
        elif it + 2 < TC:
            x_load(it + 2)
        if it >= LAG:
            ffn_part(it - LAG)
            drain_xt(5)
        if it < TC:
            ln_part(it)

    drain_xt(10 ** 9)
    for et in range(16):
        w = [tok_xtadd[(n_grp - 1, et, 0)], tok_xtadd[(n_grp - 1, et, 1)]]
        p.dma("sp", lambda e, et=et: e.dma_start(
            out=xt_o[128 * et: 128 * (et + 1), :],
            in_=xtacc[:, CDIM * et: CDIM * (et + 1)]), waits=w)

    bar = ([p.last(s) for s in ("pe", "act", "dve", "pop", "sp")]
           + p.qlast("sp") + p.qlast("act") + p.qlast("pop"))
    esA.close()

    esB = contextlib.ExitStack()
    xnT = _alloc(esB, nc, "xnT", [128, 8 * T], BF16)
    Vp = _alloc(esB, nc, "Vp", [128, TC * NPAIR * 130], BF16)
    esW = contextlib.ExitStack()
    wvf = _alloc(esW, nc, "wvf", [128, 2 * 4096], F32)
    wvb = _alloc(esW, nc, "wvb", [128, 8 * CDIM], BF16)
    bvf = _alloc(esW, nc, "bvf", [1, CDIM], F32)
    bvb = _alloc(esW, nc, "bvb", [1, CDIM], BF16)

    t_xnT = []
    for c in range(8):
        t_xnT.append(p.dma("act", lambda e, c=c: e.dma_start(
            out=xnT[:, T * c: T * (c + 1)],
            in_=xnsp[:, 128 * c: 128 * (c + 1)], transpose=True), waits=bar))
    t_wv = []
    for r in range(2):
        tl = p.dma("pop", lambda e, r=r: e.dma_start(
            out=wvf[:, 4096 * r: 4096 * (r + 1)].rearrange("p (c m) -> p c m", c=4),
            in_=wv_i[512 * r: 512 * (r + 1), :].rearrange("(c p) m -> p c m", p=128)),
            waits=bar)
        t_wv.append(p.op("dve", lambda e, r=r: e.tensor_copy(
            wvb[:, 4096 * r: 4096 * (r + 1)], wvf[:, 4096 * r: 4096 * (r + 1)]),
            waits=[tl] + bar))
    t_bv = p.dma("pop", lambda e: e.dma_start(out=bvf[:, :], in_=bvr_i), waits=bar)
    t_bvb = p.op("dve", lambda e: e.tensor_copy(bvb[:, :], bvf[:, :]),
                 waits=[t_bv] + bar)
    t_vones = p.op("dve", lambda e: e.memset(
        Vp[:, :].rearrange("p (a s) -> p a s", s=65)[:, :, 64:65], 1.0), waits=bar)

    prev_vp = None
    for tch in range(TC):
        for hf in range(2):
            pt = P4[:, 512 * hf: 512 * (hf + 1)]

            def mk_v(e, tch=tch, hf=hf, pt=pt):
                last = None
                for c in range(8):
                    last = e.matmul(
                        pt, xnT[:, T * c + 128 * tch: T * c + 128 * (tch + 1)],
                        wvb[:, CDIM * c + 512 * hf: CDIM * c + 512 * (hf + 1)],
                        start=(c == 0), stop=False)
                return e.matmul(pt, ones_r[:, :], bvb[:, 512 * hf: 512 * (hf + 1)],
                                start=False, stop=True)
            w_v = t_xnT + [t_wv[-1], t_bvb, t_ones, prev_vp]
            tmm = p.op("pe", mk_v, waits=w_v)
            t_vp_mm_last = tmm
            base = 1040 * tch + 520 * hf
            prev_vp = p.op("dve", lambda e, base=base, pt=pt: e.tensor_copy(
                Vp[:, base: base + 520].rearrange(
                    "p (a b s) -> p a b s", a=4, b=2)[:, :, :, 0:64],
                pt.rearrange("p (a b s) -> p a b s", a=4, b=2)),
                waits=[tmm, t_vones])
    t_vp_done = prev_vp

    esW.close()
    qtb = _alloc(esB, nc, "qtb", [128, 2 * T], BF16)
    ktb = _alloc(esB, nc, "ktb", [128, 2 * T], BF16)
    wqf = _alloc(esB, nc, "wqf", [128, 2 * CDIM], F32)
    wkf = _alloc(esB, nc, "wkf", [128, 2 * CDIM], F32)
    wqb = _alloc(esB, nc, "wqb", [128, 2 * CDIM], BF16)
    wkb = _alloc(esB, nc, "wkb", [128, 2 * CDIM], BF16)
    bqt = _alloc(esB, nc, "bqt", [128, 2], F32)
    bkt = _alloc(esB, nc, "bkt", [128, 2], F32)
    pTb = _alloc(esB, nc, "pTb", [128, 4 * 1024], BF16)
    yscr = _alloc(esB, nc, "yscr", [65, 2 * 512], F32)
    rrow = _alloc(esB, nc, "rrow", [1, 2 * 512], BF16)
    ytb = _alloc(esB, nc, "ytb", [128, T], BF16)

    tok_w = {}
    tok_wcv = {}
    tok_yt_sp = {}
    t_norm_last = {}
    t_projmm_last = {}
    t_proj_of = {}
    slot_free = {0: None, 1: None}
    pT_free = {0: None, 1: None, 2: None, 3: None}
    p4_free = {0: t_vp_done, 1: t_vp_done}

    def w_loads(pr):
        sl = pr % 2
        w_war = ([tok_wcv[pr - 2], t_proj_of[pr - 2]] if pr >= 2
                 else bar + [t_vp_mm_last])
        t1 = p.dma("pop", lambda e, pr=pr, sl=sl: e.dma_start(
            out=wqf[:, CDIM * sl: CDIM * (sl + 1)].rearrange("p (c m) -> p c m", c=8),
            in_=wq_i[:, 128 * pr: 128 * (pr + 1)].rearrange("(c p) m -> p c m", p=128)),
            waits=w_war)
        t2 = p.dma("pop", lambda e, pr=pr, sl=sl: e.dma_start(
            out=wkf[:, CDIM * sl: CDIM * (sl + 1)].rearrange("p (c m) -> p c m", c=8),
            in_=wk_i[:, 128 * pr: 128 * (pr + 1)].rearrange("(c p) m -> p c m", p=128)),
            waits=w_war)
        t3 = p.dma("pop", lambda e, pr=pr, sl=sl: e.dma_start(
            out=bqt[:, sl:sl + 1], in_=bqc_i[128 * pr: 128 * (pr + 1), :]),
            waits=w_war)
        t4 = p.dma("pop", lambda e, pr=pr, sl=sl: e.dma_start(
            out=bkt[:, sl:sl + 1], in_=bkc_i[128 * pr: 128 * (pr + 1), :]),
            waits=w_war)
        tok_w[pr] = [t1, t2, t3, t4]

    def w_convert(pr):
        sl = pr % 2
        w_war = [t_projmm_last[pr - 2]] if pr >= 2 else []
        ta = p.op("dve", lambda e, sl=sl: e.tensor_copy(
            wqb[:, CDIM * sl: CDIM * (sl + 1)], wqf[:, CDIM * sl: CDIM * (sl + 1)]),
            waits=[tok_w[pr][0]] + w_war)
        tb = p.op("dve", lambda e, sl=sl: e.tensor_copy(
            wkb[:, CDIM * sl: CDIM * (sl + 1)], wkf[:, CDIM * sl: CDIM * (sl + 1)]),
            waits=[tok_w[pr][1]] + w_war)
        tok_wcv[pr] = tb
        return [ta, tb]

    def make_proj_pieces(pr, wcv):
        pieces = []
        sl = pr % 2
        for gi, (which, n) in enumerate(
                [(w, n) for w in range(2) for n in range(NQT)]):
            wb, bcol, dst = ((wqb, bqt, qtb), (wkb, bkt, ktb))[which]
            pp = gi % 2

            def emit(wb=wb, bcol=bcol, dst=dst, sl=sl, n=n, pp=pp,
                     which=which, pr=pr, wcv=wcv):
                def mk_p(e):
                    last = None
                    for c in range(8):
                        last = e.matmul(
                            P4[:, 512 * pp: 512 * (pp + 1)],
                            wb[:, CDIM * sl + 128 * c: CDIM * sl + 128 * (c + 1)],
                            xnT[:, T * c + 512 * n: T * c + 512 * (n + 1)],
                            start=(c == 0), stop=(c == 7))
                    return last
                tmm = p.op("pe", mk_p, waits=t_xnT + [p4_free[pp]] + wcv)
                t_projmm_last[pr] = tmm
                tev = p.op("dve", lambda e: e.tensor_scalar_add(
                    dst[:, T * sl + 512 * n: T * sl + 512 * (n + 1)],
                    P4[:, 512 * pp: 512 * (pp + 1)], bcol[:, sl:sl + 1]),
                    waits=[tmm, t_cst, tok_w[pr][2 + which]])
                p4_free[pp] = tev
                t_proj_of[pr] = tev
            pieces.append(emit)
        return pieces

    w_loads(0)
    wcv0 = w_convert(0)
    for piece in make_proj_pieces(0, wcv0):
        piece()

    for pr in range(NPAIR):
        sl = pr % 2
        t_proj = t_proj_of[pr]
        pieces = []
        if pr + 1 < NPAIR:
            w_loads(pr + 1)
            wcvn = w_convert(pr + 1)
            pieces = list(make_proj_pieces(pr + 1, wcvn))
        n_chunks_total = sum(4 * qt + 4 for qt in range(NQT))
        stride = max(1, n_chunks_total // (len(pieces) + 1)) if pieces else 10 ** 9
        chunks_done = 0

        ysl = ytb[:, :]
        prev_tcp = []
        prev_rep = t_norm_last.get(pr - 1)
        pending = None

        def flush_pending():
            nonlocal pending, prev_rep
            if pending is None:
                return
            fqt, ftcp, ftrc = pending
            pending = None

            def mk_rep(e):
                e.matmul(P4[0:64, 0:512], ones_r[0:1, 0:64], rrow[:, 0:512],
                         start=True, stop=True)
                return e.matmul(P4[0:64, 512:1024], ones_r[0:1, 0:64],
                                rrow[:, 512:1024], start=True, stop=True)
            t_rep = p.op("pe", mk_rep,
                         waits=ftrc + [t_proj, p4_free[0], p4_free[1]])
            prev_rep = t_rep
            tn = []
            for hl in range(2):
                dst = ysl[64 * hl: 64 * (hl + 1), 512 * fqt: 512 * (fqt + 1)]
                w_n = [t_rep, ftcp[hl]]
                if pr >= 1 and fqt == 0:
                    w_n.append(tok_yt_sp[pr - 1])
                tn.append(p.op("dve", lambda e, dst=dst, hl=hl: e.tensor_tensor(
                    dst, yscr[0:64, 512 * hl: 512 * (hl + 1)],
                    P4[0:64, 512 * hl: 512 * (hl + 1)], ALU.mult), waits=w_n))
            p4_free[0] = tn[0]
            p4_free[1] = tn[1]
            t_norm_last[pr] = tn[-1]

        flat = [(qt, kc) for qt in range(NQT) for kc in range(4 * qt + 4)]
        LOOK = 2
        tok_pready = {}
        qt_avdone = {}
        n_after = {}

        def emit_front(i):
            qt, kc = flat[i]
            slot = i % 2
            pslot = i % 4
            off = max(0, 128 * kc - 512 * qt)

            def mk_s(e, kc=kc, qt=qt, slot=slot, off=off, sl=sl):
                last = None
                for hl in range(2):
                    last = e.matmul(
                        P1[:, 1024 * slot + 512 * hl + off:
                           1024 * slot + 512 * (hl + 1)],
                        ktb[64 * hl: 64 * (hl + 1),
                            T * sl + 128 * kc: T * sl + 128 * (kc + 1)],
                        qtb[64 * hl: 64 * (hl + 1),
                            T * sl + 512 * qt + off: T * sl + 512 * (qt + 1)],
                        start=True, stop=True)
                return last
            t_s = p.op("pe", mk_s, waits=[t_proj, slot_free[slot]])

            pair = (off == 0 and i % 2 == 0 and i + 1 < len(flat)
                    and flat[i + 1][0] == qt and 128 * flat[i + 1][1] < 512 * qt)
            if pair:
                emit_front.hold = t_s
                tok_pready[i] = None
                return
            if emit_front.hold is not None:
                t_sprev = emit_front.hold
                emit_front.hold = None
                pbase = 1024 * (pslot - 1)
                t_e = p.op("act", lambda e, pbase=pbase: e.activation(
                    pTb[:, pbase: pbase + 2048], P1[:, :],
                    AF.Exp, bias=CST(C_ATTNB), scale=CST(C_ATTNS)),
                    waits=[t_s, t_sprev, pT_free[pslot - 1], pT_free[pslot]])
                tok_pready[i - 1] = t_e
                tok_pready[i] = t_e
                slot_free[0] = t_e
                slot_free[1] = t_e
                return
            if off == 0:
                t_e = p.op("act", lambda e, slot=slot, pslot=pslot: e.activation(
                    pTb[:, 1024 * pslot: 1024 * (pslot + 1)],
                    P1[:, 1024 * slot: 1024 * (slot + 1)],
                    AF.Exp, bias=CST(C_ATTNB), scale=CST(C_ATTNS)),
                    waits=[t_s, pT_free[pslot]])
            else:
                def mk_e(e, slot=slot, pslot=pslot, off=off):
                    e.activation(pTb[:, 1024 * pslot + off: 1024 * pslot + 512],
                                 P1[:, 1024 * slot + off: 1024 * slot + 512],
                                 AF.Exp, bias=CST(C_ATTNB), scale=CST(C_ATTNS))
                    return e.activation(
                        pTb[:, 1024 * pslot + 512 + off: 1024 * (pslot + 1)],
                        P1[:, 1024 * slot + 512 + off: 1024 * (slot + 1)],
                        AF.Exp, bias=CST(C_ATTNB), scale=CST(C_ATTNS))
                t_e = p.op("act", mk_e, waits=[t_s, pT_free[pslot]])
            slot_free[slot] = t_e

            if kc >= 4 * qt:
                def mk_m(e, pslot=pslot, off=off):
                    e.tensor_mul(
                        pTb[:, 1024 * pslot + off: 1024 * pslot + off + 128],
                        pTb[:, 1024 * pslot + off: 1024 * pslot + off + 128],
                        tri[:, :])
                    return e.tensor_mul(
                        pTb[:, 1024 * pslot + 512 + off:
                            1024 * pslot + 512 + off + 128],
                        pTb[:, 1024 * pslot + 512 + off:
                            1024 * pslot + 512 + off + 128],
                        tri[:, :])
                tok_pready[i] = p.op("dve", mk_m, waits=[t_e, t_tri])
            else:
                tok_pready[i] = t_e

        emit_front.hold = None
        def emit_back(j):
            nonlocal pending, prev_tcp
            qt, kc = flat[j]
            nkc = 4 * qt + 4
            pslot = j % 4

            def mk_av(e, kc=kc, pslot=pslot, pr=pr, nkc=nkc, qt=qt):
                off = max(0, 128 * kc - 512 * qt)
                last = None
                for hl, pyt in ((0, P2a), (1, P2b)):
                    last = e.matmul(
                        pyt[0:65, off:512],
                        Vp[:, 1040 * kc + 130 * pr + 65 * hl:
                           1040 * kc + 130 * pr + 65 * (hl + 1)],
                        pTb[:, 1024 * pslot + 512 * hl + off:
                            1024 * pslot + 512 * (hl + 1)],
                        start=(kc == 0), stop=(kc == nkc - 1))
                return last
            w_av = [tok_pready[j]]
            if kc == 0:
                w_av += prev_tcp if prev_tcp else [t_norm_last.get(pr - 1)]
            t_av = p.op("pe", mk_av, waits=w_av)
            pT_free[pslot] = t_av
            if kc == min(5, nkc - 1):
                flush_pending()
            if kc == nkc - 1:
                w_cp0 = [t_av, t_norm_last.get(pr)] + ([t_norm_last.get(pr - 1)] if qt == 0 else [])
                tcp = []
                for hl, pyt in ((0, P2a), (1, P2b)):
                    tcp.append(p.op("dve", lambda e, hl=hl, pyt=pyt: e.tensor_copy(
                        yscr[0:64, 512 * hl: 512 * (hl + 1)], pyt[0:64, :]),
                        waits=w_cp0))
                t_cps = p.op("dve", lambda e: e.tensor_copy(
                    yscr[64:65, 0:512], P2a[64:65, 0:512]), waits=w_cp0)
                t_cps2 = p.op("dve", lambda e: e.tensor_copy(
                    yscr[64:65, 512:1024], P2b[64:65, 0:512]), waits=w_cp0)
                trc = []

                def mk_rc(e, hl):
                    with nc.allow_low_precision(reason="bf16 softmax-"
                                                "denominator reciprocal"):
                        return e.reciprocal(rrow[:, 512 * hl: 512 * (hl + 1)],
                                            yscr[64:65, 512 * hl: 512 * (hl + 1)])
                trc.append(p.op("dve", lambda e: mk_rc(e, 0),
                                waits=[t_cps, prev_rep]))
                trc.append(p.op("dve", lambda e: mk_rc(e, 1),
                                waits=[t_cps2, prev_rep]))
                pending = (qt, tcp, trc)
                prev_tcp = [t_cps2]

        for i in range(len(flat) + LOOK):
            if i < len(flat):
                emit_front(i)
            j = i - LOOK
            if j >= 0:
                emit_back(j)
                chunks_done += 1
                if pieces and chunks_done % stride == 0:
                    pieces.pop(0)()
        for piece in pieces:
            piece()
        pieces = []
        flush_pending()

        tok_yt_sp[pr] = p.dma("sp", lambda e, pr=pr, ysl=ysl: e.dma_start(
            out=yt_o[128 * pr: 128 * (pr + 1), :], in_=ysl),
            waits=[t_norm_last[pr]])
    p.op("sp", lambda e: None,
         waits=p.qlast("sp") + p.qlast("act") + p.qlast("pop"), inc=0)

    p.emit()
    esP.close()
    esB.close()
    es.close()
    return nc


def build_l2():
    nc = bass.Bass("TRN2", target_bir_lowering=False)
    xtp_i = nc.dram_tensor("xtp", [8 * 256, CDIM], F32, kind="ExternalInput").ap()
    wex_i = nc.dram_tensor("wex", [8 * CDIM, CDIM], F32, kind="ExternalInput").ap()
    bex_i = nc.dram_tensor("bex", [1, 8 * CDIM], F32, kind="ExternalInput").ap()
    cst_i = nc.dram_tensor("consts", [128, NCONST], F32, kind="ExternalInput").ap()
    ytl_o = nc.dram_tensor("ytl", [256, CDIM], BF16, kind="ExternalOutput").ap()
    xtn_sp = nc.dram_tensor("xtn_sp", [256, CDIM], BF16).ap()

    p = Prog(nc)
    es = contextlib.ExitStack()
    cst = _alloc(es, nc, "cst", [128, NCONST], F32)
    ones_r = _alloc(es, nc, "ones_r", [1, 128], BF16)
    xts = _alloc(es, nc, "xts", [128, 16 * CDIM], F32)
    acc = _alloc(es, nc, "acc", [128, 2 * CDIM], F32)
    sqs = _alloc(es, nc, "sqs", [128, CDIM], F32)
    stt = _alloc(es, nc, "stt", [128, 2 * 12], F32)
    xtn = _alloc(es, nc, "xtn", [128, 2 * CDIM], BF16)
    xtnT = _alloc(es, nc, "xtnT", [128, 8 * 256], BF16)
    wxf = _alloc(es, nc, "wxf", [128, 2 * 8 * CDIM], F32)
    wxb = _alloc(es, nc, "wxb", [128, 2 * 8 * CDIM], BF16)
    bexb = _alloc(es, nc, "bexb", [1, 8 * CDIM], BF16)
    ystg = _alloc(es, nc, "ystg", [32, 2 * CDIM], BF16)
    esP = contextlib.ExitStack()
    PY = _psum(esP, nc, "PY", [128, 1024])

    def CST(j):
        return cst[:, j:j + 1]

    t_cst = p.dma("sp", lambda e: e.dma_start(out=cst[:, :], in_=cst_i))
    t_ones = p.op("dve", lambda e: e.memset(ones_r[:, :], 1.0))
    t_bex = p.dma("sp", lambda e: e.dma_start(out=wxf[0:1, 0:8 * CDIM], in_=bex_i))
    t_bexb = p.op("dve", lambda e: e.tensor_copy(bexb[:, :], wxf[0:1, 0:8 * CDIM]),
                  waits=[t_bex])

    t_ld = []
    for r in range(8):
        t_ld.append(p.dma("sp", lambda e, r=r: e.dma_start(
            out=xts[:, 2 * CDIM * r: 2 * CDIM * (r + 1)].rearrange(
                "p (c m) -> p c m", c=2),
            in_=xtp_i[256 * r: 256 * (r + 1), :].rearrange("(c p) m -> p c m", p=128))))

    tok_xtn = []
    for i in range(2):
        dst = acc[:, CDIM * i: CDIM * (i + 1)]
        tk = p.op("dve", lambda e, dst=dst, i=i: e.tensor_copy(
            dst, xts[:, CDIM * i: CDIM * (i + 1)]), waits=[t_ld[0]])
        for r in range(1, 8):
            src = xts[:, 2 * CDIM * r + CDIM * i: 2 * CDIM * r + CDIM * (i + 1)]
            tk = p.op("dve", lambda e, dst=dst, src=src: e.tensor_add(dst, dst, src),
                      waits=[t_ld[r], tk])
        tk = p.op("dve", lambda e, dst=dst: e.tensor_scalar(
            dst, dst, CST(C_CLIPHI), CST(C_CLIPLO), ALU.min, ALU.max),
            waits=[tk, t_cst])
        xdst = xtn[:, CDIM * i: CDIM * (i + 1)]
        w_ln = [tk] + ([tok_xtn[0]] if i == 1 else [])
        tok_xtn.append(_ln_chain(p, CST, dst, stt[:, 12 * i: 12 * (i + 1)],
                                 sqs[:, :], xdst, AF.Tanh, w_ln))

    t_sp = []
    for i in range(2):
        t_sp.append(p.dma("sp", lambda e, i=i: e.dma_start(
            out=xtn_sp[128 * i: 128 * (i + 1), :],
            in_=xtn[:, CDIM * i: CDIM * (i + 1)]), waits=[tok_xtn[i]]))
    t_T = []
    for dc in range(8):
        t_T.append(p.dma("act", lambda e, dc=dc: e.dma_start(
            out=xtnT[:, 256 * dc: 256 * (dc + 1)],
            in_=xtn_sp[:, 128 * dc: 128 * (dc + 1)], transpose=True),
            waits=[t_sp[0], t_sp[1]]))

    tok_wx = {}

    def wx_load(el, extra=()):
        w = wxf[:, 8 * CDIM * (el % 2): 8 * CDIM * (el % 2 + 1)]
        tl = p.dma("pop", lambda e, el=el, w=w: e.dma_start(
            out=w.rearrange("p (c m) -> p c m", c=8),
            in_=wex_i[CDIM * el: CDIM * (el + 1), :].rearrange(
                "(c p) m -> p c m", p=128)), waits=list(extra))
        tcv = p.op("dve", lambda e, el=el, w=w: e.tensor_copy(
            wxb[:, 8 * CDIM * (el % 2): 8 * CDIM * (el % 2 + 1)], w),
            waits=[tl] + list(extra))
        tok_wx[el] = (tl, tcv)

    wx_load(0, extra=[t_bexb])
    wx_load(1)
    tok_ev = {}
    tok_st = {}
    prev_mm = None
    for el in range(8):
        def mk_ex(e, el=el):
            last = None
            for hh in range(2):
                for dc in range(8):
                    last = e.matmul(
                        PY[0:32, 512 * hh: 512 * (hh + 1)],
                        xtnT[:, 256 * dc + 32 * el: 256 * dc + 32 * (el + 1)],
                        wxb[:, 8 * CDIM * (el % 2) + CDIM * dc + 512 * hh:
                            8 * CDIM * (el % 2) + CDIM * dc + 512 * (hh + 1)],
                        start=(dc == 0), stop=False)
                last = e.matmul(PY[0:32, 512 * hh: 512 * (hh + 1)],
                                ones_r[0:1, 0:32],
                                bexb[0:1, CDIM * el + 512 * hh: CDIM * el + 512 * (hh + 1)],
                                start=False, stop=True)
            return last
        w_ex = t_T + [tok_wx[el][1], t_bexb, t_ones,
                tok_ev.get(el - 1)]
        t_mm = p.op("pe", mk_ex, waits=w_ex)
        prev_mm = t_mm
        dst = ystg[:, CDIM * (el % 2): CDIM * (el % 2 + 1)]
        w_ev = [t_mm] + ([tok_st[el - 2]] if el >= 2 else [])
        tok_ev[el] = p.op("dve", lambda e, dst=dst: e.tensor_copy(dst, PY[0:32, :]),
                          waits=w_ev)
        tok_st[el] = p.dma("sp", lambda e, el=el, dst=dst: e.dma_start(
            out=ytl_o[32 * el: 32 * (el + 1), :], in_=dst), waits=[tok_ev[el]])
        if el + 2 < 8:
            wx_load(el + 2, extra=[tok_wx[el][1], t_mm])

    p.op("sp", lambda e: None,
         waits=p.qlast("sp") + p.qlast("act") + p.qlast("pop"), inc=0)
    p.emit()
    esP.close()
    es.close()
    return nc


def build_l3(T=4096):
    TC = T // 128
    NB = T // 512
    nc = bass.Bass("TRN2", target_bir_lowering=False)
    x_i = nc.dram_tensor("x", [T, CDIM], F32, kind="ExternalInput").ap()
    yt_i = nc.dram_tensor("yt", [CDIM, T], BF16, kind="ExternalInput").ap()
    csp_i = nc.dram_tensor("c_sp", [T, EDIM], BF16, kind="ExternalInput").ap()
    wo_i = nc.dram_tensor("wo", [CDIM, CDIM], F32, kind="ExternalInput").ap()
    bor_i = nc.dram_tensor("bor", [1, CDIM], F32, kind="ExternalInput").ap()
    ytl_i = nc.dram_tensor("ytl", [EDIM, CDIM], BF16, kind="ExternalInput").ap()
    out_o = nc.dram_tensor("out", [T, CDIM], F32, kind="ExternalOutput").ap()

    p = Prog(nc)
    es = contextlib.ExitStack()
    ones_r = _alloc(es, nc, "ones_r", [1, 128], BF16)
    wof = _alloc(es, nc, "wof", [128, 8 * CDIM], F32)
    wob = _alloc(es, nc, "wob", [128, 8 * CDIM], BF16)
    bof = _alloc(es, nc, "bof", [1, CDIM], F32)
    bob = _alloc(es, nc, "bob", [1, CDIM], BF16)
    ytl = _alloc(es, nc, "ytl_sb", [128, 16 * CDIM], BF16)
    ytsb = _alloc(es, nc, "ytsb", [128, 8 * T], BF16)
    ctb = _alloc(es, nc, "ctb", [128, 2 * 16 * 512], BF16)
    xb = _alloc(es, nc, "xb", [128, 2 * CDIM], F32)
    ob = _alloc(es, nc, "ob", [128, 2 * CDIM], F32)
    esP = contextlib.ExitStack()
    PO = _psum(esP, nc, "PO", [128, 2048])

    t_ones = p.op("dve", lambda e: e.memset(ones_r[:, :], 1.0))
    t_wo = p.dma("sp", lambda e: e.dma_start(
        out=wof[:, :].rearrange("p (c m) -> p c m", c=8),
        in_=wo_i.rearrange("(c p) m -> p c m", p=128)))
    t_wob = p.op("dve", lambda e: e.tensor_copy(wob[:, :], wof[:, :]), waits=[t_wo])
    t_bo = p.dma("sp", lambda e: e.dma_start(out=bof[:, :], in_=bor_i))
    t_bob = p.op("dve", lambda e: e.tensor_copy(bob[:, :], bof[:, :]), waits=[t_bo])
    t_ytl = p.dma("sp", lambda e: e.dma_start(
        out=ytl[:, :].rearrange("p (c m) -> p c m", c=16),
        in_=ytl_i.rearrange("(c p) m -> p c m", p=128)))
    t_yt = []
    for c in range(8):
        t_yt.append(p.dma("sp", lambda e, c=c: e.dma_start(
            out=ytsb[:, T * c: T * (c + 1)], in_=yt_i[128 * c: 128 * (c + 1), :])))

    tok_ct = [None] * NB
    tok_blk = [None] * NB
    for b in range(NB):
        sl = b % 2
        tks = []
        for ec in range(16):
            w_ct = [tok_blk[b - 2]] if b >= 2 else []
            tks.append(p.dma("act", lambda e, b=b, sl=sl, ec=ec: e.dma_start(
                out=ctb[:, 8192 * sl + 512 * ec: 8192 * sl + 512 * (ec + 1)],
                in_=csp_i[512 * b: 512 * (b + 1), 128 * ec: 128 * (ec + 1)],
                transpose=True), waits=w_ct))
        tok_ct[b] = tks

        for tt in range(4 * b, min(4 * b + 4, TC)):
            xsl = xb[:, CDIM * (tt % 2): CDIM * (tt % 2 + 1)]
            osl = ob[:, CDIM * (tt % 2): CDIM * (tt % 2 + 1)]
            w_x = [tok_ev[(tt - 2, 0)], tok_ev[(tt - 2, 1)]] if tt >= 2 else []
            t_x = p.dma("sp", lambda e, xsl=xsl, tt=tt: e.dma_start(
                out=xsl, in_=x_i[128 * tt: 128 * (tt + 1), :]), waits=w_x)
            if tt == 0:
                tok_ev = {}
                tok_stt = {}
            t_mm_last = None
            for hh in range(2):
                pb = PO[:, 1024 * (tt % 2) + 512 * hh:
                        1024 * (tt % 2) + 512 * (hh + 1)]

                def mk_o(e, tt=tt, hh=hh, pb=pb, sl=sl):
                    last = None
                    for c in range(8):
                        last = e.matmul(
                            pb, ytsb[:, T * c + 128 * tt: T * c + 128 * (tt + 1)],
                            wob[:, CDIM * c + 512 * hh: CDIM * c + 512 * (hh + 1)],
                            start=(c == 0), stop=False)
                    for ec in range(16):
                        last = e.matmul(
                            pb,
                            ctb[:, 8192 * sl + 512 * ec + 128 * (tt % 4):
                                8192 * sl + 512 * ec + 128 * (tt % 4) + 128],
                            ytl[:, CDIM * ec + 512 * hh: CDIM * ec + 512 * (hh + 1)],
                            start=False, stop=False)
                    return e.matmul(pb, ones_r[:, :], bob[:, 512 * hh: 512 * (hh + 1)],
                                    start=False, stop=True)
                w_o = [t_wob, t_bob, t_ytl, t_ones] + t_yt + tok_ct[b]
                if tt >= 2:
                    w_o.append(tok_ev[(tt - 2, hh)])
                t_mm = p.op("pe", mk_o, waits=w_o)
                t_mm_last = t_mm
                w_ev = [t_mm, t_x] + ([tok_stt[tt - 2]] if tt >= 2 else [])
                tok_ev[(tt, hh)] = p.op(
                    "dve", lambda e, osl=osl, pb=pb, hh=hh, xsl=xsl: e.tensor_add(
                        osl[:, 512 * hh: 512 * (hh + 1)], pb,
                        xsl[:, 512 * hh: 512 * (hh + 1)]), waits=w_ev)
            tok_stt[tt] = p.dma("sp", lambda e, osl=osl, tt=tt: e.dma_start(
                out=out_o[128 * tt: 128 * (tt + 1), :], in_=osl),
                waits=[tok_ev[(tt, 0)], tok_ev[(tt, 1)]])
            if tt % 4 == 3:
                tok_blk[tt // 4] = t_mm_last

    p.op("sp", lambda e: None, waits=p.qlast("sp") + p.qlast("act"), inc=0)
    p.emit()
    esP.close()
    es.close()
    return nc


_NC_CACHE = {}


def _get(builder, key, *args):
    if key not in _NC_CACHE:
        _NC_CACHE[key] = builder(*args)
    return _NC_CACHE[key]


def kernel(L0_attn_x, L0_attn_Wq, L0_attn_bq, L0_attn_Wk, L0_attn_bk,
           L0_attn_Wv, L0_attn_bv, L0_attn_Wo, L0_attn_bo,
           L0_ffn_phi_w, L0_ffn_phi_b, L0_ffn_expert_w, L0_ffn_expert_b):
    T = L0_attn_x.shape[1]
    f32 = np.float32
    cc = np.ascontiguousarray
    consts = np.tile(np.asarray(CONST_VALS, f32), (128, 1))
    tri = np.asarray((np.arange(128)[:, None] <= np.arange(128)[None, :]),
                     dtype=ml_dtypes.bfloat16)

    x = np.asarray(L0_attn_x, f32)
    phiw = cc(np.asarray(L0_ffn_phi_w, f32))
    phib = cc(np.asarray(L0_ffn_phi_b, f32).reshape(1, EDIM))
    wq = cc(np.asarray(L0_attn_Wq, f32))
    wk = cc(np.asarray(L0_attn_Wk, f32))
    wv = cc(np.asarray(L0_attn_Wv, f32))
    wo = cc(np.asarray(L0_attn_Wo, f32))
    bqc = cc(np.asarray(L0_attn_bq, f32).reshape(CDIM, 1))
    bkc = cc(np.asarray(L0_attn_bk, f32).reshape(CDIM, 1))
    bvr = cc(np.asarray(L0_attn_bv, f32).reshape(1, CDIM))
    bor = cc(np.asarray(L0_attn_bo, f32).reshape(1, CDIM))
    wex = np.asarray(L0_ffn_expert_w, f32)
    bex = np.asarray(L0_ffn_expert_b, f32)

    nc1 = _get(build_l1, ("l1", T), T)
    in1 = [dict(x=cc(x[b]), phiw=phiw, phib=phib, wq=wq, wk=wk, wv=wv,
                bqc=bqc, bkc=bkc, bvr=bvr, tri=tri, consts=consts)
           for b in range(NCORES)]
    r1 = run_bass_kernel_spmd(nc1, in1, core_ids=list(range(NCORES))).results

    nc2 = _get(build_l2, ("l2",))
    in2 = []
    for r in range(NCORES):
        xtp = cc(np.stack([np.asarray(r1[b]["xt_part"])[256 * r: 256 * (r + 1), :]
                           for b in range(NCORES)]).reshape(8 * 256, CDIM))
        in2.append(dict(xtp=xtp,
                        wex=cc(wex[8 * r: 8 * (r + 1)].reshape(8 * CDIM, CDIM)),
                        bex=cc(bex[8 * r: 8 * (r + 1)].reshape(1, 8 * CDIM)),
                        consts=consts))
    r2 = run_bass_kernel_spmd(nc2, in2, core_ids=list(range(NCORES))).results

    ytl_full = cc(np.concatenate([np.asarray(r2[r]["ytl"]) for r in range(NCORES)],
                                 axis=0))
    nc3 = _get(build_l3, ("l3", T), T)
    in3 = [dict(x=cc(x[b]), yt=cc(np.asarray(r1[b]["yt_sp"])),
                c_sp=cc(np.asarray(r1[b]["c_sp"])), wo=wo, bor=bor, ytl=ytl_full)
           for b in range(NCORES)]
    r3 = run_bass_kernel_spmd(nc3, in3, core_ids=list(range(NCORES))).results

    return np.stack([np.asarray(r3[b]["out"], f32) for b in range(NCORES)])


# version 16
# speedup vs baseline: 1.0556x; 1.0556x over previous
import contextlib

import numpy as np
import ml_dtypes
import concourse.bass as bass
import concourse.mybir as mybir
from concourse.bass_utils import run_bass_kernel_spmd

F32 = mybir.dt.float32
BF16 = mybir.dt.bfloat16
AF = mybir.ActivationFunctionType
ALU = mybir.AluOpType
AX = mybir.AxisListType

NCORES = 8
CDIM = 1024
EDIM = 2048
NEXP, PSLOT = 64, 32
NPAIR = 8
EPS = 1e-5

C_INVC, C_EPS, C_NHALF, C_FFNB, C_ATTNS, C_ATTNB = 0, 1, 2, 3, 4, 5
C_CLIPLO, C_CLIPHI, C_ONE, C_ZERO, C_NEG1 = 6, 7, 8, 9, 10
NCONST = 11
CONST_VALS = [1.0 / CDIM, EPS, -0.5, -2.0, 0.125, -4.0,
              -33000.0, 65000.0, 1.0, 0.0, -1.0]


class Prog:

    ENGS = ("pe", "act", "dve", "pop", "sp")
    QDEPTH = {"sp": 8, "act": 6, "pop": 4}

    def __init__(self, nc):
        self.nc = nc
        self.ops = {e: [] for e in self.ENGS}
        self.cnt = {s: 0 for s in self.ENGS}
        for e, d in self.QDEPTH.items():
            for j in range(d):
                self.cnt[f"q_{e}{j}"] = 0
        self.qn = {e: 0 for e in self.QDEPTH}

    def op(self, eng, builder, waits=(), inc=1, sem=None):
        sem = sem or eng
        self.cnt[sem] += inc
        tok = (sem, self.cnt[sem])
        self.ops[eng].append((builder, inc, sem, [w for w in waits if w]))
        return tok

    def dma(self, eng, builder, waits=()):
        j = self.qn[eng] % self.QDEPTH[eng]
        self.qn[eng] += 1
        sem = f"q_{eng}{j}"
        auto = [(sem, self.cnt[sem])] if self.cnt[sem] else []
        return self.op(eng, builder, waits=list(waits) + auto, inc=16, sem=sem)

    def last(self, sem):
        return (sem, self.cnt[sem])

    def qlast(self, eng):
        return [(f"q_{eng}{j}", self.cnt[f"q_{eng}{j}"])
                for j in range(self.QDEPTH[eng])
                if self.cnt[f"q_{eng}{j}"]]

    def emit(self):
        nc = self.nc
        es = contextlib.ExitStack()
        sems = {s: es.enter_context(nc.semaphore("sem_" + s))
                for s in self.cnt}
        with nc.Block() as blk:
            sect = {"pe": blk.tensor, "act": blk.scalar, "dve": blk.vector,
                    "pop": blk.gpsimd, "sp": blk.sync}

            def mk(ename):
                ops = self.ops[ename]

                def run(eng):
                    waited = {}
                    for builder, inc, sem, waits in ops:
                        for (s, v) in waits:
                            if waited.get(s, 0) < v:
                                eng.wait_ge(sems[s], v)
                                waited[s] = v
                        inst = builder(eng)
                        if inst is not None and inc > 0:
                            inst.then_inc(sems[sem], inc)
                return run

            for e in self.ENGS:
                sect[e](mk(e))
        es.close()


def _alloc(es, nc, name, shape, dt):
    return es.enter_context(nc.sbuf_tensor(name, shape, dt))


def _psum(es, nc, name, shape, dt=F32):
    return es.enter_context(nc.psum_tensor(name, shape, dt))


def _ln_chain(p, CST, src, st, sqs, dst_act, dst_fn, waits):
    s1, sq, m, msq, t1, var, lnv, rstd, nmr = (st[:, j:j + 1] for j in range(9))
    ta = p.op("dve", lambda e: e.reduce_sum(s1, src, AX.X), waits=waits)
    tb = p.op("act", lambda e: e.activation(sqs, src, AF.Square, accum_out=sq),
              waits=waits)
    tm = p.op("dve", lambda e: e.tensor_scalar_mul(m, s1, CST(C_INVC)), waits=[ta])
    tmsq = p.op("dve", lambda e: e.tensor_scalar_mul(msq, sq, CST(C_INVC)), waits=[tb])
    tt1 = p.op("dve", lambda e: e.tensor_mul(t1, m, m), waits=[tm])
    tvar = p.op("dve", lambda e: e.tensor_sub(var, msq, t1), waits=[tmsq, tt1])
    tlnv = p.op("act", lambda e: e.activation(lnv, var, AF.Ln, bias=CST(C_EPS)),
                waits=[tvar])
    trstd = p.op("act", lambda e: e.activation(rstd, lnv, AF.Exp, scale=CST(C_NHALF)),
                 waits=[tlnv])
    tnmr = p.op("dve", lambda e: e.tensor_scalar(nmr, m, rstd, CST(C_NEG1),
                                                 ALU.mult, ALU.mult), waits=[trstd])
    return p.op("act", lambda e: e.activation(dst_act, src, dst_fn,
                                              scale=rstd, bias=nmr),
                waits=[tnmr] + list(waits))


def build_l1(T=4096):
    TC = T // 128
    NQT = T // 512
    spc = min(8, TC)
    XN_RING = spc + 4
    LAG = 4

    nc = bass.Bass("TRN2", target_bir_lowering=False)
    x_i = nc.dram_tensor("x", [T, CDIM], F32, kind="ExternalInput").ap()
    phiw_i = nc.dram_tensor("phiw", [EDIM, CDIM], F32, kind="ExternalInput").ap()
    phib_i = nc.dram_tensor("phib", [1, EDIM], F32, kind="ExternalInput").ap()
    wq_i = nc.dram_tensor("wq", [CDIM, CDIM], F32, kind="ExternalInput").ap()
    wk_i = nc.dram_tensor("wk", [CDIM, CDIM], F32, kind="ExternalInput").ap()
    wv_i = nc.dram_tensor("wv", [CDIM, CDIM], F32, kind="ExternalInput").ap()
    bqc_i = nc.dram_tensor("bqc", [CDIM, 1], F32, kind="ExternalInput").ap()
    bkc_i = nc.dram_tensor("bkc", [CDIM, 1], F32, kind="ExternalInput").ap()
    bvr_i = nc.dram_tensor("bvr", [1, CDIM], F32, kind="ExternalInput").ap()
    tri_i = nc.dram_tensor("tri", [128, 128], BF16, kind="ExternalInput").ap()
    cst_i = nc.dram_tensor("consts", [128, NCONST], F32, kind="ExternalInput").ap()
    xt_o = nc.dram_tensor("xt_part", [EDIM, CDIM], F32, kind="ExternalOutput").ap()
    csp_o = nc.dram_tensor("c_sp", [T, EDIM], BF16, kind="ExternalOutput").ap()
    yt_o = nc.dram_tensor("yt_sp", [CDIM, T], BF16, kind="ExternalOutput").ap()
    xnsp = nc.dram_tensor("xn_sp", [T, CDIM], BF16).ap()
    phibf = nc.dram_tensor("phi_bf", [EDIM, CDIM], BF16).ap()

    p = Prog(nc)
    es = contextlib.ExitStack()
    cst = _alloc(es, nc, "cst", [128, NCONST], F32)
    ones_r = _alloc(es, nc, "ones_r", [1, 128], BF16)
    tri = _alloc(es, nc, "tri_sb", [128, 128], BF16)

    def CST(j):
        return cst[:, j:j + 1]

    esA = contextlib.ExitStack()
    phiT = _alloc(esA, nc, "phiT", [128, 8 * EDIM], BF16)
    phibb = _alloc(esA, nc, "phibb", [1, EDIM], BF16)
    esStg = contextlib.ExitStack()
    phibf32 = _alloc(esStg, nc, "phibf32", [1, EDIM], F32)
    pstg = _alloc(esStg, nc, "pstg", [128, 4 * CDIM], F32)
    pstgb = _alloc(esStg, nc, "pstgb", [128, 4 * CDIM], BF16)
    esStg.close()
    xb = _alloc(esA, nc, "xb", [128, 3 * CDIM], F32)
    stt = _alloc(esA, nc, "stt", [128, 3 * 12], F32)
    xnb = _alloc(esA, nc, "xnb", [128, XN_RING * CDIM], BF16)
    xnTs = _alloc(esA, nc, "xnTs", [128, 2 * 512 * 8], BF16)
    Et = _alloc(esA, nc, "Et", [128, 3 * EDIM], BF16)
    dsum = _alloc(esA, nc, "dsum", [128, 2 * 64], F32)
    csum = _alloc(esA, nc, "csum", [128, 2 * 32], F32)
    drec = _alloc(esA, nc, "drec", [128, 2 * 64], F32)
    crec = _alloc(esA, nc, "crec", [128, 2 * 32], F32)
    Dbuf = _alloc(esA, nc, "Dbuf", [128, spc * EDIM], BF16)
    Cbuf = _alloc(esA, nc, "Cbuf", [128, 2 * EDIM], BF16)
    xtacc = _alloc(esA, nc, "xtacc", [128, 16 * CDIM], F32)

    esP = contextlib.ExitStack()
    P1 = _psum(esP, nc, "P1", [128, 2048])
    P2a = _psum(esP, nc, "P2a", [128, 512])
    P2b = _psum(esP, nc, "P2b", [128, 512])
    P4 = _psum(esP, nc, "P4", [128, 1024])

    t_cst = p.dma("sp", lambda e: e.dma_start(out=cst[:, :], in_=cst_i))
    t_tri = p.dma("sp", lambda e: e.dma_start(out=tri[:, :], in_=tri_i))
    t_ones = p.op("dve", lambda e: e.memset(ones_r[:, :], 1.0))
    t_phibld = p.dma("sp", lambda e: e.dma_start(out=phibf32[:, :], in_=phib_i))
    t_phibb = p.op("dve", lambda e: e.tensor_copy(phibb[:, :], phibf32[:, :]),
                   waits=[t_phibld])

    t_sp_prev = None
    t_cv_prev = None
    for r in range(4):
        rows = phiw_i[512 * r: 512 * (r + 1), :]
        tl = p.dma("pop", lambda e, rows=rows: e.dma_start(
            out=pstg[:, :].rearrange("p (c m) -> p c m", c=4),
            in_=rows.rearrange("(c p) m -> p c m", p=128)),
            waits=[t_cv_prev] if t_cv_prev else [])
        tc = p.op("dve", lambda e: e.tensor_copy(pstgb[:, :], pstg[:, :]),
                  waits=[tl, t_sp_prev] if t_sp_prev else [tl])
        t_sp_prev = p.dma("pop", lambda e, r=r: e.dma_start(
            out=phibf[512 * r: 512 * (r + 1), :].rearrange("(c p) m -> p c m", p=128),
            in_=pstgb[:, :].rearrange("p (c m) -> p c m", c=4)),
            waits=[tc])
        t_cv_prev = tc
    t_phiT = []
    for c in range(8):
        t_phiT.append(p.dma("act", lambda e, c=c: e.dma_start(
            out=phiT[:, EDIM * c: EDIM * (c + 1)],
            in_=phibf[:, 128 * c: 128 * (c + 1)], transpose=True),
            waits=[t_sp_prev]))
    t_stg_free = [t_cv_prev, t_sp_prev, t_phibb]

    tok_x = [None] * TC
    tok_xn = [None] * TC
    tok_xnsp = [None] * TC
    tok_sup = [None] * ((TC + 3) // 4)
    tok_E = [None] * TC
    tok_D = [None] * TC
    tok_C = [None] * TC
    tok_csp = [None] * TC
    tok_logits = [None] * TC
    tok_xtmm = {}
    tok_xtadd = {}
    xt_bank_last = {0: None, 1: None, 2: None, 3: None}
    n_grp = (TC + spc - 1) // spc


    def x_load(t):
        sl = t % 3
        xv = xb[:, CDIM * sl: CDIM * (sl + 1)]
        w = [t_cst] + ([tok_xn[t - 3]] if t >= 3 else [])
        if t == 0:
            w += t_stg_free
        tok_x[t] = p.dma("sp", lambda e, xv=xv, t=t: e.dma_start(
            out=xv, in_=x_i[128 * t: 128 * (t + 1), :]), waits=w)

    def ln_part(t):
        sl = t % 3
        xv = xb[:, CDIM * sl: CDIM * (sl + 1)]
        st = stt[:, 12 * sl: 12 * (sl + 1)]
        xnv = xnb[:, CDIM * (t % XN_RING): CDIM * (t % XN_RING + 1)]
        w_xn = [tok_x[t]] + ([tok_xn[t - 3]] if t >= 3 else [])
        if t >= XN_RING:
            gprev = (t - XN_RING) // spc
            while (gprev, 15, 1) not in tok_xtmm:
                drain_xt(1)
            w_xn.append(tok_xtmm[(gprev, 15, 1)])
        tok_xn[t] = _ln_chain(p, CST, xv, st, xnv, xnv, AF.Identity, w_xn)
        tok_xnsp[t] = p.dma("sp", lambda e: e.dma_start(
            out=xnsp[128 * t: 128 * (t + 1), :], in_=xnv), waits=[tok_xn[t]])
        if t % 4 == 3:
            s = t // 4
            sv = xnTs[:, 4096 * (s % 2): 4096 * (s % 2 + 1)]
            w_sup = [tok_xnsp[4 * (t // 4) + j] for j in range(4)]
            if s >= 2:
                w_sup.append(tok_logits[4 * (s - 2) + 3])
            tks = []
            for c in range(8):
                tks.append(p.dma("act", lambda e, sv=sv, s=s, c=c: e.dma_start(
                    out=sv[:, 512 * c: 512 * (c + 1)],
                    in_=xnsp[512 * s: 512 * (s + 1), 128 * c: 128 * (c + 1)],
                    transpose=True), waits=w_sup))
            tok_sup[s] = tks

    def ffn_pe_act(t):
        s = t // 4
        sv = xnTs[:, 4096 * (s % 2): 4096 * (s % 2 + 1)]
        esl = Et[:, EDIM * (t % 3): EDIM * (t % 3 + 1)]
        t_eh = []
        for h in range(2):
            def mk_lg(e, sv=sv, t=t, h=h):
                last = None
                for c in range(8):
                    lh = sv[:, 512 * c + 128 * (t % 4): 512 * c + 128 * (t % 4) + 128]
                    for g in (2 * h, 2 * h + 1):
                        last = e.matmul(
                            P1[:, 512 * g: 512 * (g + 1)], lh,
                            phiT[:, EDIM * c + 512 * g: EDIM * c + 512 * (g + 1)],
                            start=(c == 0), stop=False)
                for g in (2 * h, 2 * h + 1):
                    last = e.matmul(P1[:, 512 * g: 512 * (g + 1)], ones_r[:, :],
                                    phibb[:, 512 * g: 512 * (g + 1)],
                                    start=False, stop=True)
                return last
            w_lg = tok_sup[s] + t_phiT + [t_phibb, t_ones]
            if t >= 1:
                w_lg.append(ffn_pe_act.exp_h[h])
            t_lg = p.op("pe", mk_lg, waits=w_lg)
            w_E = [t_lg] + ([tok_C[t - 3], tok_D[t - 3]] if t >= 3 else [])
            te = p.op("act", lambda e, esl=esl, h=h: e.activation(
                esl[:, 1024 * h: 1024 * (h + 1)], P1[:, 1024 * h: 1024 * (h + 1)],
                AF.Exp, bias=CST(C_FFNB)), waits=w_E)
            t_eh.append(te)
            ffn_pe_act.exp_h[h] = te
        tok_logits[t] = t_lg
        tok_E[t] = t_eh[1]

        ffn_state[t] = (esl, t_eh)

    def ffn_dve(t):
        esl, t_eh = ffn_state.pop(t)
        dsl = dsum[:, 64 * (t % 2): 64 * (t % 2 + 1)]
        csl = csum[:, 32 * (t % 2): 32 * (t % 2 + 1)]
        drl = drec[:, 64 * (t % 2): 64 * (t % 2 + 1)]
        crl = crec[:, 32 * (t % 2): 32 * (t % 2 + 1)]
        w_slot = [tok_D[t - 2], tok_C[t - 2]] if t >= 2 else []
        td1 = p.op("dve", lambda e, dsl=dsl, esl=esl: e.reduce_sum(
            dsl, esl.rearrange("p (n w) -> p n w", w=PSLOT), AX.X),
            waits=[t_eh[0], t_eh[1]] + w_slot)
        tc1 = p.op("dve", lambda e, csl=csl, esl=esl: e.reduce_sum(
            csl, esl.rearrange("p (n w) -> p w n", w=PSLOT), AX.X),
            waits=[t_eh[0], t_eh[1]] + w_slot)
        td2 = p.op("dve", lambda e, drl=drl, dsl=dsl: e.reciprocal(drl, dsl),
                   waits=[td1])
        tc2 = p.op("dve", lambda e, crl=crl, csl=csl: e.reciprocal(crl, csl),
                   waits=[tc1])
        dv = Dbuf[:, EDIM * (t % spc): EDIM * (t % spc + 1)]
        w_D = [td2]
        if t >= spc:
            gp = (t - spc) // spc
            while (gp, 15, 1) not in tok_xtmm:
                drain_xt(1)
            w_D.append(tok_xtmm[(gp, 15, 1)])
        tok_D[t] = p.op("dve", lambda e, dv=dv, esl=esl, drl=drl: e.tensor_tensor(
            dv.rearrange("p (n w) -> p n w", w=PSLOT),
            esl.rearrange("p (n w) -> p n w", w=PSLOT),
            drl.unsqueeze(2).broadcast_to([128, NEXP, PSLOT]), ALU.mult), waits=w_D)
        cv = Cbuf[:, EDIM * (t % 2): EDIM * (t % 2 + 1)]
        w_C = [tc2] + ([tok_csp[t - 2]] if t >= 2 else [])
        tok_C[t] = p.op("dve", lambda e, cv=cv, esl=esl, crl=crl: e.tensor_tensor(
            cv.rearrange("p (n w) -> p n w", w=PSLOT),
            esl.rearrange("p (n w) -> p n w", w=PSLOT),
            crl.unsqueeze(1).broadcast_to([128, NEXP, PSLOT]), ALU.mult), waits=w_C)
        tok_csp[t] = p.dma("sp", lambda e, cv=cv, t=t: e.dma_start(
            out=csp_o[128 * t: 128 * (t + 1), :], in_=cv), waits=[tok_C[t]])

        if (t + 1) % spc == 0 or t == TC - 1:
            g = t // spc
            g_lo = g * spc
            g_n = t - g_lo + 1
            for et in range(16):
                for dh in range(2):
                    xt_tasks.append((g, g_lo, g_n, et, dh, tok_D[t]))

    XT_BANKS = (P2a[:, 0:512], P2b[:, 0:512], P4[:, 0:512], P4[:, 512:1024])

    def drain_xt(k):
        while xt_tasks and k > 0:
            k -= 1
            g, g_lo, g_n, et, dh, t_d = xt_tasks.pop(0)
            bk = drain_xt.n % 4
            drain_xt.n += 1
            pt = XT_BANKS[bk]
            w_mm = [t_d, xt_bank_last[bk]]

            def mk_xt(e, g_lo=g_lo, g_n=g_n, et=et, dh=dh, pt=pt):
                last = None
                for j in range(g_n):
                    tch = g_lo + j
                    dvj = Dbuf[:, EDIM * (tch % spc) + 128 * et:
                               EDIM * (tch % spc) + 128 * (et + 1)]
                    xnj = xnb[:, CDIM * (tch % XN_RING) + 512 * dh:
                              CDIM * (tch % XN_RING) + 512 * (dh + 1)]
                    last = e.matmul(pt, dvj, xnj,
                                    start=(j == 0), stop=(j == g_n - 1))
                return last
            tok_xtmm[(g, et, dh)] = p.op("pe", mk_xt, waits=w_mm)
            dst = xtacc[:, CDIM * et + 512 * dh: CDIM * et + 512 * (dh + 1)]
            if g == 0:
                tk = p.op("dve", lambda e, dst=dst, pt=pt: e.tensor_copy(
                    dst, pt), waits=[tok_xtmm[(g, et, dh)]])
            else:
                tk = p.op("dve", lambda e, dst=dst, pt=pt: e.tensor_add(
                    dst, dst, pt),
                    waits=[tok_xtmm[(g, et, dh)], tok_xtadd[(g - 1, et, dh)]])
            tok_xtadd[(g, et, dh)] = tk
            xt_bank_last[bk] = tk

    drain_xt.n = 0
    xt_tasks = []
    ffn_pe_act.exp_h = {0: None, 1: None}
    ffn_state = {}
    ffn_pe_act.exp_h = {0: None, 1: None}
    ffn_state = {}
    for it in range(TC + LAG):
        if it == 0:
            x_load(0)
            x_load(1)
            x_load(2)
        elif it + 2 < TC:
            x_load(it + 2)
        if it >= LAG:
            ffn_pe_act(it - LAG)
            drain_xt(5)
        if it < TC:
            ln_part(it)
        if it >= LAG:
            ffn_dve(it - LAG)

    drain_xt(10 ** 9)
    for et in range(16):
        w = [tok_xtadd[(n_grp - 1, et, 0)], tok_xtadd[(n_grp - 1, et, 1)]]
        p.dma("sp", lambda e, et=et: e.dma_start(
            out=xt_o[128 * et: 128 * (et + 1), :],
            in_=xtacc[:, CDIM * et: CDIM * (et + 1)]), waits=w)

    bar = ([p.last(s) for s in ("pe", "act", "dve", "pop", "sp")]
           + p.qlast("sp") + p.qlast("act") + p.qlast("pop"))
    esA.close()

    esB = contextlib.ExitStack()
    xnT = _alloc(esB, nc, "xnT", [128, 8 * T], BF16)
    Vp = _alloc(esB, nc, "Vp", [128, TC * NPAIR * 130], BF16)
    esW = contextlib.ExitStack()
    wvf = _alloc(esW, nc, "wvf", [128, 2 * 4096], F32)
    wvb = _alloc(esW, nc, "wvb", [128, 8 * CDIM], BF16)
    bvf = _alloc(esW, nc, "bvf", [1, CDIM], F32)
    bvb = _alloc(esW, nc, "bvb", [1, CDIM], BF16)

    t_xnT = []
    for c in range(8):
        t_xnT.append(p.dma("act", lambda e, c=c: e.dma_start(
            out=xnT[:, T * c: T * (c + 1)],
            in_=xnsp[:, 128 * c: 128 * (c + 1)], transpose=True), waits=bar))
    t_wv = []
    for r in range(2):
        tl = p.dma("pop", lambda e, r=r: e.dma_start(
            out=wvf[:, 4096 * r: 4096 * (r + 1)].rearrange("p (c m) -> p c m", c=4),
            in_=wv_i[512 * r: 512 * (r + 1), :].rearrange("(c p) m -> p c m", p=128)),
            waits=bar)
        t_wv.append(p.op("dve", lambda e, r=r: e.tensor_copy(
            wvb[:, 4096 * r: 4096 * (r + 1)], wvf[:, 4096 * r: 4096 * (r + 1)]),
            waits=[tl] + bar))
    t_bv = p.dma("pop", lambda e: e.dma_start(out=bvf[:, :], in_=bvr_i), waits=bar)
    t_bvb = p.op("dve", lambda e: e.tensor_copy(bvb[:, :], bvf[:, :]),
                 waits=[t_bv] + bar)
    t_vones = p.op("dve", lambda e: e.memset(
        Vp[:, :].rearrange("p (a s) -> p a s", s=65)[:, :, 64:65], 1.0), waits=bar)

    prev_vp = None
    for tch in range(TC):
        for hf in range(2):
            pt = P4[:, 512 * hf: 512 * (hf + 1)]

            def mk_v(e, tch=tch, hf=hf, pt=pt):
                last = None
                for c in range(8):
                    last = e.matmul(
                        pt, xnT[:, T * c + 128 * tch: T * c + 128 * (tch + 1)],
                        wvb[:, CDIM * c + 512 * hf: CDIM * c + 512 * (hf + 1)],
                        start=(c == 0), stop=False)
                return e.matmul(pt, ones_r[:, :], bvb[:, 512 * hf: 512 * (hf + 1)],
                                start=False, stop=True)
            w_v = t_xnT + [t_wv[-1], t_bvb, t_ones, prev_vp]
            tmm = p.op("pe", mk_v, waits=w_v)
            t_vp_mm_last = tmm
            base = 1040 * tch + 520 * hf
            prev_vp = p.op("dve", lambda e, base=base, pt=pt: e.tensor_copy(
                Vp[:, base: base + 520].rearrange(
                    "p (a b s) -> p a b s", a=4, b=2)[:, :, :, 0:64],
                pt.rearrange("p (a b s) -> p a b s", a=4, b=2)),
                waits=[tmm, t_vones])
    t_vp_done = prev_vp

    esW.close()
    qtb = _alloc(esB, nc, "qtb", [128, 2 * T], BF16)
    ktb = _alloc(esB, nc, "ktb", [128, 2 * T], BF16)
    wqf = _alloc(esB, nc, "wqf", [128, 2 * CDIM], F32)
    wkf = _alloc(esB, nc, "wkf", [128, 2 * CDIM], F32)
    wqb = _alloc(esB, nc, "wqb", [128, 2 * CDIM], BF16)
    wkb = _alloc(esB, nc, "wkb", [128, 2 * CDIM], BF16)
    bqt = _alloc(esB, nc, "bqt", [128, 2], F32)
    bkt = _alloc(esB, nc, "bkt", [128, 2], F32)
    pTb = _alloc(esB, nc, "pTb", [128, 4 * 1024], BF16)
    yscr = _alloc(esB, nc, "yscr", [65, 2 * 512], F32)
    rrow = _alloc(esB, nc, "rrow", [1, 2 * 512], BF16)
    ytb = _alloc(esB, nc, "ytb", [128, T], BF16)

    tok_w = {}
    tok_wcv = {}
    tok_yt_sp = {}
    t_norm_last = {}
    t_projmm_last = {}
    t_proj_of = {}
    slot_free = {0: None, 1: None}
    pT_free = {0: None, 1: None, 2: None, 3: None}
    p4_free = {0: t_vp_done, 1: t_vp_done}

    def w_loads(pr):
        sl = pr % 2
        w_war = ([tok_wcv[pr - 2], t_proj_of[pr - 2]] if pr >= 2
                 else bar + [t_vp_mm_last])
        t1 = p.dma("pop", lambda e, pr=pr, sl=sl: e.dma_start(
            out=wqf[:, CDIM * sl: CDIM * (sl + 1)].rearrange("p (c m) -> p c m", c=8),
            in_=wq_i[:, 128 * pr: 128 * (pr + 1)].rearrange("(c p) m -> p c m", p=128)),
            waits=w_war)
        t2 = p.dma("pop", lambda e, pr=pr, sl=sl: e.dma_start(
            out=wkf[:, CDIM * sl: CDIM * (sl + 1)].rearrange("p (c m) -> p c m", c=8),
            in_=wk_i[:, 128 * pr: 128 * (pr + 1)].rearrange("(c p) m -> p c m", p=128)),
            waits=w_war)
        t3 = p.dma("pop", lambda e, pr=pr, sl=sl: e.dma_start(
            out=bqt[:, sl:sl + 1], in_=bqc_i[128 * pr: 128 * (pr + 1), :]),
            waits=w_war)
        t4 = p.dma("pop", lambda e, pr=pr, sl=sl: e.dma_start(
            out=bkt[:, sl:sl + 1], in_=bkc_i[128 * pr: 128 * (pr + 1), :]),
            waits=w_war)
        tok_w[pr] = [t1, t2, t3, t4]

    def w_convert(pr):
        sl = pr % 2
        w_war = [t_projmm_last[pr - 2]] if pr >= 2 else []
        ta = p.op("dve", lambda e, sl=sl: e.tensor_copy(
            wqb[:, CDIM * sl: CDIM * (sl + 1)], wqf[:, CDIM * sl: CDIM * (sl + 1)]),
            waits=[tok_w[pr][0]] + w_war)
        tb = p.op("dve", lambda e, sl=sl: e.tensor_copy(
            wkb[:, CDIM * sl: CDIM * (sl + 1)], wkf[:, CDIM * sl: CDIM * (sl + 1)]),
            waits=[tok_w[pr][1]] + w_war)
        tok_wcv[pr] = tb
        return [ta, tb]

    def make_proj_pieces(pr, wcv):
        pieces = []
        sl = pr % 2
        for gi, (which, n) in enumerate(
                [(w, n) for w in range(2) for n in range(NQT)]):
            wb, bcol, dst = ((wqb, bqt, qtb), (wkb, bkt, ktb))[which]
            pp = gi % 2

            def emit(wb=wb, bcol=bcol, dst=dst, sl=sl, n=n, pp=pp,
                     which=which, pr=pr, wcv=wcv):
                def mk_p(e):
                    last = None
                    for c in range(8):
                        last = e.matmul(
                            P4[:, 512 * pp: 512 * (pp + 1)],
                            wb[:, CDIM * sl + 128 * c: CDIM * sl + 128 * (c + 1)],
                            xnT[:, T * c + 512 * n: T * c + 512 * (n + 1)],
                            start=(c == 0), stop=(c == 7))
                    return last
                tmm = p.op("pe", mk_p, waits=t_xnT + [p4_free[pp]] + wcv)
                t_projmm_last[pr] = tmm
                tev = p.op("dve", lambda e: e.tensor_scalar_add(
                    dst[:, T * sl + 512 * n: T * sl + 512 * (n + 1)],
                    P4[:, 512 * pp: 512 * (pp + 1)], bcol[:, sl:sl + 1]),
                    waits=[tmm, t_cst, tok_w[pr][2 + which]])
                p4_free[pp] = tev
                t_proj_of[pr] = tev
            pieces.append(emit)
        return pieces

    w_loads(0)
    wcv0 = w_convert(0)
    for piece in make_proj_pieces(0, wcv0):
        piece()

    for pr in range(NPAIR):
        sl = pr % 2
        t_proj = t_proj_of[pr]
        pieces = []
        if pr + 1 < NPAIR:
            w_loads(pr + 1)
            wcvn = w_convert(pr + 1)
            pieces = list(make_proj_pieces(pr + 1, wcvn))
        n_chunks_total = sum(4 * qt + 4 for qt in range(NQT))
        stride = max(1, n_chunks_total // (len(pieces) + 1)) if pieces else 10 ** 9
        chunks_done = 0

        ysl = ytb[:, :]
        prev_tcp = []
        prev_rep = t_norm_last.get(pr - 1)
        pending = None

        def flush_pending():
            nonlocal pending, prev_rep
            if pending is None:
                return
            fqt, ftcp, ftrc = pending
            pending = None

            def mk_rep(e):
                e.matmul(P4[0:64, 0:512], ones_r[0:1, 0:64], rrow[:, 0:512],
                         start=True, stop=True)
                return e.matmul(P4[0:64, 512:1024], ones_r[0:1, 0:64],
                                rrow[:, 512:1024], start=True, stop=True)
            t_rep = p.op("pe", mk_rep,
                         waits=ftrc + [t_proj, p4_free[0], p4_free[1]])
            prev_rep = t_rep
            tn = []
            for hl in range(2):
                dst = ysl[64 * hl: 64 * (hl + 1), 512 * fqt: 512 * (fqt + 1)]
                w_n = [t_rep, ftcp[hl]]
                if pr >= 1 and fqt == 0:
                    w_n.append(tok_yt_sp[pr - 1])
                tn.append(p.op("dve", lambda e, dst=dst, hl=hl: e.tensor_tensor(
                    dst, yscr[0:64, 512 * hl: 512 * (hl + 1)],
                    P4[0:64, 512 * hl: 512 * (hl + 1)], ALU.mult), waits=w_n))
            p4_free[0] = tn[0]
            p4_free[1] = tn[1]
            t_norm_last[pr] = tn[-1]

        flat = [(qt, kc) for qt in range(NQT) for kc in range(4 * qt + 4)]
        LOOK = 2
        tok_pready = {}
        qt_avdone = {}
        n_after = {}

        def emit_front(i):
            qt, kc = flat[i]
            slot = i % 2
            pslot = i % 4
            off = max(0, 128 * kc - 512 * qt)

            def mk_s(e, kc=kc, qt=qt, slot=slot, off=off, sl=sl):
                last = None
                for hl in range(2):
                    last = e.matmul(
                        P1[:, 1024 * slot + 512 * hl + off:
                           1024 * slot + 512 * (hl + 1)],
                        ktb[64 * hl: 64 * (hl + 1),
                            T * sl + 128 * kc: T * sl + 128 * (kc + 1)],
                        qtb[64 * hl: 64 * (hl + 1),
                            T * sl + 512 * qt + off: T * sl + 512 * (qt + 1)],
                        start=True, stop=True)
                return last
            t_s = p.op("pe", mk_s, waits=[t_proj, slot_free[slot]])

            pair = (off == 0 and i % 2 == 0 and i + 1 < len(flat)
                    and flat[i + 1][0] == qt and 128 * flat[i + 1][1] < 512 * qt)
            if pair:
                emit_front.hold = t_s
                tok_pready[i] = None
                return
            if emit_front.hold is not None:
                t_sprev = emit_front.hold
                emit_front.hold = None
                pbase = 1024 * (pslot - 1)
                t_e = p.op("act", lambda e, pbase=pbase: e.activation(
                    pTb[:, pbase: pbase + 2048], P1[:, :],
                    AF.Exp, bias=CST(C_ATTNB), scale=CST(C_ATTNS)),
                    waits=[t_s, t_sprev, pT_free[pslot - 1], pT_free[pslot]])
                tok_pready[i - 1] = t_e
                tok_pready[i] = t_e
                slot_free[0] = t_e
                slot_free[1] = t_e
                return
            if off == 0:
                t_e = p.op("act", lambda e, slot=slot, pslot=pslot: e.activation(
                    pTb[:, 1024 * pslot: 1024 * (pslot + 1)],
                    P1[:, 1024 * slot: 1024 * (slot + 1)],
                    AF.Exp, bias=CST(C_ATTNB), scale=CST(C_ATTNS)),
                    waits=[t_s, pT_free[pslot]])
            else:
                def mk_e(e, slot=slot, pslot=pslot, off=off):
                    e.activation(pTb[:, 1024 * pslot + off: 1024 * pslot + 512],
                                 P1[:, 1024 * slot + off: 1024 * slot + 512],
                                 AF.Exp, bias=CST(C_ATTNB), scale=CST(C_ATTNS))
                    return e.activation(
                        pTb[:, 1024 * pslot + 512 + off: 1024 * (pslot + 1)],
                        P1[:, 1024 * slot + 512 + off: 1024 * (slot + 1)],
                        AF.Exp, bias=CST(C_ATTNB), scale=CST(C_ATTNS))
                t_e = p.op("act", mk_e, waits=[t_s, pT_free[pslot]])
            slot_free[slot] = t_e

            if kc >= 4 * qt:
                def mk_m(e, pslot=pslot, off=off):
                    e.tensor_mul(
                        pTb[:, 1024 * pslot + off: 1024 * pslot + off + 128],
                        pTb[:, 1024 * pslot + off: 1024 * pslot + off + 128],
                        tri[:, :])
                    return e.tensor_mul(
                        pTb[:, 1024 * pslot + 512 + off:
                            1024 * pslot + 512 + off + 128],
                        pTb[:, 1024 * pslot + 512 + off:
                            1024 * pslot + 512 + off + 128],
                        tri[:, :])
                tok_pready[i] = p.op("dve", mk_m, waits=[t_e, t_tri])
            else:
                tok_pready[i] = t_e

        emit_front.hold = None
        def emit_back(j):
            nonlocal pending, prev_tcp
            qt, kc = flat[j]
            nkc = 4 * qt + 4
            pslot = j % 4

            def mk_av(e, kc=kc, pslot=pslot, pr=pr, nkc=nkc, qt=qt):
                off = max(0, 128 * kc - 512 * qt)
                last = None
                for hl, pyt in ((0, P2a), (1, P2b)):
                    last = e.matmul(
                        pyt[0:65, off:512],
                        Vp[:, 1040 * kc + 130 * pr + 65 * hl:
                           1040 * kc + 130 * pr + 65 * (hl + 1)],
                        pTb[:, 1024 * pslot + 512 * hl + off:
                            1024 * pslot + 512 * (hl + 1)],
                        start=(kc == 0), stop=(kc == nkc - 1))
                return last
            w_av = [tok_pready[j]]
            if kc == 0:
                w_av += prev_tcp if prev_tcp else [t_norm_last.get(pr - 1)]
            t_av = p.op("pe", mk_av, waits=w_av)
            pT_free[pslot] = t_av
            if kc == min(5, nkc - 1):
                flush_pending()
            if kc == nkc - 1:
                w_cp0 = [t_av, t_norm_last.get(pr)] + ([t_norm_last.get(pr - 1)] if qt == 0 else [])
                tcp = []
                for hl, pyt in ((0, P2a), (1, P2b)):
                    tcp.append(p.op("dve", lambda e, hl=hl, pyt=pyt: e.tensor_copy(
                        yscr[0:64, 512 * hl: 512 * (hl + 1)], pyt[0:64, :]),
                        waits=w_cp0))
                t_cps = p.op("dve", lambda e: e.tensor_copy(
                    yscr[64:65, 0:512], P2a[64:65, 0:512]), waits=w_cp0)
                t_cps2 = p.op("dve", lambda e: e.tensor_copy(
                    yscr[64:65, 512:1024], P2b[64:65, 0:512]), waits=w_cp0)
                trc = []

                def mk_rc(e, hl):
                    with nc.allow_low_precision(reason="bf16 softmax-"
                                                "denominator reciprocal"):
                        return e.reciprocal(rrow[:, 512 * hl: 512 * (hl + 1)],
                                            yscr[64:65, 512 * hl: 512 * (hl + 1)])
                trc.append(p.op("dve", lambda e: mk_rc(e, 0),
                                waits=[t_cps, prev_rep]))
                trc.append(p.op("dve", lambda e: mk_rc(e, 1),
                                waits=[t_cps2, prev_rep]))
                pending = (qt, tcp, trc)
                prev_tcp = [t_cps2]

        for i in range(len(flat) + LOOK):
            if i < len(flat):
                emit_front(i)
            j = i - LOOK
            if j >= 0:
                emit_back(j)
                chunks_done += 1
                if pieces and chunks_done % stride == 0:
                    pieces.pop(0)()
        for piece in pieces:
            piece()
        pieces = []
        flush_pending()

        tok_yt_sp[pr] = p.dma("sp", lambda e, pr=pr, ysl=ysl: e.dma_start(
            out=yt_o[128 * pr: 128 * (pr + 1), :], in_=ysl),
            waits=[t_norm_last[pr]])
    p.op("sp", lambda e: None,
         waits=p.qlast("sp") + p.qlast("act") + p.qlast("pop"), inc=0)

    p.emit()
    esP.close()
    esB.close()
    es.close()
    return nc


def build_l2():
    nc = bass.Bass("TRN2", target_bir_lowering=False)
    xtp_i = nc.dram_tensor("xtp", [8 * 256, CDIM], F32, kind="ExternalInput").ap()
    wex_i = nc.dram_tensor("wex", [8 * CDIM, CDIM], F32, kind="ExternalInput").ap()
    bex_i = nc.dram_tensor("bex", [1, 8 * CDIM], F32, kind="ExternalInput").ap()
    cst_i = nc.dram_tensor("consts", [128, NCONST], F32, kind="ExternalInput").ap()
    ytl_o = nc.dram_tensor("ytl", [256, CDIM], BF16, kind="ExternalOutput").ap()
    xtn_sp = nc.dram_tensor("xtn_sp", [256, CDIM], BF16).ap()

    p = Prog(nc)
    es = contextlib.ExitStack()
    cst = _alloc(es, nc, "cst", [128, NCONST], F32)
    ones_r = _alloc(es, nc, "ones_r", [1, 128], BF16)
    xts = _alloc(es, nc, "xts", [128, 16 * CDIM], F32)
    acc = _alloc(es, nc, "acc", [128, 2 * CDIM], F32)
    sqs = _alloc(es, nc, "sqs", [128, CDIM], F32)
    stt = _alloc(es, nc, "stt", [128, 2 * 12], F32)
    xtn = _alloc(es, nc, "xtn", [128, 2 * CDIM], BF16)
    xtnT = _alloc(es, nc, "xtnT", [128, 8 * 256], BF16)
    wxf = _alloc(es, nc, "wxf", [128, 2 * 8 * CDIM], F32)
    wxb = _alloc(es, nc, "wxb", [128, 2 * 8 * CDIM], BF16)
    bexb = _alloc(es, nc, "bexb", [1, 8 * CDIM], BF16)
    ystg = _alloc(es, nc, "ystg", [32, 2 * CDIM], BF16)
    esP = contextlib.ExitStack()
    PY = _psum(esP, nc, "PY", [128, 1024])

    def CST(j):
        return cst[:, j:j + 1]

    t_cst = p.dma("sp", lambda e: e.dma_start(out=cst[:, :], in_=cst_i))
    t_ones = p.op("dve", lambda e: e.memset(ones_r[:, :], 1.0))
    t_bex = p.dma("sp", lambda e: e.dma_start(out=wxf[0:1, 0:8 * CDIM], in_=bex_i))
    t_bexb = p.op("dve", lambda e: e.tensor_copy(bexb[:, :], wxf[0:1, 0:8 * CDIM]),
                  waits=[t_bex])

    t_ld = []
    for r in range(8):
        t_ld.append(p.dma("sp", lambda e, r=r: e.dma_start(
            out=xts[:, 2 * CDIM * r: 2 * CDIM * (r + 1)].rearrange(
                "p (c m) -> p c m", c=2),
            in_=xtp_i[256 * r: 256 * (r + 1), :].rearrange("(c p) m -> p c m", p=128))))

    tok_xtn = []
    for i in range(2):
        dst = acc[:, CDIM * i: CDIM * (i + 1)]
        tk = p.op("dve", lambda e, dst=dst, i=i: e.tensor_copy(
            dst, xts[:, CDIM * i: CDIM * (i + 1)]), waits=[t_ld[0]])
        for r in range(1, 8):
            src = xts[:, 2 * CDIM * r + CDIM * i: 2 * CDIM * r + CDIM * (i + 1)]
            tk = p.op("dve", lambda e, dst=dst, src=src: e.tensor_add(dst, dst, src),
                      waits=[t_ld[r], tk])
        tk = p.op("dve", lambda e, dst=dst: e.tensor_scalar(
            dst, dst, CST(C_CLIPHI), CST(C_CLIPLO), ALU.min, ALU.max),
            waits=[tk, t_cst])
        xdst = xtn[:, CDIM * i: CDIM * (i + 1)]
        w_ln = [tk] + ([tok_xtn[0]] if i == 1 else [])
        tok_xtn.append(_ln_chain(p, CST, dst, stt[:, 12 * i: 12 * (i + 1)],
                                 sqs[:, :], xdst, AF.Tanh, w_ln))

    t_sp = []
    for i in range(2):
        t_sp.append(p.dma("sp", lambda e, i=i: e.dma_start(
            out=xtn_sp[128 * i: 128 * (i + 1), :],
            in_=xtn[:, CDIM * i: CDIM * (i + 1)]), waits=[tok_xtn[i]]))
    t_T = []
    for dc in range(8):
        t_T.append(p.dma("act", lambda e, dc=dc: e.dma_start(
            out=xtnT[:, 256 * dc: 256 * (dc + 1)],
            in_=xtn_sp[:, 128 * dc: 128 * (dc + 1)], transpose=True),
            waits=[t_sp[0], t_sp[1]]))

    tok_wx = {}

    def wx_load(el, extra=()):
        w = wxf[:, 8 * CDIM * (el % 2): 8 * CDIM * (el % 2 + 1)]
        tl = p.dma("pop", lambda e, el=el, w=w: e.dma_start(
            out=w.rearrange("p (c m) -> p c m", c=8),
            in_=wex_i[CDIM * el: CDIM * (el + 1), :].rearrange(
                "(c p) m -> p c m", p=128)), waits=list(extra))
        tcv = p.op("dve", lambda e, el=el, w=w: e.tensor_copy(
            wxb[:, 8 * CDIM * (el % 2): 8 * CDIM * (el % 2 + 1)], w),
            waits=[tl] + list(extra))
        tok_wx[el] = (tl, tcv)

    wx_load(0, extra=[t_bexb])
    wx_load(1)
    tok_ev = {}
    tok_st = {}
    prev_mm = None
    for el in range(8):
        def mk_ex(e, el=el):
            last = None
            for hh in range(2):
                for dc in range(8):
                    last = e.matmul(
                        PY[0:32, 512 * hh: 512 * (hh + 1)],
                        xtnT[:, 256 * dc + 32 * el: 256 * dc + 32 * (el + 1)],
                        wxb[:, 8 * CDIM * (el % 2) + CDIM * dc + 512 * hh:
                            8 * CDIM * (el % 2) + CDIM * dc + 512 * (hh + 1)],
                        start=(dc == 0), stop=False)
                last = e.matmul(PY[0:32, 512 * hh: 512 * (hh + 1)],
                                ones_r[0:1, 0:32],
                                bexb[0:1, CDIM * el + 512 * hh: CDIM * el + 512 * (hh + 1)],
                                start=False, stop=True)
            return last
        w_ex = t_T + [tok_wx[el][1], t_bexb, t_ones,
                tok_ev.get(el - 1)]
        t_mm = p.op("pe", mk_ex, waits=w_ex)
        prev_mm = t_mm
        dst = ystg[:, CDIM * (el % 2): CDIM * (el % 2 + 1)]
        w_ev = [t_mm] + ([tok_st[el - 2]] if el >= 2 else [])
        tok_ev[el] = p.op("dve", lambda e, dst=dst: e.tensor_copy(dst, PY[0:32, :]),
                          waits=w_ev)
        tok_st[el] = p.dma("sp", lambda e, el=el, dst=dst: e.dma_start(
            out=ytl_o[32 * el: 32 * (el + 1), :], in_=dst), waits=[tok_ev[el]])
        if el + 2 < 8:
            wx_load(el + 2, extra=[tok_wx[el][1], t_mm])

    p.op("sp", lambda e: None,
         waits=p.qlast("sp") + p.qlast("act") + p.qlast("pop"), inc=0)
    p.emit()
    esP.close()
    es.close()
    return nc


def build_l3(T=4096):
    TC = T // 128
    NB = T // 512
    nc = bass.Bass("TRN2", target_bir_lowering=False)
    x_i = nc.dram_tensor("x", [T, CDIM], F32, kind="ExternalInput").ap()
    yt_i = nc.dram_tensor("yt", [CDIM, T], BF16, kind="ExternalInput").ap()
    csp_i = nc.dram_tensor("c_sp", [T, EDIM], BF16, kind="ExternalInput").ap()
    wo_i = nc.dram_tensor("wo", [CDIM, CDIM], F32, kind="ExternalInput").ap()
    bor_i = nc.dram_tensor("bor", [1, CDIM], F32, kind="ExternalInput").ap()
    ytl_i = nc.dram_tensor("ytl", [EDIM, CDIM], BF16, kind="ExternalInput").ap()
    out_o = nc.dram_tensor("out", [T, CDIM], F32, kind="ExternalOutput").ap()

    p = Prog(nc)
    es = contextlib.ExitStack()
    ones_r = _alloc(es, nc, "ones_r", [1, 128], BF16)
    wof = _alloc(es, nc, "wof", [128, 8 * CDIM], F32)
    wob = _alloc(es, nc, "wob", [128, 8 * CDIM], BF16)
    bof = _alloc(es, nc, "bof", [1, CDIM], F32)
    bob = _alloc(es, nc, "bob", [1, CDIM], BF16)
    ytl = _alloc(es, nc, "ytl_sb", [128, 16 * CDIM], BF16)
    ytsb = _alloc(es, nc, "ytsb", [128, 8 * T], BF16)
    ctb = _alloc(es, nc, "ctb", [128, 2 * 16 * 512], BF16)
    xb = _alloc(es, nc, "xb", [128, 2 * CDIM], F32)
    ob = _alloc(es, nc, "ob", [128, 2 * CDIM], F32)
    esP = contextlib.ExitStack()
    PO = _psum(esP, nc, "PO", [128, 2048])

    t_ones = p.op("dve", lambda e: e.memset(ones_r[:, :], 1.0))
    t_wo = p.dma("sp", lambda e: e.dma_start(
        out=wof[:, :].rearrange("p (c m) -> p c m", c=8),
        in_=wo_i.rearrange("(c p) m -> p c m", p=128)))
    t_wob = p.op("dve", lambda e: e.tensor_copy(wob[:, :], wof[:, :]), waits=[t_wo])
    t_bo = p.dma("sp", lambda e: e.dma_start(out=bof[:, :], in_=bor_i))
    t_bob = p.op("dve", lambda e: e.tensor_copy(bob[:, :], bof[:, :]), waits=[t_bo])
    t_ytl = p.dma("sp", lambda e: e.dma_start(
        out=ytl[:, :].rearrange("p (c m) -> p c m", c=16),
        in_=ytl_i.rearrange("(c p) m -> p c m", p=128)))
    t_yt = []
    for c in range(8):
        t_yt.append(p.dma("sp", lambda e, c=c: e.dma_start(
            out=ytsb[:, T * c: T * (c + 1)], in_=yt_i[128 * c: 128 * (c + 1), :])))

    tok_ct = [None] * NB
    tok_blk = [None] * NB
    for b in range(NB):
        sl = b % 2
        tks = []
        for ec in range(16):
            w_ct = [tok_blk[b - 2]] if b >= 2 else []
            tks.append(p.dma("act", lambda e, b=b, sl=sl, ec=ec: e.dma_start(
                out=ctb[:, 8192 * sl + 512 * ec: 8192 * sl + 512 * (ec + 1)],
                in_=csp_i[512 * b: 512 * (b + 1), 128 * ec: 128 * (ec + 1)],
                transpose=True), waits=w_ct))
        tok_ct[b] = tks

        for tt in range(4 * b, min(4 * b + 4, TC)):
            xsl = xb[:, CDIM * (tt % 2): CDIM * (tt % 2 + 1)]
            osl = ob[:, CDIM * (tt % 2): CDIM * (tt % 2 + 1)]
            w_x = [tok_ev[(tt - 2, 0)], tok_ev[(tt - 2, 1)]] if tt >= 2 else []
            t_x = p.dma("sp", lambda e, xsl=xsl, tt=tt: e.dma_start(
                out=xsl, in_=x_i[128 * tt: 128 * (tt + 1), :]), waits=w_x)
            if tt == 0:
                tok_ev = {}
                tok_stt = {}
            t_mm_last = None
            for hh in range(2):
                pb = PO[:, 1024 * (tt % 2) + 512 * hh:
                        1024 * (tt % 2) + 512 * (hh + 1)]

                def mk_o(e, tt=tt, hh=hh, pb=pb, sl=sl):
                    last = None
                    for c in range(8):
                        last = e.matmul(
                            pb, ytsb[:, T * c + 128 * tt: T * c + 128 * (tt + 1)],
                            wob[:, CDIM * c + 512 * hh: CDIM * c + 512 * (hh + 1)],
                            start=(c == 0), stop=False)
                    for ec in range(16):
                        last = e.matmul(
                            pb,
                            ctb[:, 8192 * sl + 512 * ec + 128 * (tt % 4):
                                8192 * sl + 512 * ec + 128 * (tt % 4) + 128],
                            ytl[:, CDIM * ec + 512 * hh: CDIM * ec + 512 * (hh + 1)],
                            start=False, stop=False)
                    return e.matmul(pb, ones_r[:, :], bob[:, 512 * hh: 512 * (hh + 1)],
                                    start=False, stop=True)
                w_o = [t_wob, t_bob, t_ytl, t_ones] + t_yt + tok_ct[b]
                if tt >= 2:
                    w_o.append(tok_ev[(tt - 2, hh)])
                t_mm = p.op("pe", mk_o, waits=w_o)
                t_mm_last = t_mm
                w_ev = [t_mm, t_x] + ([tok_stt[tt - 2]] if tt >= 2 else [])
                tok_ev[(tt, hh)] = p.op(
                    "dve", lambda e, osl=osl, pb=pb, hh=hh, xsl=xsl: e.tensor_add(
                        osl[:, 512 * hh: 512 * (hh + 1)], pb,
                        xsl[:, 512 * hh: 512 * (hh + 1)]), waits=w_ev)
            tok_stt[tt] = p.dma("sp", lambda e, osl=osl, tt=tt: e.dma_start(
                out=out_o[128 * tt: 128 * (tt + 1), :], in_=osl),
                waits=[tok_ev[(tt, 0)], tok_ev[(tt, 1)]])
            if tt % 4 == 3:
                tok_blk[tt // 4] = t_mm_last

    p.op("sp", lambda e: None, waits=p.qlast("sp") + p.qlast("act"), inc=0)
    p.emit()
    esP.close()
    es.close()
    return nc


_NC_CACHE = {}


def _get(builder, key, *args):
    if key not in _NC_CACHE:
        _NC_CACHE[key] = builder(*args)
    return _NC_CACHE[key]


def kernel(L0_attn_x, L0_attn_Wq, L0_attn_bq, L0_attn_Wk, L0_attn_bk,
           L0_attn_Wv, L0_attn_bv, L0_attn_Wo, L0_attn_bo,
           L0_ffn_phi_w, L0_ffn_phi_b, L0_ffn_expert_w, L0_ffn_expert_b):
    T = L0_attn_x.shape[1]
    f32 = np.float32
    cc = np.ascontiguousarray
    consts = np.tile(np.asarray(CONST_VALS, f32), (128, 1))
    tri = np.asarray((np.arange(128)[:, None] <= np.arange(128)[None, :]),
                     dtype=ml_dtypes.bfloat16)

    x = np.asarray(L0_attn_x, f32)
    phiw = cc(np.asarray(L0_ffn_phi_w, f32))
    phib = cc(np.asarray(L0_ffn_phi_b, f32).reshape(1, EDIM))
    wq = cc(np.asarray(L0_attn_Wq, f32))
    wk = cc(np.asarray(L0_attn_Wk, f32))
    wv = cc(np.asarray(L0_attn_Wv, f32))
    wo = cc(np.asarray(L0_attn_Wo, f32))
    bqc = cc(np.asarray(L0_attn_bq, f32).reshape(CDIM, 1))
    bkc = cc(np.asarray(L0_attn_bk, f32).reshape(CDIM, 1))
    bvr = cc(np.asarray(L0_attn_bv, f32).reshape(1, CDIM))
    bor = cc(np.asarray(L0_attn_bo, f32).reshape(1, CDIM))
    wex = np.asarray(L0_ffn_expert_w, f32)
    bex = np.asarray(L0_ffn_expert_b, f32)

    nc1 = _get(build_l1, ("l1", T), T)
    in1 = [dict(x=cc(x[b]), phiw=phiw, phib=phib, wq=wq, wk=wk, wv=wv,
                bqc=bqc, bkc=bkc, bvr=bvr, tri=tri, consts=consts)
           for b in range(NCORES)]
    r1 = run_bass_kernel_spmd(nc1, in1, core_ids=list(range(NCORES))).results

    nc2 = _get(build_l2, ("l2",))
    in2 = []
    for r in range(NCORES):
        xtp = cc(np.stack([np.asarray(r1[b]["xt_part"])[256 * r: 256 * (r + 1), :]
                           for b in range(NCORES)]).reshape(8 * 256, CDIM))
        in2.append(dict(xtp=xtp,
                        wex=cc(wex[8 * r: 8 * (r + 1)].reshape(8 * CDIM, CDIM)),
                        bex=cc(bex[8 * r: 8 * (r + 1)].reshape(1, 8 * CDIM)),
                        consts=consts))
    r2 = run_bass_kernel_spmd(nc2, in2, core_ids=list(range(NCORES))).results

    ytl_full = cc(np.concatenate([np.asarray(r2[r]["ytl"]) for r in range(NCORES)],
                                 axis=0))
    nc3 = _get(build_l3, ("l3", T), T)
    in3 = [dict(x=cc(x[b]), yt=cc(np.asarray(r1[b]["yt_sp"])),
                c_sp=cc(np.asarray(r1[b]["c_sp"])), wo=wo, bor=bor, ytl=ytl_full)
           for b in range(NCORES)]
    r3 = run_bass_kernel_spmd(nc3, in3, core_ids=list(range(NCORES))).results

    return np.stack([np.asarray(r3[b]["out"], f32) for b in range(NCORES)])


# version 17
# speedup vs baseline: 1.0885x; 1.0312x over previous
import contextlib

import numpy as np
import ml_dtypes
import concourse.bass as bass
import concourse.mybir as mybir
from concourse.bass_utils import run_bass_kernel_spmd

F32 = mybir.dt.float32
BF16 = mybir.dt.bfloat16
AF = mybir.ActivationFunctionType
ALU = mybir.AluOpType
AX = mybir.AxisListType

NCORES = 8
CDIM = 1024
EDIM = 2048
NEXP, PSLOT = 64, 32
NPAIR = 8
EPS = 1e-5

C_INVC, C_EPS, C_NHALF, C_FFNB, C_ATTNS, C_ATTNB = 0, 1, 2, 3, 4, 5
C_CLIPLO, C_CLIPHI, C_ONE, C_ZERO, C_NEG1 = 6, 7, 8, 9, 10
NCONST = 11
CONST_VALS = [1.0 / CDIM, EPS, -0.5, -2.0, 0.125, -4.0,
              -33000.0, 65000.0, 1.0, 0.0, -1.0]


class Prog:

    ENGS = ("pe", "act", "dve", "pop", "sp")
    QDEPTH = {"sp": 8, "act": 6, "pop": 4}

    def __init__(self, nc):
        self.nc = nc
        self.ops = {e: [] for e in self.ENGS}
        self.cnt = {s: 0 for s in self.ENGS}
        for e, d in self.QDEPTH.items():
            for j in range(d):
                self.cnt[f"q_{e}{j}"] = 0
        self.qn = {e: 0 for e in self.QDEPTH}

    def op(self, eng, builder, waits=(), inc=1, sem=None):
        sem = sem or eng
        self.cnt[sem] += inc
        tok = (sem, self.cnt[sem])
        self.ops[eng].append((builder, inc, sem, [w for w in waits if w]))
        return tok

    def dma(self, eng, builder, waits=()):
        j = self.qn[eng] % self.QDEPTH[eng]
        self.qn[eng] += 1
        sem = f"q_{eng}{j}"
        auto = [(sem, self.cnt[sem])] if self.cnt[sem] else []
        return self.op(eng, builder, waits=list(waits) + auto, inc=16, sem=sem)

    def last(self, sem):
        return (sem, self.cnt[sem])

    def qlast(self, eng):
        return [(f"q_{eng}{j}", self.cnt[f"q_{eng}{j}"])
                for j in range(self.QDEPTH[eng])
                if self.cnt[f"q_{eng}{j}"]]

    def emit(self):
        nc = self.nc
        es = contextlib.ExitStack()
        sems = {s: es.enter_context(nc.semaphore("sem_" + s))
                for s in self.cnt}
        with nc.Block() as blk:
            sect = {"pe": blk.tensor, "act": blk.scalar, "dve": blk.vector,
                    "pop": blk.gpsimd, "sp": blk.sync}

            def mk(ename):
                ops = self.ops[ename]

                def run(eng):
                    waited = {}
                    for builder, inc, sem, waits in ops:
                        for (s, v) in waits:
                            if waited.get(s, 0) < v:
                                eng.wait_ge(sems[s], v)
                                waited[s] = v
                        inst = builder(eng)
                        if inst is not None and inc > 0:
                            inst.then_inc(sems[sem], inc)
                return run

            for e in self.ENGS:
                sect[e](mk(e))
        es.close()


def _alloc(es, nc, name, shape, dt):
    return es.enter_context(nc.sbuf_tensor(name, shape, dt))


def _psum(es, nc, name, shape, dt=F32):
    return es.enter_context(nc.psum_tensor(name, shape, dt))


def _ln_chain(p, CST, src, st, sqs, dst_act, dst_fn, waits):
    s1, sq, m, msq, t1, var, lnv, rstd, nmr = (st[:, j:j + 1] for j in range(9))
    ta = p.op("dve", lambda e: e.reduce_sum(s1, src, AX.X), waits=waits)
    tb = p.op("act", lambda e: e.activation(sqs, src, AF.Square, accum_out=sq),
              waits=waits)
    tm = p.op("dve", lambda e: e.tensor_scalar_mul(m, s1, CST(C_INVC)), waits=[ta])
    tmsq = p.op("dve", lambda e: e.tensor_scalar_mul(msq, sq, CST(C_INVC)), waits=[tb])
    tt1 = p.op("dve", lambda e: e.tensor_mul(t1, m, m), waits=[tm])
    tvar = p.op("dve", lambda e: e.tensor_sub(var, msq, t1), waits=[tmsq, tt1])
    tlnv = p.op("act", lambda e: e.activation(lnv, var, AF.Ln, bias=CST(C_EPS)),
                waits=[tvar])
    trstd = p.op("act", lambda e: e.activation(rstd, lnv, AF.Exp, scale=CST(C_NHALF)),
                 waits=[tlnv])
    tnmr = p.op("dve", lambda e: e.tensor_scalar(nmr, m, rstd, CST(C_NEG1),
                                                 ALU.mult, ALU.mult), waits=[trstd])
    return p.op("act", lambda e: e.activation(dst_act, src, dst_fn,
                                              scale=rstd, bias=nmr),
                waits=[tnmr] + list(waits))


def build_l1(T=4096):
    TC = T // 128
    NQT = T // 512
    spc = min(8, TC)
    XN_RING = spc + 6
    LAG = 6

    nc = bass.Bass("TRN2", target_bir_lowering=False)
    x_i = nc.dram_tensor("x", [T, CDIM], F32, kind="ExternalInput").ap()
    phiw_i = nc.dram_tensor("phiw", [EDIM, CDIM], F32, kind="ExternalInput").ap()
    phib_i = nc.dram_tensor("phib", [1, EDIM], F32, kind="ExternalInput").ap()
    wq_i = nc.dram_tensor("wq", [CDIM, CDIM], F32, kind="ExternalInput").ap()
    wk_i = nc.dram_tensor("wk", [CDIM, CDIM], F32, kind="ExternalInput").ap()
    wv_i = nc.dram_tensor("wv", [CDIM, CDIM], F32, kind="ExternalInput").ap()
    bqc_i = nc.dram_tensor("bqc", [CDIM, 1], F32, kind="ExternalInput").ap()
    bkc_i = nc.dram_tensor("bkc", [CDIM, 1], F32, kind="ExternalInput").ap()
    bvr_i = nc.dram_tensor("bvr", [1, CDIM], F32, kind="ExternalInput").ap()
    tri_i = nc.dram_tensor("tri", [128, 128], BF16, kind="ExternalInput").ap()
    cst_i = nc.dram_tensor("consts", [128, NCONST], F32, kind="ExternalInput").ap()
    xt_o = nc.dram_tensor("xt_part", [EDIM, CDIM], F32, kind="ExternalOutput").ap()
    csp_o = nc.dram_tensor("c_sp", [T, EDIM], BF16, kind="ExternalOutput").ap()
    yt_o = nc.dram_tensor("yt_sp", [CDIM, T], BF16, kind="ExternalOutput").ap()
    xnsp = nc.dram_tensor("xn_sp", [T, CDIM], BF16).ap()
    phibf = nc.dram_tensor("phi_bf", [EDIM, CDIM], BF16).ap()

    p = Prog(nc)
    es = contextlib.ExitStack()
    cst = _alloc(es, nc, "cst", [128, NCONST], F32)
    ones_r = _alloc(es, nc, "ones_r", [1, 128], BF16)
    tri = _alloc(es, nc, "tri_sb", [128, 128], BF16)

    def CST(j):
        return cst[:, j:j + 1]

    esA = contextlib.ExitStack()
    phiT = _alloc(esA, nc, "phiT", [128, 8 * EDIM], BF16)
    phibb = _alloc(esA, nc, "phibb", [1, EDIM], BF16)
    esStg = contextlib.ExitStack()
    phibf32 = _alloc(esStg, nc, "phibf32", [1, EDIM], F32)
    pstg = _alloc(esStg, nc, "pstg", [128, 4 * CDIM], F32)
    pstgb = _alloc(esStg, nc, "pstgb", [128, 4 * CDIM], BF16)
    esStg.close()
    xb = _alloc(esA, nc, "xb", [128, 3 * CDIM], F32)
    stt = _alloc(esA, nc, "stt", [128, 3 * 12], F32)
    xnb = _alloc(esA, nc, "xnb", [128, XN_RING * CDIM], BF16)
    xnTs = _alloc(esA, nc, "xnTs", [128, 2 * 512 * 8], BF16)
    Et = _alloc(esA, nc, "Et", [128, 2 * EDIM], BF16)
    dsum = _alloc(esA, nc, "dsum", [128, 2 * 64], F32)
    csum = _alloc(esA, nc, "csum", [128, 2 * 32], F32)
    drec = _alloc(esA, nc, "drec", [128, 2 * 64], F32)
    crec = _alloc(esA, nc, "crec", [128, 2 * 32], F32)
    Dbuf = _alloc(esA, nc, "Dbuf", [128, spc * EDIM], BF16)
    Cbuf = _alloc(esA, nc, "Cbuf", [128, 2 * EDIM], BF16)
    xtacc = _alloc(esA, nc, "xtacc", [128, 16 * CDIM], F32)

    esP = contextlib.ExitStack()
    P1 = _psum(esP, nc, "P1", [128, 2048])
    P2a = _psum(esP, nc, "P2a", [128, 512])
    P2b = _psum(esP, nc, "P2b", [128, 512])
    P4 = _psum(esP, nc, "P4", [128, 1024])

    t_cst = p.dma("sp", lambda e: e.dma_start(out=cst[:, :], in_=cst_i))
    t_tri = p.dma("sp", lambda e: e.dma_start(out=tri[:, :], in_=tri_i))
    t_ones = p.op("dve", lambda e: e.memset(ones_r[:, :], 1.0))
    t_phibld = p.dma("sp", lambda e: e.dma_start(out=phibf32[:, :], in_=phib_i))
    t_phibb = p.op("dve", lambda e: e.tensor_copy(phibb[:, :], phibf32[:, :]),
                   waits=[t_phibld])

    t_sp_prev = None
    t_cv_prev = None
    for r in range(4):
        rows = phiw_i[512 * r: 512 * (r + 1), :]
        tl = p.dma("pop", lambda e, rows=rows: e.dma_start(
            out=pstg[:, :].rearrange("p (c m) -> p c m", c=4),
            in_=rows.rearrange("(c p) m -> p c m", p=128)),
            waits=[t_cv_prev] if t_cv_prev else [])
        tc = p.op("dve", lambda e: e.tensor_copy(pstgb[:, :], pstg[:, :]),
                  waits=[tl, t_sp_prev] if t_sp_prev else [tl])
        t_sp_prev = p.dma("pop", lambda e, r=r: e.dma_start(
            out=phibf[512 * r: 512 * (r + 1), :].rearrange("(c p) m -> p c m", p=128),
            in_=pstgb[:, :].rearrange("p (c m) -> p c m", c=4)),
            waits=[tc])
        t_cv_prev = tc
    t_phiT = []
    for c in range(8):
        t_phiT.append(p.dma("sp", lambda e, c=c: e.dma_start(
            out=phiT[:, EDIM * c: EDIM * (c + 1)],
            in_=phibf[:, 128 * c: 128 * (c + 1)], transpose=True),
            waits=[t_sp_prev]))
    t_stg_free = [t_cv_prev, t_sp_prev, t_phibb]

    tok_x = [None] * TC
    tok_xn = [None] * TC
    tok_xnsp = [None] * TC
    tok_sup = [None] * ((TC + 3) // 4)
    tok_E = [None] * TC
    tok_D = [None] * TC
    tok_C = [None] * TC
    tok_csp = [None] * TC
    tok_logits = [None] * TC
    tok_xtmm = {}
    tok_xtadd = {}
    xt_bank_last = {0: None, 1: None, 2: None, 3: None}
    n_grp = (TC + spc - 1) // spc


    def x_load(t):
        sl = t % 3
        xv = xb[:, CDIM * sl: CDIM * (sl + 1)]
        w = [t_cst] + ([tok_xn[t - 3]] if t >= 3 else [])
        if t == 0:
            w += t_stg_free
        tok_x[t] = p.dma("sp", lambda e, xv=xv, t=t: e.dma_start(
            out=xv, in_=x_i[128 * t: 128 * (t + 1), :]), waits=w)

    def ln_part(t):
        sl = t % 3
        xv = xb[:, CDIM * sl: CDIM * (sl + 1)]
        st = stt[:, 12 * sl: 12 * (sl + 1)]
        xnv = xnb[:, CDIM * (t % XN_RING): CDIM * (t % XN_RING + 1)]
        w_xn = [tok_x[t]] + ([tok_xn[t - 3]] if t >= 3 else [])
        if t >= XN_RING:
            gprev = (t - XN_RING) // spc
            while (gprev, 15, 1) not in tok_xtmm:
                drain_xt(1)
            w_xn.append(tok_xtmm[(gprev, 15, 1)])
        tok_xn[t] = _ln_chain(p, CST, xv, st, xnv, xnv, AF.Identity, w_xn)
        tok_xnsp[t] = p.dma("sp", lambda e: e.dma_start(
            out=xnsp[128 * t: 128 * (t + 1), :], in_=xnv), waits=[tok_xn[t]])
        if t % 4 == 3:
            s = t // 4
            sv = xnTs[:, 4096 * (s % 2): 4096 * (s % 2 + 1)]
            w_sup = [tok_xnsp[4 * (t // 4) + j] for j in range(4)]
            if s >= 2:
                w_sup.append(tok_logits[4 * (s - 2) + 3])
            tks = []
            for c in range(8):
                tks.append(p.dma("sp", lambda e, sv=sv, s=s, c=c: e.dma_start(
                    out=sv[:, 512 * c: 512 * (c + 1)],
                    in_=xnsp[512 * s: 512 * (s + 1), 128 * c: 128 * (c + 1)],
                    transpose=True), waits=w_sup))
            tok_sup[s] = tks

    def ffn_pe_act(t):
        s = t // 4
        sv = xnTs[:, 4096 * (s % 2): 4096 * (s % 2 + 1)]
        esl = Et[:, EDIM * (t % 2): EDIM * (t % 2 + 1)]
        t_eh = []
        for h in range(2):
            def mk_lg(e, sv=sv, t=t, h=h):
                last = None
                for c in range(8):
                    lh = sv[:, 512 * c + 128 * (t % 4): 512 * c + 128 * (t % 4) + 128]
                    for g in (2 * h, 2 * h + 1):
                        last = e.matmul(
                            P1[:, 512 * g: 512 * (g + 1)], lh,
                            phiT[:, EDIM * c + 512 * g: EDIM * c + 512 * (g + 1)],
                            start=(c == 0), stop=False)
                for g in (2 * h, 2 * h + 1):
                    last = e.matmul(P1[:, 512 * g: 512 * (g + 1)], ones_r[:, :],
                                    phibb[:, 512 * g: 512 * (g + 1)],
                                    start=False, stop=True)
                return last
            w_lg = tok_sup[s] + t_phiT + [t_phibb, t_ones]
            if t >= 1:
                w_lg.append(ffn_pe_act.exp_h[h])
            t_lg = p.op("pe", mk_lg, waits=w_lg)
            w_E = [t_lg] + ([tok_C[t - 2], tok_D[t - 2]] if t >= 2 else [])
            te = p.op("act", lambda e, esl=esl, h=h: e.activation(
                esl[:, 1024 * h: 1024 * (h + 1)], P1[:, 1024 * h: 1024 * (h + 1)],
                AF.Exp, bias=CST(C_FFNB)), waits=w_E)
            t_eh.append(te)
            ffn_pe_act.exp_h[h] = te
        tok_logits[t] = t_lg
        tok_E[t] = t_eh[1]

        ffn_state[t] = (esl, t_eh)

    def ffn_dve(t):
        esl, t_eh = ffn_state.pop(t)
        dsl = dsum[:, 64 * (t % 2): 64 * (t % 2 + 1)]
        csl = csum[:, 32 * (t % 2): 32 * (t % 2 + 1)]
        drl = drec[:, 64 * (t % 2): 64 * (t % 2 + 1)]
        crl = crec[:, 32 * (t % 2): 32 * (t % 2 + 1)]
        w_slot = [tok_D[t - 2], tok_C[t - 2]] if t >= 2 else []
        td1 = p.op("dve", lambda e, dsl=dsl, esl=esl: e.reduce_sum(
            dsl, esl.rearrange("p (n w) -> p n w", w=PSLOT), AX.X),
            waits=[t_eh[0], t_eh[1]] + w_slot)
        tc1 = p.op("dve", lambda e, csl=csl, esl=esl: e.reduce_sum(
            csl, esl.rearrange("p (n w) -> p w n", w=PSLOT), AX.X),
            waits=[t_eh[0], t_eh[1]] + w_slot)
        td2 = p.op("dve", lambda e, drl=drl, dsl=dsl: e.reciprocal(drl, dsl),
                   waits=[td1])
        tc2 = p.op("dve", lambda e, crl=crl, csl=csl: e.reciprocal(crl, csl),
                   waits=[tc1])
        dv = Dbuf[:, EDIM * (t % spc): EDIM * (t % spc + 1)]
        w_D = [td2]
        if t >= spc:
            gp = (t - spc) // spc
            while (gp, 15, 1) not in tok_xtmm:
                drain_xt(1)
            w_D.append(tok_xtmm[(gp, 15, 1)])
        tok_D[t] = p.op("dve", lambda e, dv=dv, esl=esl, drl=drl: e.tensor_tensor(
            dv.rearrange("p (n w) -> p n w", w=PSLOT),
            esl.rearrange("p (n w) -> p n w", w=PSLOT),
            drl.unsqueeze(2).broadcast_to([128, NEXP, PSLOT]), ALU.mult), waits=w_D)
        cv = Cbuf[:, EDIM * (t % 2): EDIM * (t % 2 + 1)]
        w_C = [tc2] + ([tok_csp[t - 2]] if t >= 2 else [])
        tok_C[t] = p.op("dve", lambda e, cv=cv, esl=esl, crl=crl: e.tensor_tensor(
            cv.rearrange("p (n w) -> p n w", w=PSLOT),
            esl.rearrange("p (n w) -> p n w", w=PSLOT),
            crl.unsqueeze(1).broadcast_to([128, NEXP, PSLOT]), ALU.mult), waits=w_C)
        tok_csp[t] = p.dma("sp", lambda e, cv=cv, t=t: e.dma_start(
            out=csp_o[128 * t: 128 * (t + 1), :], in_=cv), waits=[tok_C[t]])

        if (t + 1) % spc == 0 or t == TC - 1:
            g = t // spc
            g_lo = g * spc
            g_n = t - g_lo + 1
            for et in range(16):
                for dh in range(2):
                    xt_tasks.append((g, g_lo, g_n, et, dh, tok_D[t]))

    XT_BANKS = (P2a[:, 0:512], P2b[:, 0:512], P4[:, 0:512], P4[:, 512:1024])

    def drain_xt(k):
        while xt_tasks and k > 0:
            k -= 1
            g, g_lo, g_n, et, dh, t_d = xt_tasks.pop(0)
            bk = drain_xt.n % 4
            drain_xt.n += 1
            pt = XT_BANKS[bk]
            w_mm = [t_d, xt_bank_last[bk]]

            def mk_xt(e, g_lo=g_lo, g_n=g_n, et=et, dh=dh, pt=pt):
                last = None
                for j in range(g_n):
                    tch = g_lo + j
                    dvj = Dbuf[:, EDIM * (tch % spc) + 128 * et:
                               EDIM * (tch % spc) + 128 * (et + 1)]
                    xnj = xnb[:, CDIM * (tch % XN_RING) + 512 * dh:
                              CDIM * (tch % XN_RING) + 512 * (dh + 1)]
                    last = e.matmul(pt, dvj, xnj,
                                    start=(j == 0), stop=(j == g_n - 1))
                return last
            tok_xtmm[(g, et, dh)] = p.op("pe", mk_xt, waits=w_mm)
            dst = xtacc[:, CDIM * et + 512 * dh: CDIM * et + 512 * (dh + 1)]
            if g == 0:
                tk = p.op("dve", lambda e, dst=dst, pt=pt: e.tensor_copy(
                    dst, pt), waits=[tok_xtmm[(g, et, dh)]])
            else:
                tk = p.op("dve", lambda e, dst=dst, pt=pt: e.tensor_add(
                    dst, dst, pt),
                    waits=[tok_xtmm[(g, et, dh)], tok_xtadd[(g - 1, et, dh)]])
            tok_xtadd[(g, et, dh)] = tk
            xt_bank_last[bk] = tk

    drain_xt.n = 0
    xt_tasks = []
    ffn_pe_act.exp_h = {0: None, 1: None}
    ffn_state = {}
    ffn_pe_act.exp_h = {0: None, 1: None}
    ffn_state = {}
    for it in range(TC + LAG):
        if it == 0:
            x_load(0)
            x_load(1)
            x_load(2)
        elif it + 2 < TC:
            x_load(it + 2)
        if it >= LAG:
            ffn_pe_act(it - LAG)
            drain_xt(5)
        if it < TC:
            ln_part(it)
        if it >= LAG:
            ffn_dve(it - LAG)

    drain_xt(10 ** 9)
    for et in range(16):
        w = [tok_xtadd[(n_grp - 1, et, 0)], tok_xtadd[(n_grp - 1, et, 1)]]
        p.dma("sp", lambda e, et=et: e.dma_start(
            out=xt_o[128 * et: 128 * (et + 1), :],
            in_=xtacc[:, CDIM * et: CDIM * (et + 1)]), waits=w)

    bar = ([p.last(s) for s in ("pe", "act", "dve", "pop", "sp")]
           + p.qlast("sp") + p.qlast("act") + p.qlast("pop"))
    esA.close()

    esB = contextlib.ExitStack()
    xnT = _alloc(esB, nc, "xnT", [128, 8 * T], BF16)
    Vp = _alloc(esB, nc, "Vp", [128, TC * NPAIR * 130], BF16)
    esW = contextlib.ExitStack()
    wvf = _alloc(esW, nc, "wvf", [128, 2 * 4096], F32)
    wvb = _alloc(esW, nc, "wvb", [128, 8 * CDIM], BF16)
    bvf = _alloc(esW, nc, "bvf", [1, CDIM], F32)
    bvb = _alloc(esW, nc, "bvb", [1, CDIM], BF16)

    t_xnT = []
    for c in range(8):
        t_xnT.append(p.dma("sp", lambda e, c=c: e.dma_start(
            out=xnT[:, T * c: T * (c + 1)],
            in_=xnsp[:, 128 * c: 128 * (c + 1)], transpose=True), waits=bar))
    t_wv = []
    for r in range(2):
        tl = p.dma("pop", lambda e, r=r: e.dma_start(
            out=wvf[:, 4096 * r: 4096 * (r + 1)].rearrange("p (c m) -> p c m", c=4),
            in_=wv_i[512 * r: 512 * (r + 1), :].rearrange("(c p) m -> p c m", p=128)),
            waits=bar)
        t_wv.append(p.op("dve", lambda e, r=r: e.tensor_copy(
            wvb[:, 4096 * r: 4096 * (r + 1)], wvf[:, 4096 * r: 4096 * (r + 1)]),
            waits=[tl] + bar))
    t_bv = p.dma("pop", lambda e: e.dma_start(out=bvf[:, :], in_=bvr_i), waits=bar)
    t_bvb = p.op("dve", lambda e: e.tensor_copy(bvb[:, :], bvf[:, :]),
                 waits=[t_bv] + bar)
    t_vones = p.op("dve", lambda e: e.memset(
        Vp[:, :].rearrange("p (a s) -> p a s", s=65)[:, :, 64:65], 1.0), waits=bar)

    prev_vp = None
    for tch in range(TC):
        for hf in range(2):
            pt = P4[:, 512 * hf: 512 * (hf + 1)]

            def mk_v(e, tch=tch, hf=hf, pt=pt):
                last = None
                for c in range(8):
                    last = e.matmul(
                        pt, xnT[:, T * c + 128 * tch: T * c + 128 * (tch + 1)],
                        wvb[:, CDIM * c + 512 * hf: CDIM * c + 512 * (hf + 1)],
                        start=(c == 0), stop=False)
                return e.matmul(pt, ones_r[:, :], bvb[:, 512 * hf: 512 * (hf + 1)],
                                start=False, stop=True)
            w_v = t_xnT + [t_wv[-1], t_bvb, t_ones, prev_vp]
            tmm = p.op("pe", mk_v, waits=w_v)
            t_vp_mm_last = tmm
            base = 1040 * tch + 520 * hf
            prev_vp = p.op("dve", lambda e, base=base, pt=pt: e.tensor_copy(
                Vp[:, base: base + 520].rearrange(
                    "p (a b s) -> p a b s", a=4, b=2)[:, :, :, 0:64],
                pt.rearrange("p (a b s) -> p a b s", a=4, b=2)),
                waits=[tmm, t_vones])
    t_vp_done = prev_vp

    esW.close()
    qtb = _alloc(esB, nc, "qtb", [128, 2 * T], BF16)
    ktb = _alloc(esB, nc, "ktb", [128, 2 * T], BF16)
    wqf = _alloc(esB, nc, "wqf", [128, 2 * CDIM], F32)
    wkf = _alloc(esB, nc, "wkf", [128, 2 * CDIM], F32)
    wqb = _alloc(esB, nc, "wqb", [128, 2 * CDIM], BF16)
    wkb = _alloc(esB, nc, "wkb", [128, 2 * CDIM], BF16)
    bqt = _alloc(esB, nc, "bqt", [128, 2], F32)
    bkt = _alloc(esB, nc, "bkt", [128, 2], F32)
    pTb = _alloc(esB, nc, "pTb", [128, 4 * 1024], BF16)
    yscr = _alloc(esB, nc, "yscr", [65, 2 * 512], F32)
    rrow = _alloc(esB, nc, "rrow", [1, 2 * 512], BF16)
    ytb = _alloc(esB, nc, "ytb", [128, T], BF16)

    tok_w = {}
    tok_wcv = {}
    tok_yt_sp = {}
    t_norm_last = {}
    t_projmm_last = {}
    t_proj_of = {}
    slot_free = {0: None, 1: None}
    pT_free = {0: None, 1: None, 2: None, 3: None}
    p4_free = {0: t_vp_done, 1: t_vp_done}

    def w_loads(pr):
        sl = pr % 2
        w_war = ([tok_wcv[pr - 2], t_proj_of[pr - 2]] if pr >= 2
                 else bar + [t_vp_mm_last])
        t1 = p.dma("pop", lambda e, pr=pr, sl=sl: e.dma_start(
            out=wqf[:, CDIM * sl: CDIM * (sl + 1)].rearrange("p (c m) -> p c m", c=8),
            in_=wq_i[:, 128 * pr: 128 * (pr + 1)].rearrange("(c p) m -> p c m", p=128)),
            waits=w_war)
        t2 = p.dma("pop", lambda e, pr=pr, sl=sl: e.dma_start(
            out=wkf[:, CDIM * sl: CDIM * (sl + 1)].rearrange("p (c m) -> p c m", c=8),
            in_=wk_i[:, 128 * pr: 128 * (pr + 1)].rearrange("(c p) m -> p c m", p=128)),
            waits=w_war)
        t3 = p.dma("pop", lambda e, pr=pr, sl=sl: e.dma_start(
            out=bqt[:, sl:sl + 1], in_=bqc_i[128 * pr: 128 * (pr + 1), :]),
            waits=w_war)
        t4 = p.dma("pop", lambda e, pr=pr, sl=sl: e.dma_start(
            out=bkt[:, sl:sl + 1], in_=bkc_i[128 * pr: 128 * (pr + 1), :]),
            waits=w_war)
        tok_w[pr] = [t1, t2, t3, t4]

    def w_convert(pr):
        sl = pr % 2
        w_war = [t_projmm_last[pr - 2]] if pr >= 2 else []
        ta = p.op("dve", lambda e, sl=sl: e.tensor_copy(
            wqb[:, CDIM * sl: CDIM * (sl + 1)], wqf[:, CDIM * sl: CDIM * (sl + 1)]),
            waits=[tok_w[pr][0]] + w_war)
        tb = p.op("dve", lambda e, sl=sl: e.tensor_copy(
            wkb[:, CDIM * sl: CDIM * (sl + 1)], wkf[:, CDIM * sl: CDIM * (sl + 1)]),
            waits=[tok_w[pr][1]] + w_war)
        tok_wcv[pr] = tb
        return [ta, tb]

    def make_proj_pieces(pr, wcv):
        pieces = []
        sl = pr % 2
        for gi, (which, n) in enumerate(
                [(w, n) for w in range(2) for n in range(NQT)]):
            wb, bcol, dst = ((wqb, bqt, qtb), (wkb, bkt, ktb))[which]
            pp = gi % 2

            def emit(wb=wb, bcol=bcol, dst=dst, sl=sl, n=n, pp=pp,
                     which=which, pr=pr, wcv=wcv):
                def mk_p(e):
                    last = None
                    for c in range(8):
                        last = e.matmul(
                            P4[:, 512 * pp: 512 * (pp + 1)],
                            wb[:, CDIM * sl + 128 * c: CDIM * sl + 128 * (c + 1)],
                            xnT[:, T * c + 512 * n: T * c + 512 * (n + 1)],
                            start=(c == 0), stop=(c == 7))
                    return last
                tmm = p.op("pe", mk_p, waits=t_xnT + [p4_free[pp]] + wcv)
                t_projmm_last[pr] = tmm
                tev = p.op("dve", lambda e: e.tensor_scalar_add(
                    dst[:, T * sl + 512 * n: T * sl + 512 * (n + 1)],
                    P4[:, 512 * pp: 512 * (pp + 1)], bcol[:, sl:sl + 1]),
                    waits=[tmm, t_cst, tok_w[pr][2 + which]])
                p4_free[pp] = tev
                t_proj_of[pr] = tev
            pieces.append(emit)
        return pieces

    w_loads(0)
    wcv0 = w_convert(0)
    for piece in make_proj_pieces(0, wcv0):
        piece()

    for pr in range(NPAIR):
        sl = pr % 2
        t_proj = t_proj_of[pr]
        pieces = []
        if pr + 1 < NPAIR:
            w_loads(pr + 1)
            wcvn = w_convert(pr + 1)
            pieces = list(make_proj_pieces(pr + 1, wcvn))
        n_chunks_total = sum(4 * qt + 4 for qt in range(NQT))
        stride = max(1, n_chunks_total // (len(pieces) + 1)) if pieces else 10 ** 9
        chunks_done = 0

        ysl = ytb[:, :]
        prev_tcp = []
        prev_rep = t_norm_last.get(pr - 1)
        pending = None

        def flush_pending():
            nonlocal pending, prev_rep
            if pending is None:
                return
            fqt, ftcp, ftrc = pending
            pending = None

            def mk_rep(e):
                e.matmul(P4[0:64, 0:512], ones_r[0:1, 0:64], rrow[:, 0:512],
                         start=True, stop=True)
                return e.matmul(P4[0:64, 512:1024], ones_r[0:1, 0:64],
                                rrow[:, 512:1024], start=True, stop=True)
            t_rep = p.op("pe", mk_rep,
                         waits=ftrc + [t_proj, p4_free[0], p4_free[1]])
            prev_rep = t_rep
            tn = []
            for hl in range(2):
                dst = ysl[64 * hl: 64 * (hl + 1), 512 * fqt: 512 * (fqt + 1)]
                w_n = [t_rep, ftcp[hl]]
                if pr >= 1 and fqt == 0:
                    w_n.append(tok_yt_sp[pr - 1])
                tn.append(p.op("dve", lambda e, dst=dst, hl=hl: e.tensor_tensor(
                    dst, yscr[0:64, 512 * hl: 512 * (hl + 1)],
                    P4[0:64, 512 * hl: 512 * (hl + 1)], ALU.mult), waits=w_n))
            p4_free[0] = tn[0]
            p4_free[1] = tn[1]
            t_norm_last[pr] = tn[-1]

        flat = [(qt, kc) for qt in range(NQT) for kc in range(4 * qt + 4)]
        LOOK = 2
        tok_pready = {}
        qt_avdone = {}
        n_after = {}

        def emit_front(i):
            qt, kc = flat[i]
            slot = i % 2
            pslot = i % 4
            off = max(0, 128 * kc - 512 * qt)

            def mk_s(e, kc=kc, qt=qt, slot=slot, off=off, sl=sl):
                last = None
                for hl in range(2):
                    last = e.matmul(
                        P1[:, 1024 * slot + 512 * hl + off:
                           1024 * slot + 512 * (hl + 1)],
                        ktb[64 * hl: 64 * (hl + 1),
                            T * sl + 128 * kc: T * sl + 128 * (kc + 1)],
                        qtb[64 * hl: 64 * (hl + 1),
                            T * sl + 512 * qt + off: T * sl + 512 * (qt + 1)],
                        start=True, stop=True)
                return last
            t_s = p.op("pe", mk_s, waits=[t_proj, slot_free[slot]])

            pair = (off == 0 and i % 2 == 0 and i + 1 < len(flat)
                    and flat[i + 1][0] == qt and 128 * flat[i + 1][1] < 512 * qt)
            if pair:
                emit_front.hold = t_s
                tok_pready[i] = None
                return
            if emit_front.hold is not None:
                t_sprev = emit_front.hold
                emit_front.hold = None
                pbase = 1024 * (pslot - 1)
                t_e = p.op("act", lambda e, pbase=pbase: e.activation(
                    pTb[:, pbase: pbase + 2048], P1[:, :],
                    AF.Exp, bias=CST(C_ATTNB), scale=CST(C_ATTNS)),
                    waits=[t_s, t_sprev, pT_free[pslot - 1], pT_free[pslot]])
                tok_pready[i - 1] = t_e
                tok_pready[i] = t_e
                slot_free[0] = t_e
                slot_free[1] = t_e
                return
            if off == 0:
                t_e = p.op("act", lambda e, slot=slot, pslot=pslot: e.activation(
                    pTb[:, 1024 * pslot: 1024 * (pslot + 1)],
                    P1[:, 1024 * slot: 1024 * (slot + 1)],
                    AF.Exp, bias=CST(C_ATTNB), scale=CST(C_ATTNS)),
                    waits=[t_s, pT_free[pslot]])
            else:
                def mk_e(e, slot=slot, pslot=pslot, off=off):
                    e.activation(pTb[:, 1024 * pslot + off: 1024 * pslot + 512],
                                 P1[:, 1024 * slot + off: 1024 * slot + 512],
                                 AF.Exp, bias=CST(C_ATTNB), scale=CST(C_ATTNS))
                    return e.activation(
                        pTb[:, 1024 * pslot + 512 + off: 1024 * (pslot + 1)],
                        P1[:, 1024 * slot + 512 + off: 1024 * (slot + 1)],
                        AF.Exp, bias=CST(C_ATTNB), scale=CST(C_ATTNS))
                t_e = p.op("act", mk_e, waits=[t_s, pT_free[pslot]])
            slot_free[slot] = t_e

            if kc >= 4 * qt:
                def mk_m(e, pslot=pslot, off=off):
                    e.tensor_mul(
                        pTb[:, 1024 * pslot + off: 1024 * pslot + off + 128],
                        pTb[:, 1024 * pslot + off: 1024 * pslot + off + 128],
                        tri[:, :])
                    return e.tensor_mul(
                        pTb[:, 1024 * pslot + 512 + off:
                            1024 * pslot + 512 + off + 128],
                        pTb[:, 1024 * pslot + 512 + off:
                            1024 * pslot + 512 + off + 128],
                        tri[:, :])
                tok_pready[i] = p.op("dve", mk_m, waits=[t_e, t_tri])
            else:
                tok_pready[i] = t_e

        emit_front.hold = None
        def emit_back(j):
            nonlocal pending, prev_tcp
            qt, kc = flat[j]
            nkc = 4 * qt + 4
            pslot = j % 4

            def mk_av(e, kc=kc, pslot=pslot, pr=pr, nkc=nkc, qt=qt):
                off = max(0, 128 * kc - 512 * qt)
                last = None
                for hl, pyt in ((0, P2a), (1, P2b)):
                    last = e.matmul(
                        pyt[0:65, off:512],
                        Vp[:, 1040 * kc + 130 * pr + 65 * hl:
                           1040 * kc + 130 * pr + 65 * (hl + 1)],
                        pTb[:, 1024 * pslot + 512 * hl + off:
                            1024 * pslot + 512 * (hl + 1)],
                        start=(kc == 0), stop=(kc == nkc - 1))
                return last
            w_av = [tok_pready[j]]
            if kc == 0:
                w_av += prev_tcp if prev_tcp else [t_norm_last.get(pr - 1)]
            t_av = p.op("pe", mk_av, waits=w_av)
            pT_free[pslot] = t_av
            if kc == min(5, nkc - 1):
                flush_pending()
            if kc == nkc - 1:
                w_cp0 = [t_av, t_norm_last.get(pr)] + ([t_norm_last.get(pr - 1)] if qt == 0 else [])
                tcp = []
                for hl, pyt in ((0, P2a), (1, P2b)):
                    tcp.append(p.op("dve", lambda e, hl=hl, pyt=pyt: e.tensor_copy(
                        yscr[0:64, 512 * hl: 512 * (hl + 1)], pyt[0:64, :]),
                        waits=w_cp0))
                t_cps = p.op("dve", lambda e: e.tensor_copy(
                    yscr[64:65, 0:512], P2a[64:65, 0:512]), waits=w_cp0)
                t_cps2 = p.op("dve", lambda e: e.tensor_copy(
                    yscr[64:65, 512:1024], P2b[64:65, 0:512]), waits=w_cp0)
                trc = []

                def mk_rc(e, hl):
                    with nc.allow_low_precision(reason="bf16 softmax-"
                                                "denominator reciprocal"):
                        return e.reciprocal(rrow[:, 512 * hl: 512 * (hl + 1)],
                                            yscr[64:65, 512 * hl: 512 * (hl + 1)])
                trc.append(p.op("dve", lambda e: mk_rc(e, 0),
                                waits=[t_cps, prev_rep]))
                trc.append(p.op("dve", lambda e: mk_rc(e, 1),
                                waits=[t_cps2, prev_rep]))
                pending = (qt, tcp, trc)
                prev_tcp = [t_cps2]

        for i in range(len(flat) + LOOK):
            if i < len(flat):
                emit_front(i)
            j = i - LOOK
            if j >= 0:
                emit_back(j)
                chunks_done += 1
                if pieces and chunks_done % stride == 0:
                    pieces.pop(0)()
        for piece in pieces:
            piece()
        pieces = []
        flush_pending()

        tok_yt_sp[pr] = p.dma("sp", lambda e, pr=pr, ysl=ysl: e.dma_start(
            out=yt_o[128 * pr: 128 * (pr + 1), :], in_=ysl),
            waits=[t_norm_last[pr]])
    p.op("sp", lambda e: None,
         waits=p.qlast("sp") + p.qlast("act") + p.qlast("pop"), inc=0)

    p.emit()
    esP.close()
    esB.close()
    es.close()
    return nc


def build_l2():
    nc = bass.Bass("TRN2", target_bir_lowering=False)
    xtp_i = nc.dram_tensor("xtp", [8 * 256, CDIM], F32, kind="ExternalInput").ap()
    wex_i = nc.dram_tensor("wex", [8 * CDIM, CDIM], F32, kind="ExternalInput").ap()
    bex_i = nc.dram_tensor("bex", [1, 8 * CDIM], F32, kind="ExternalInput").ap()
    cst_i = nc.dram_tensor("consts", [128, NCONST], F32, kind="ExternalInput").ap()
    ytl_o = nc.dram_tensor("ytl", [256, CDIM], BF16, kind="ExternalOutput").ap()
    xtn_sp = nc.dram_tensor("xtn_sp", [256, CDIM], BF16).ap()

    p = Prog(nc)
    es = contextlib.ExitStack()
    cst = _alloc(es, nc, "cst", [128, NCONST], F32)
    ones_r = _alloc(es, nc, "ones_r", [1, 128], BF16)
    xts = _alloc(es, nc, "xts", [128, 16 * CDIM], F32)
    acc = _alloc(es, nc, "acc", [128, 2 * CDIM], F32)
    sqs = _alloc(es, nc, "sqs", [128, CDIM], F32)
    stt = _alloc(es, nc, "stt", [128, 2 * 12], F32)
    xtn = _alloc(es, nc, "xtn", [128, 2 * CDIM], BF16)
    xtnT = _alloc(es, nc, "xtnT", [128, 8 * 256], BF16)
    wxf = _alloc(es, nc, "wxf", [128, 2 * 8 * CDIM], F32)
    wxb = _alloc(es, nc, "wxb", [128, 2 * 8 * CDIM], BF16)
    bexb = _alloc(es, nc, "bexb", [1, 8 * CDIM], BF16)
    ystg = _alloc(es, nc, "ystg", [32, 2 * CDIM], BF16)
    esP = contextlib.ExitStack()
    PY = _psum(esP, nc, "PY", [128, 1024])

    def CST(j):
        return cst[:, j:j + 1]

    t_cst = p.dma("sp", lambda e: e.dma_start(out=cst[:, :], in_=cst_i))
    t_ones = p.op("dve", lambda e: e.memset(ones_r[:, :], 1.0))
    t_bex = p.dma("sp", lambda e: e.dma_start(out=wxf[0:1, 0:8 * CDIM], in_=bex_i))
    t_bexb = p.op("dve", lambda e: e.tensor_copy(bexb[:, :], wxf[0:1, 0:8 * CDIM]),
                  waits=[t_bex])

    t_ld = []
    for r in range(8):
        t_ld.append(p.dma("sp", lambda e, r=r: e.dma_start(
            out=xts[:, 2 * CDIM * r: 2 * CDIM * (r + 1)].rearrange(
                "p (c m) -> p c m", c=2),
            in_=xtp_i[256 * r: 256 * (r + 1), :].rearrange("(c p) m -> p c m", p=128))))

    tok_xtn = []
    for i in range(2):
        dst = acc[:, CDIM * i: CDIM * (i + 1)]
        tk = p.op("dve", lambda e, dst=dst, i=i: e.tensor_copy(
            dst, xts[:, CDIM * i: CDIM * (i + 1)]), waits=[t_ld[0]])
        for r in range(1, 8):
            src = xts[:, 2 * CDIM * r + CDIM * i: 2 * CDIM * r + CDIM * (i + 1)]
            tk = p.op("dve", lambda e, dst=dst, src=src: e.tensor_add(dst, dst, src),
                      waits=[t_ld[r], tk])
        tk = p.op("dve", lambda e, dst=dst: e.tensor_scalar(
            dst, dst, CST(C_CLIPHI), CST(C_CLIPLO), ALU.min, ALU.max),
            waits=[tk, t_cst])
        xdst = xtn[:, CDIM * i: CDIM * (i + 1)]
        w_ln = [tk] + ([tok_xtn[0]] if i == 1 else [])
        tok_xtn.append(_ln_chain(p, CST, dst, stt[:, 12 * i: 12 * (i + 1)],
                                 sqs[:, :], xdst, AF.Tanh, w_ln))

    t_sp = []
    for i in range(2):
        t_sp.append(p.dma("sp", lambda e, i=i: e.dma_start(
            out=xtn_sp[128 * i: 128 * (i + 1), :],
            in_=xtn[:, CDIM * i: CDIM * (i + 1)]), waits=[tok_xtn[i]]))
    t_T = []
    for dc in range(8):
        t_T.append(p.dma("act", lambda e, dc=dc: e.dma_start(
            out=xtnT[:, 256 * dc: 256 * (dc + 1)],
            in_=xtn_sp[:, 128 * dc: 128 * (dc + 1)], transpose=True),
            waits=[t_sp[0], t_sp[1]]))

    tok_wx = {}

    def wx_load(el, extra=()):
        w = wxf[:, 8 * CDIM * (el % 2): 8 * CDIM * (el % 2 + 1)]
        tl = p.dma("pop", lambda e, el=el, w=w: e.dma_start(
            out=w.rearrange("p (c m) -> p c m", c=8),
            in_=wex_i[CDIM * el: CDIM * (el + 1), :].rearrange(
                "(c p) m -> p c m", p=128)), waits=list(extra))
        tcv = p.op("dve", lambda e, el=el, w=w: e.tensor_copy(
            wxb[:, 8 * CDIM * (el % 2): 8 * CDIM * (el % 2 + 1)], w),
            waits=[tl] + list(extra))
        tok_wx[el] = (tl, tcv)

    wx_load(0, extra=[t_bexb])
    wx_load(1)
    tok_ev = {}
    tok_st = {}
    prev_mm = None
    for el in range(8):
        def mk_ex(e, el=el):
            last = None
            for hh in range(2):
                for dc in range(8):
                    last = e.matmul(
                        PY[0:32, 512 * hh: 512 * (hh + 1)],
                        xtnT[:, 256 * dc + 32 * el: 256 * dc + 32 * (el + 1)],
                        wxb[:, 8 * CDIM * (el % 2) + CDIM * dc + 512 * hh:
                            8 * CDIM * (el % 2) + CDIM * dc + 512 * (hh + 1)],
                        start=(dc == 0), stop=False)
                last = e.matmul(PY[0:32, 512 * hh: 512 * (hh + 1)],
                                ones_r[0:1, 0:32],
                                bexb[0:1, CDIM * el + 512 * hh: CDIM * el + 512 * (hh + 1)],
                                start=False, stop=True)
            return last
        w_ex = t_T + [tok_wx[el][1], t_bexb, t_ones,
                tok_ev.get(el - 1)]
        t_mm = p.op("pe", mk_ex, waits=w_ex)
        prev_mm = t_mm
        dst = ystg[:, CDIM * (el % 2): CDIM * (el % 2 + 1)]
        w_ev = [t_mm] + ([tok_st[el - 2]] if el >= 2 else [])
        tok_ev[el] = p.op("dve", lambda e, dst=dst: e.tensor_copy(dst, PY[0:32, :]),
                          waits=w_ev)
        tok_st[el] = p.dma("sp", lambda e, el=el, dst=dst: e.dma_start(
            out=ytl_o[32 * el: 32 * (el + 1), :], in_=dst), waits=[tok_ev[el]])
        if el + 2 < 8:
            wx_load(el + 2, extra=[tok_wx[el][1], t_mm])

    p.op("sp", lambda e: None,
         waits=p.qlast("sp") + p.qlast("act") + p.qlast("pop"), inc=0)
    p.emit()
    esP.close()
    es.close()
    return nc


def build_l3(T=4096):
    TC = T // 128
    NB = T // 512
    nc = bass.Bass("TRN2", target_bir_lowering=False)
    x_i = nc.dram_tensor("x", [T, CDIM], F32, kind="ExternalInput").ap()
    yt_i = nc.dram_tensor("yt", [CDIM, T], BF16, kind="ExternalInput").ap()
    csp_i = nc.dram_tensor("c_sp", [T, EDIM], BF16, kind="ExternalInput").ap()
    wo_i = nc.dram_tensor("wo", [CDIM, CDIM], F32, kind="ExternalInput").ap()
    bor_i = nc.dram_tensor("bor", [1, CDIM], F32, kind="ExternalInput").ap()
    ytl_i = nc.dram_tensor("ytl", [EDIM, CDIM], BF16, kind="ExternalInput").ap()
    out_o = nc.dram_tensor("out", [T, CDIM], F32, kind="ExternalOutput").ap()

    p = Prog(nc)
    es = contextlib.ExitStack()
    ones_r = _alloc(es, nc, "ones_r", [1, 128], BF16)
    wof = _alloc(es, nc, "wof", [128, 8 * CDIM], F32)
    wob = _alloc(es, nc, "wob", [128, 8 * CDIM], BF16)
    bof = _alloc(es, nc, "bof", [1, CDIM], F32)
    bob = _alloc(es, nc, "bob", [1, CDIM], BF16)
    ytl = _alloc(es, nc, "ytl_sb", [128, 16 * CDIM], BF16)
    ytsb = _alloc(es, nc, "ytsb", [128, 8 * T], BF16)
    ctb = _alloc(es, nc, "ctb", [128, 2 * 16 * 512], BF16)
    xb = _alloc(es, nc, "xb", [128, 2 * CDIM], F32)
    ob = _alloc(es, nc, "ob", [128, 2 * CDIM], F32)
    esP = contextlib.ExitStack()
    PO = _psum(esP, nc, "PO", [128, 2048])

    t_ones = p.op("dve", lambda e: e.memset(ones_r[:, :], 1.0))
    t_wo = p.dma("sp", lambda e: e.dma_start(
        out=wof[:, :].rearrange("p (c m) -> p c m", c=8),
        in_=wo_i.rearrange("(c p) m -> p c m", p=128)))
    t_wob = p.op("dve", lambda e: e.tensor_copy(wob[:, :], wof[:, :]), waits=[t_wo])
    t_bo = p.dma("sp", lambda e: e.dma_start(out=bof[:, :], in_=bor_i))
    t_bob = p.op("dve", lambda e: e.tensor_copy(bob[:, :], bof[:, :]), waits=[t_bo])
    t_ytl = p.dma("sp", lambda e: e.dma_start(
        out=ytl[:, :].rearrange("p (c m) -> p c m", c=16),
        in_=ytl_i.rearrange("(c p) m -> p c m", p=128)))
    t_yt = []
    for c in range(8):
        t_yt.append(p.dma("sp", lambda e, c=c: e.dma_start(
            out=ytsb[:, T * c: T * (c + 1)], in_=yt_i[128 * c: 128 * (c + 1), :])))

    tok_ct = [None] * NB
    tok_blk = [None] * NB
    for b in range(NB):
        sl = b % 2
        tks = []
        for ec in range(16):
            w_ct = [tok_blk[b - 2]] if b >= 2 else []
            tks.append(p.dma("act", lambda e, b=b, sl=sl, ec=ec: e.dma_start(
                out=ctb[:, 8192 * sl + 512 * ec: 8192 * sl + 512 * (ec + 1)],
                in_=csp_i[512 * b: 512 * (b + 1), 128 * ec: 128 * (ec + 1)],
                transpose=True), waits=w_ct))
        tok_ct[b] = tks

        for tt in range(4 * b, min(4 * b + 4, TC)):
            xsl = xb[:, CDIM * (tt % 2): CDIM * (tt % 2 + 1)]
            osl = ob[:, CDIM * (tt % 2): CDIM * (tt % 2 + 1)]
            w_x = [tok_ev[(tt - 2, 0)], tok_ev[(tt - 2, 1)]] if tt >= 2 else []
            t_x = p.dma("sp", lambda e, xsl=xsl, tt=tt: e.dma_start(
                out=xsl, in_=x_i[128 * tt: 128 * (tt + 1), :]), waits=w_x)
            if tt == 0:
                tok_ev = {}
                tok_stt = {}
            t_mm_last = None
            for hh in range(2):
                pb = PO[:, 1024 * (tt % 2) + 512 * hh:
                        1024 * (tt % 2) + 512 * (hh + 1)]

                def mk_o(e, tt=tt, hh=hh, pb=pb, sl=sl):
                    last = None
                    for c in range(8):
                        last = e.matmul(
                            pb, ytsb[:, T * c + 128 * tt: T * c + 128 * (tt + 1)],
                            wob[:, CDIM * c + 512 * hh: CDIM * c + 512 * (hh + 1)],
                            start=(c == 0), stop=False)
                    for ec in range(16):
                        last = e.matmul(
                            pb,
                            ctb[:, 8192 * sl + 512 * ec + 128 * (tt % 4):
                                8192 * sl + 512 * ec + 128 * (tt % 4) + 128],
                            ytl[:, CDIM * ec + 512 * hh: CDIM * ec + 512 * (hh + 1)],
                            start=False, stop=False)
                    return e.matmul(pb, ones_r[:, :], bob[:, 512 * hh: 512 * (hh + 1)],
                                    start=False, stop=True)
                w_o = [t_wob, t_bob, t_ytl, t_ones] + t_yt + tok_ct[b]
                if tt >= 2:
                    w_o.append(tok_ev[(tt - 2, hh)])
                t_mm = p.op("pe", mk_o, waits=w_o)
                t_mm_last = t_mm
                w_ev = [t_mm, t_x] + ([tok_stt[tt - 2]] if tt >= 2 else [])
                tok_ev[(tt, hh)] = p.op(
                    "dve", lambda e, osl=osl, pb=pb, hh=hh, xsl=xsl: e.tensor_add(
                        osl[:, 512 * hh: 512 * (hh + 1)], pb,
                        xsl[:, 512 * hh: 512 * (hh + 1)]), waits=w_ev)
            tok_stt[tt] = p.dma("sp", lambda e, osl=osl, tt=tt: e.dma_start(
                out=out_o[128 * tt: 128 * (tt + 1), :], in_=osl),
                waits=[tok_ev[(tt, 0)], tok_ev[(tt, 1)]])
            if tt % 4 == 3:
                tok_blk[tt // 4] = t_mm_last

    p.op("sp", lambda e: None, waits=p.qlast("sp") + p.qlast("act"), inc=0)
    p.emit()
    esP.close()
    es.close()
    return nc


_NC_CACHE = {}


def _get(builder, key, *args):
    if key not in _NC_CACHE:
        _NC_CACHE[key] = builder(*args)
    return _NC_CACHE[key]


def kernel(L0_attn_x, L0_attn_Wq, L0_attn_bq, L0_attn_Wk, L0_attn_bk,
           L0_attn_Wv, L0_attn_bv, L0_attn_Wo, L0_attn_bo,
           L0_ffn_phi_w, L0_ffn_phi_b, L0_ffn_expert_w, L0_ffn_expert_b):
    T = L0_attn_x.shape[1]
    f32 = np.float32
    cc = np.ascontiguousarray
    consts = np.tile(np.asarray(CONST_VALS, f32), (128, 1))
    tri = np.asarray((np.arange(128)[:, None] <= np.arange(128)[None, :]),
                     dtype=ml_dtypes.bfloat16)

    x = np.asarray(L0_attn_x, f32)
    phiw = cc(np.asarray(L0_ffn_phi_w, f32))
    phib = cc(np.asarray(L0_ffn_phi_b, f32).reshape(1, EDIM))
    wq = cc(np.asarray(L0_attn_Wq, f32))
    wk = cc(np.asarray(L0_attn_Wk, f32))
    wv = cc(np.asarray(L0_attn_Wv, f32))
    wo = cc(np.asarray(L0_attn_Wo, f32))
    bqc = cc(np.asarray(L0_attn_bq, f32).reshape(CDIM, 1))
    bkc = cc(np.asarray(L0_attn_bk, f32).reshape(CDIM, 1))
    bvr = cc(np.asarray(L0_attn_bv, f32).reshape(1, CDIM))
    bor = cc(np.asarray(L0_attn_bo, f32).reshape(1, CDIM))
    wex = np.asarray(L0_ffn_expert_w, f32)
    bex = np.asarray(L0_ffn_expert_b, f32)

    nc1 = _get(build_l1, ("l1", T), T)
    in1 = [dict(x=cc(x[b]), phiw=phiw, phib=phib, wq=wq, wk=wk, wv=wv,
                bqc=bqc, bkc=bkc, bvr=bvr, tri=tri, consts=consts)
           for b in range(NCORES)]
    r1 = run_bass_kernel_spmd(nc1, in1, core_ids=list(range(NCORES))).results

    nc2 = _get(build_l2, ("l2",))
    in2 = []
    for r in range(NCORES):
        xtp = cc(np.stack([np.asarray(r1[b]["xt_part"])[256 * r: 256 * (r + 1), :]
                           for b in range(NCORES)]).reshape(8 * 256, CDIM))
        in2.append(dict(xtp=xtp,
                        wex=cc(wex[8 * r: 8 * (r + 1)].reshape(8 * CDIM, CDIM)),
                        bex=cc(bex[8 * r: 8 * (r + 1)].reshape(1, 8 * CDIM)),
                        consts=consts))
    r2 = run_bass_kernel_spmd(nc2, in2, core_ids=list(range(NCORES))).results

    ytl_full = cc(np.concatenate([np.asarray(r2[r]["ytl"]) for r in range(NCORES)],
                                 axis=0))
    nc3 = _get(build_l3, ("l3", T), T)
    in3 = [dict(x=cc(x[b]), yt=cc(np.asarray(r1[b]["yt_sp"])),
                c_sp=cc(np.asarray(r1[b]["c_sp"])), wo=wo, bor=bor, ytl=ytl_full)
           for b in range(NCORES)]
    r3 = run_bass_kernel_spmd(nc3, in3, core_ids=list(range(NCORES))).results

    return np.stack([np.asarray(r3[b]["out"], f32) for b in range(NCORES)])


# version 21 (final)
# speedup vs baseline: 1.2551x; 1.1530x over previous
import contextlib

import numpy as np
import ml_dtypes
import concourse.bass as bass
import concourse.mybir as mybir
from concourse.bass_utils import run_bass_kernel_spmd

F32 = mybir.dt.float32
BF16 = mybir.dt.bfloat16
AF = mybir.ActivationFunctionType
ALU = mybir.AluOpType
AX = mybir.AxisListType

NCORES = 8
CDIM = 1024
EDIM = 2048
NEXP, PSLOT = 64, 32
NPAIR = 8
EPS = 1e-5

C_INVC, C_EPS, C_NHALF, C_FFNB, C_ATTNS, C_ATTNB = 0, 1, 2, 3, 4, 5
C_CLIPLO, C_CLIPHI, C_ONE, C_ZERO, C_NEG1 = 6, 7, 8, 9, 10
NCONST = 11
CONST_VALS = [1.0 / CDIM, EPS, -0.5, -2.0, 0.125, -4.0,
              -33000.0, 65000.0, 1.0, 0.0, -1.0]


class Prog:

    ENGS = ("pe", "act", "dve", "pop", "sp")
    QDEPTH = {"sp": 8, "act": 6, "pop": 4}

    def __init__(self, nc):
        self.nc = nc
        self.ops = {e: [] for e in self.ENGS}
        self.cnt = {s: 0 for s in self.ENGS}
        for e, d in self.QDEPTH.items():
            for j in range(d):
                self.cnt[f"q_{e}{j}"] = 0
        self.qn = {e: 0 for e in self.QDEPTH}

    def op(self, eng, builder, waits=(), inc=1, sem=None):
        sem = sem or eng
        self.cnt[sem] += inc
        tok = (sem, self.cnt[sem])
        self.ops[eng].append((builder, inc, sem, [w for w in waits if w]))
        return tok

    def dma(self, eng, builder, waits=()):
        j = self.qn[eng] % self.QDEPTH[eng]
        self.qn[eng] += 1
        sem = f"q_{eng}{j}"
        auto = [(sem, self.cnt[sem])] if self.cnt[sem] else []
        return self.op(eng, builder, waits=list(waits) + auto, inc=16, sem=sem)

    def last(self, sem):
        return (sem, self.cnt[sem])

    def qlast(self, eng):
        return [(f"q_{eng}{j}", self.cnt[f"q_{eng}{j}"])
                for j in range(self.QDEPTH[eng])
                if self.cnt[f"q_{eng}{j}"]]

    def emit(self):
        nc = self.nc
        es = contextlib.ExitStack()
        sems = {s: es.enter_context(nc.semaphore("sem_" + s))
                for s in self.cnt}
        with nc.Block() as blk:
            sect = {"pe": blk.tensor, "act": blk.scalar, "dve": blk.vector,
                    "pop": blk.gpsimd, "sp": blk.sync}

            def mk(ename):
                ops = self.ops[ename]

                def run(eng):
                    waited = {}
                    for builder, inc, sem, waits in ops:
                        for (s, v) in waits:
                            if waited.get(s, 0) < v:
                                eng.wait_ge(sems[s], v)
                                waited[s] = v
                        inst = builder(eng)
                        if inst is not None and inc > 0:
                            inst.then_inc(sems[sem], inc)
                return run

            for e in self.ENGS:
                sect[e](mk(e))
        es.close()


def _alloc(es, nc, name, shape, dt):
    return es.enter_context(nc.sbuf_tensor(name, shape, dt))


def _psum(es, nc, name, shape, dt=F32):
    return es.enter_context(nc.psum_tensor(name, shape, dt))


def _ln_chain(p, CST, src, st, sqs, dst_act, dst_fn, waits):
    s1, sq, m, msq, t1, var, lnv, rstd, nmr = (st[:, j:j + 1] for j in range(9))
    ta = p.op("dve", lambda e: e.reduce_sum(s1, src, AX.X), waits=waits)
    tb = p.op("act", lambda e: e.activation(sqs, src, AF.Square, accum_out=sq),
              waits=waits)
    tm = p.op("dve", lambda e: e.tensor_scalar_mul(m, s1, CST(C_INVC)), waits=[ta])
    tmsq = p.op("dve", lambda e: e.tensor_scalar_mul(msq, sq, CST(C_INVC)), waits=[tb])
    tt1 = p.op("dve", lambda e: e.tensor_mul(t1, m, m), waits=[tm])
    tvar = p.op("dve", lambda e: e.tensor_sub(var, msq, t1), waits=[tmsq, tt1])
    tlnv = p.op("act", lambda e: e.activation(lnv, var, AF.Ln, bias=CST(C_EPS)),
                waits=[tvar])
    trstd = p.op("act", lambda e: e.activation(rstd, lnv, AF.Exp, scale=CST(C_NHALF)),
                 waits=[tlnv])
    tnmr = p.op("dve", lambda e: e.tensor_scalar(nmr, m, rstd, CST(C_NEG1),
                                                 ALU.mult, ALU.mult), waits=[trstd])
    return p.op("act", lambda e: e.activation(dst_act, src, dst_fn,
                                              scale=rstd, bias=nmr),
                waits=[tnmr] + list(waits))


def build_l1(T=4096):
    TC = T // 128
    NQT = T // 512
    spc = min(8, TC)
    XN_RING = spc + 6
    LAG = 6

    nc = bass.Bass("TRN2", target_bir_lowering=False)
    x_i = nc.dram_tensor("x", [T, CDIM], F32, kind="ExternalInput").ap()
    phiw_i = nc.dram_tensor("phiw", [EDIM, CDIM], F32, kind="ExternalInput").ap()
    phib_i = nc.dram_tensor("phib", [1, EDIM], F32, kind="ExternalInput").ap()
    wq_i = nc.dram_tensor("wq", [CDIM, CDIM], F32, kind="ExternalInput").ap()
    wk_i = nc.dram_tensor("wk", [CDIM, CDIM], F32, kind="ExternalInput").ap()
    wv_i = nc.dram_tensor("wv", [CDIM, CDIM], F32, kind="ExternalInput").ap()
    bqc_i = nc.dram_tensor("bqc", [CDIM, 1], F32, kind="ExternalInput").ap()
    bkc_i = nc.dram_tensor("bkc", [CDIM, 1], F32, kind="ExternalInput").ap()
    bvr_i = nc.dram_tensor("bvr", [1, CDIM], F32, kind="ExternalInput").ap()
    tri_i = nc.dram_tensor("tri", [128, 128], BF16, kind="ExternalInput").ap()
    cst_i = nc.dram_tensor("consts", [128, NCONST], F32, kind="ExternalInput").ap()
    xt_o = nc.dram_tensor("xt_part", [EDIM, CDIM], F32, kind="ExternalOutput").ap()
    csp_o = nc.dram_tensor("c_sp", [T, EDIM], BF16, kind="ExternalOutput").ap()
    yt_o = nc.dram_tensor("yt_sp", [CDIM, T], BF16, kind="ExternalOutput").ap()
    xnsp = nc.dram_tensor("xn_sp", [T, CDIM], BF16).ap()
    phibf = nc.dram_tensor("phi_bf", [EDIM, CDIM], BF16).ap()

    p = Prog(nc)
    es = contextlib.ExitStack()
    cst = _alloc(es, nc, "cst", [128, NCONST], F32)
    ones_r = _alloc(es, nc, "ones_r", [1, 128], BF16)
    tri = _alloc(es, nc, "tri_sb", [128, 128], BF16)

    def CST(j):
        return cst[:, j:j + 1]

    esA = contextlib.ExitStack()
    phiT = _alloc(esA, nc, "phiT", [128, 8 * EDIM], BF16)
    phibb = _alloc(esA, nc, "phibb", [1, EDIM], BF16)
    esStg = contextlib.ExitStack()
    phibf32 = _alloc(esStg, nc, "phibf32", [1, EDIM], F32)
    pstg = _alloc(esStg, nc, "pstg", [128, 4 * CDIM], F32)
    pstgb = _alloc(esStg, nc, "pstgb", [128, 4 * CDIM], BF16)
    esStg.close()
    xb = _alloc(esA, nc, "xb", [128, 3 * CDIM], F32)
    stt = _alloc(esA, nc, "stt", [128, 3 * 12], F32)
    xnb = _alloc(esA, nc, "xnb", [128, XN_RING * CDIM], BF16)
    xnTs = _alloc(esA, nc, "xnTs", [128, 2 * 512 * 8], BF16)
    Et = _alloc(esA, nc, "Et", [128, 2 * EDIM], BF16)
    dsum = _alloc(esA, nc, "dsum", [128, 2 * 64], F32)
    csum = _alloc(esA, nc, "csum", [128, 2 * 32], F32)
    drec = _alloc(esA, nc, "drec", [128, 2 * 64], F32)
    crec = _alloc(esA, nc, "crec", [128, 2 * 32], F32)
    Dbuf = _alloc(esA, nc, "Dbuf", [128, spc * EDIM], BF16)
    Cbuf = _alloc(esA, nc, "Cbuf", [128, 2 * EDIM], BF16)
    xtacc = _alloc(esA, nc, "xtacc", [128, 16 * CDIM], F32)

    esP = contextlib.ExitStack()
    P1 = _psum(esP, nc, "P1", [128, 2048])
    P2a = _psum(esP, nc, "P2a", [128, 512])
    P2b = _psum(esP, nc, "P2b", [128, 512])
    P4 = _psum(esP, nc, "P4", [128, 1024])

    t_cst = p.dma("sp", lambda e: e.dma_start(out=cst[:, :], in_=cst_i))
    t_tri = p.dma("sp", lambda e: e.dma_start(out=tri[:, :], in_=tri_i))
    t_ones = p.op("dve", lambda e: e.memset(ones_r[:, :], 1.0))
    t_phibld = p.dma("sp", lambda e: e.dma_start(out=phibf32[:, :], in_=phib_i))
    t_phibb = p.op("dve", lambda e: e.tensor_copy(phibb[:, :], phibf32[:, :]),
                   waits=[t_phibld])

    t_sp_prev = None
    t_cv_prev = None
    for r in range(4):
        rows = phiw_i[512 * r: 512 * (r + 1), :]
        tl = p.dma("pop", lambda e, rows=rows: e.dma_start(
            out=pstg[:, :].rearrange("p (c m) -> p c m", c=4),
            in_=rows.rearrange("(c p) m -> p c m", p=128)),
            waits=[t_cv_prev] if t_cv_prev else [])
        tc = p.op("dve", lambda e: e.tensor_copy(pstgb[:, :], pstg[:, :]),
                  waits=[tl, t_sp_prev] if t_sp_prev else [tl])
        t_sp_prev = p.dma("pop", lambda e, r=r: e.dma_start(
            out=phibf[512 * r: 512 * (r + 1), :].rearrange("(c p) m -> p c m", p=128),
            in_=pstgb[:, :].rearrange("p (c m) -> p c m", c=4)),
            waits=[tc])
        t_cv_prev = tc
    t_phiT = []
    for c in range(8):
        t_phiT.append(p.dma("sp", lambda e, c=c: e.dma_start(
            out=phiT[:, EDIM * c: EDIM * (c + 1)],
            in_=phibf[:, 128 * c: 128 * (c + 1)], transpose=True),
            waits=[t_sp_prev]))
    t_stg_free = [t_cv_prev, t_sp_prev, t_phibb]

    tok_x = [None] * TC
    tok_xn = [None] * TC
    tok_xnsp = [None] * TC
    tok_sup = [None] * ((TC + 3) // 4)
    tok_E = [None] * TC
    tok_D = [None] * TC
    tok_C = [None] * TC
    tok_csp = [None] * TC
    tok_logits = [None] * TC
    tok_xtmm = {}
    tok_xtadd = {}
    xt_bank_last = {0: None, 1: None, 2: None, 3: None}
    n_grp = (TC + spc - 1) // spc


    def x_load(t):
        sl = t % 3
        xv = xb[:, CDIM * sl: CDIM * (sl + 1)]
        w = [t_cst] + ([tok_xn[t - 3]] if t >= 3 else [])
        if t == 0:
            w += t_stg_free
        tok_x[t] = p.dma("sp", lambda e, xv=xv, t=t: e.dma_start(
            out=xv, in_=x_i[128 * t: 128 * (t + 1), :]), waits=w)

    def ln_part(t):
        sl = t % 3
        xv = xb[:, CDIM * sl: CDIM * (sl + 1)]
        st = stt[:, 12 * sl: 12 * (sl + 1)]
        xnv = xnb[:, CDIM * (t % XN_RING): CDIM * (t % XN_RING + 1)]
        w_xn = [tok_x[t]] + ([tok_xn[t - 3]] if t >= 3 else [])
        if t >= XN_RING:
            gprev = (t - XN_RING) // spc
            while (gprev, 15, 1) not in tok_xtmm:
                drain_xt(1)
            w_xn.append(tok_xtmm[(gprev, 15, 1)])
        tok_xn[t] = _ln_chain(p, CST, xv, st, xnv, xnv, AF.Identity, w_xn)
        tok_xnsp[t] = p.dma("sp", lambda e: e.dma_start(
            out=xnsp[128 * t: 128 * (t + 1), :], in_=xnv), waits=[tok_xn[t]])
        if t % 4 == 3:
            s = t // 4
            sv = xnTs[:, 4096 * (s % 2): 4096 * (s % 2 + 1)]
            w_sup = [tok_xnsp[4 * (t // 4) + j] for j in range(4)]
            if s >= 2:
                w_sup.append(tok_logits[4 * (s - 2) + 3])
            tks = []
            for c in range(8):
                tks.append(p.dma("sp", lambda e, sv=sv, s=s, c=c: e.dma_start(
                    out=sv[:, 512 * c: 512 * (c + 1)],
                    in_=xnsp[512 * s: 512 * (s + 1), 128 * c: 128 * (c + 1)],
                    transpose=True), waits=w_sup))
            tok_sup[s] = tks

    def ffn_pe_act(t):
        s = t // 4
        sv = xnTs[:, 4096 * (s % 2): 4096 * (s % 2 + 1)]
        esl = Et[:, EDIM * (t % 2): EDIM * (t % 2 + 1)]
        t_eh = []
        for h in range(2):
            def mk_lg(e, sv=sv, t=t, h=h):
                last = None
                for c in range(8):
                    lh = sv[:, 512 * c + 128 * (t % 4): 512 * c + 128 * (t % 4) + 128]
                    for g in (2 * h, 2 * h + 1):
                        last = e.matmul(
                            P1[:, 512 * g: 512 * (g + 1)], lh,
                            phiT[:, EDIM * c + 512 * g: EDIM * c + 512 * (g + 1)],
                            start=(c == 0), stop=False)
                for g in (2 * h, 2 * h + 1):
                    last = e.matmul(P1[:, 512 * g: 512 * (g + 1)], ones_r[:, :],
                                    phibb[:, 512 * g: 512 * (g + 1)],
                                    start=False, stop=True)
                return last
            w_lg = tok_sup[s] + t_phiT + [t_phibb, t_ones]
            if t >= 1:
                w_lg.append(ffn_pe_act.exp_h[h])
            t_lg = p.op("pe", mk_lg, waits=w_lg)
            w_E = [t_lg] + ([tok_C[t - 2], tok_D[t - 2]] if t >= 2 else [])
            te = p.op("act", lambda e, esl=esl, h=h: e.activation(
                esl[:, 1024 * h: 1024 * (h + 1)], P1[:, 1024 * h: 1024 * (h + 1)],
                AF.Exp, bias=CST(C_FFNB)), waits=w_E)
            t_eh.append(te)
            ffn_pe_act.exp_h[h] = te
        tok_logits[t] = t_lg
        tok_E[t] = t_eh[1]

        ffn_state[t] = (esl, t_eh)

    def ffn_dve(t):
        esl, t_eh = ffn_state.pop(t)
        dsl = dsum[:, 64 * (t % 2): 64 * (t % 2 + 1)]
        csl = csum[:, 32 * (t % 2): 32 * (t % 2 + 1)]
        drl = drec[:, 64 * (t % 2): 64 * (t % 2 + 1)]
        crl = crec[:, 32 * (t % 2): 32 * (t % 2 + 1)]
        w_slot = [tok_D[t - 2], tok_C[t - 2]] if t >= 2 else []
        td1 = p.op("dve", lambda e, dsl=dsl, esl=esl: e.reduce_sum(
            dsl, esl.rearrange("p (n w) -> p n w", w=PSLOT), AX.X),
            waits=[t_eh[0], t_eh[1]] + w_slot)
        tc1 = p.op("dve", lambda e, csl=csl, esl=esl: e.reduce_sum(
            csl, esl.rearrange("p (n w) -> p w n", w=PSLOT), AX.X),
            waits=[t_eh[0], t_eh[1]] + w_slot)
        td2 = p.op("dve", lambda e, drl=drl, dsl=dsl: e.reciprocal(drl, dsl),
                   waits=[td1])
        tc2 = p.op("dve", lambda e, crl=crl, csl=csl: e.reciprocal(crl, csl),
                   waits=[tc1])
        dv = Dbuf[:, EDIM * (t % spc): EDIM * (t % spc + 1)]
        w_D = [td2]
        if t >= spc:
            gp = (t - spc) // spc
            while (gp, 15, 1) not in tok_xtmm:
                drain_xt(1)
            w_D.append(tok_xtmm[(gp, 15, 1)])
        tok_D[t] = p.op("dve", lambda e, dv=dv, esl=esl, drl=drl: e.tensor_tensor(
            dv.rearrange("p (n w) -> p n w", w=PSLOT),
            esl.rearrange("p (n w) -> p n w", w=PSLOT),
            drl.unsqueeze(2).broadcast_to([128, NEXP, PSLOT]), ALU.mult), waits=w_D)
        cv = Cbuf[:, EDIM * (t % 2): EDIM * (t % 2 + 1)]
        w_C = [tc2] + ([tok_csp[t - 2]] if t >= 2 else [])
        tok_C[t] = p.op("dve", lambda e, cv=cv, esl=esl, crl=crl: e.tensor_tensor(
            cv.rearrange("p (n w) -> p n w", w=PSLOT),
            esl.rearrange("p (n w) -> p n w", w=PSLOT),
            crl.unsqueeze(1).broadcast_to([128, NEXP, PSLOT]), ALU.mult), waits=w_C)
        tok_csp[t] = p.dma("sp", lambda e, cv=cv, t=t: e.dma_start(
            out=csp_o[128 * t: 128 * (t + 1), :], in_=cv), waits=[tok_C[t]])

        if (t + 1) % spc == 0 or t == TC - 1:
            g = t // spc
            g_lo = g * spc
            g_n = t - g_lo + 1
            for et in range(16):
                for dh in range(2):
                    xt_tasks.append((g, g_lo, g_n, et, dh, tok_D[t]))

    XT_BANKS = (P2a[:, 0:512], P2b[:, 0:512], P4[:, 0:512], P4[:, 512:1024])

    def drain_xt(k):
        while xt_tasks and k > 0:
            k -= 1
            g, g_lo, g_n, et, dh, t_d = xt_tasks.pop(0)
            bk = drain_xt.n % 4
            drain_xt.n += 1
            pt = XT_BANKS[bk]
            w_mm = [t_d, xt_bank_last[bk]]

            def mk_xt(e, g_lo=g_lo, g_n=g_n, et=et, dh=dh, pt=pt):
                last = None
                for j in range(g_n):
                    tch = g_lo + j
                    dvj = Dbuf[:, EDIM * (tch % spc) + 128 * et:
                               EDIM * (tch % spc) + 128 * (et + 1)]
                    xnj = xnb[:, CDIM * (tch % XN_RING) + 512 * dh:
                              CDIM * (tch % XN_RING) + 512 * (dh + 1)]
                    last = e.matmul(pt, dvj, xnj,
                                    start=(j == 0), stop=(j == g_n - 1))
                return last
            tok_xtmm[(g, et, dh)] = p.op("pe", mk_xt, waits=w_mm)
            dst = xtacc[:, CDIM * et + 512 * dh: CDIM * et + 512 * (dh + 1)]
            if g == 0:
                tk = p.op("dve", lambda e, dst=dst, pt=pt: e.tensor_copy(
                    dst, pt), waits=[tok_xtmm[(g, et, dh)]])
            else:
                tk = p.op("dve", lambda e, dst=dst, pt=pt: e.tensor_add(
                    dst, dst, pt),
                    waits=[tok_xtmm[(g, et, dh)], tok_xtadd[(g - 1, et, dh)]])
            tok_xtadd[(g, et, dh)] = tk
            xt_bank_last[bk] = tk

    drain_xt.n = 0
    xt_tasks = []
    ffn_pe_act.exp_h = {0: None, 1: None}
    ffn_state = {}
    ffn_pe_act.exp_h = {0: None, 1: None}
    ffn_state = {}
    for it in range(TC + LAG):
        if it == 0:
            x_load(0)
            x_load(1)
            x_load(2)
        elif it + 2 < TC:
            x_load(it + 2)
        if it >= LAG:
            ffn_pe_act(it - LAG)
            drain_xt(5)
        if it < TC:
            ln_part(it)
        if it >= LAG:
            ffn_dve(it - LAG)

    drain_xt(10 ** 9)
    for et in range(16):
        w = [tok_xtadd[(n_grp - 1, et, 0)], tok_xtadd[(n_grp - 1, et, 1)]]
        p.dma("sp", lambda e, et=et: e.dma_start(
            out=xt_o[128 * et: 128 * (et + 1), :],
            in_=xtacc[:, CDIM * et: CDIM * (et + 1)]), waits=w)

    bar = ([p.last(s) for s in ("pe", "act", "dve", "pop", "sp")]
           + p.qlast("sp") + p.qlast("act") + p.qlast("pop"))
    esA.close()

    esB = contextlib.ExitStack()
    xnT = _alloc(esB, nc, "xnT", [128, 8 * T], BF16)
    Vp = _alloc(esB, nc, "Vp", [128, TC * NPAIR * 130], BF16)
    esW = contextlib.ExitStack()
    wvf = _alloc(esW, nc, "wvf", [128, 2 * 4096], F32)
    wvb = _alloc(esW, nc, "wvb", [128, 8 * CDIM], BF16)
    bvf = _alloc(esW, nc, "bvf", [1, CDIM], F32)
    bvb = _alloc(esW, nc, "bvb", [1, CDIM], BF16)

    t_xnT = []
    for c in range(8):
        t_xnT.append(p.dma("sp", lambda e, c=c: e.dma_start(
            out=xnT[:, T * c: T * (c + 1)],
            in_=xnsp[:, 128 * c: 128 * (c + 1)], transpose=True), waits=bar))
    t_xnT_h0 = t_xnT
    t_wv = []
    for r in range(2):
        tl = p.dma("pop", lambda e, r=r: e.dma_start(
            out=wvf[:, 4096 * r: 4096 * (r + 1)].rearrange("p (c m) -> p c m", c=4),
            in_=wv_i[512 * r: 512 * (r + 1), :].rearrange("(c p) m -> p c m", p=128)),
            waits=bar)
        t_wv.append(p.op("dve", lambda e, r=r: e.tensor_copy(
            wvb[:, 4096 * r: 4096 * (r + 1)], wvf[:, 4096 * r: 4096 * (r + 1)]),
            waits=[tl] + bar))
    t_bv = p.dma("pop", lambda e: e.dma_start(out=bvf[:, :], in_=bvr_i), waits=bar)
    t_bvb = p.op("dve", lambda e: e.tensor_copy(bvb[:, :], bvf[:, :]),
                 waits=[t_bv] + bar)
    t_vones = p.op("dve", lambda e: e.memset(
        Vp[:, :].rearrange("p (a s) -> p a s", s=65)[:, :, 64:65], 1.0), waits=bar)

    prev_vp = None
    for tch in range(TC):
        for hf in range(2):
            pt = P4[:, 512 * hf: 512 * (hf + 1)]

            def mk_v(e, tch=tch, hf=hf, pt=pt):
                last = None
                for c in range(8):
                    last = e.matmul(
                        pt, xnT[:, T * c + 128 * tch: T * c + 128 * (tch + 1)],
                        wvb[:, CDIM * c + 512 * hf: CDIM * c + 512 * (hf + 1)],
                        start=(c == 0), stop=False)
                return e.matmul(pt, ones_r[:, :], bvb[:, 512 * hf: 512 * (hf + 1)],
                                start=False, stop=True)
            w_v = ((t_xnT_h0 if 128 * (tch + 1) <= T // 2 else t_xnT)
                   + [t_wv[-1], t_bvb, t_ones, prev_vp])
            tmm = p.op("pe", mk_v, waits=w_v)
            t_vp_mm_last = tmm
            base = 1040 * tch + 520 * hf
            prev_vp = p.op("dve", lambda e, base=base, pt=pt: e.tensor_copy(
                Vp[:, base: base + 520].rearrange(
                    "p (a b s) -> p a b s", a=4, b=2)[:, :, :, 0:64],
                pt.rearrange("p (a b s) -> p a b s", a=4, b=2)),
                waits=[tmm, t_vones])
    t_vp_done = prev_vp

    esW.close()
    qtb = _alloc(esB, nc, "qtb", [128, 2 * T], BF16)
    ktb = _alloc(esB, nc, "ktb", [128, 2 * T], BF16)
    wqf = _alloc(esB, nc, "wqf", [128, 2 * CDIM], F32)
    wkf = _alloc(esB, nc, "wkf", [128, 2 * CDIM], F32)
    wqb = _alloc(esB, nc, "wqb", [128, 2 * CDIM], BF16)
    wkb = _alloc(esB, nc, "wkb", [128, 2 * CDIM], BF16)
    bqt = _alloc(esB, nc, "bqt", [128, 2], F32)
    bkt = _alloc(esB, nc, "bkt", [128, 2], F32)
    NPSL = 4
    pTb = _alloc(esB, nc, "pTb", [128, NPSL * 1024], BF16)
    yscr = _alloc(esB, nc, "yscr", [65, 2 * 512], F32)
    rrow = _alloc(esB, nc, "rrow", [1, 2 * 512], BF16)
    ytb = _alloc(esB, nc, "ytb", [128, T], BF16)

    tok_w = {}
    tok_wcv = {}
    tok_yt_sp = {}
    t_norm_last = {}
    t_projmm_last = {}
    t_proj_of = {}
    slot_free = {0: None, 1: None}
    pT_free = {j: None for j in range(NPSL)}
    p4_free = {0: t_vp_done, 1: t_vp_done}

    def w_loads(pr):
        sl = pr % 2
        w_war = ([tok_wcv[pr - 2], t_proj_of[pr - 2]] if pr >= 2
                 else bar + [t_vp_mm_last])
        t1 = p.dma("pop", lambda e, pr=pr, sl=sl: e.dma_start(
            out=wqf[:, CDIM * sl: CDIM * (sl + 1)].rearrange("p (c m) -> p c m", c=8),
            in_=wq_i[:, 128 * pr: 128 * (pr + 1)].rearrange("(c p) m -> p c m", p=128)),
            waits=w_war)
        t2 = p.dma("pop", lambda e, pr=pr, sl=sl: e.dma_start(
            out=wkf[:, CDIM * sl: CDIM * (sl + 1)].rearrange("p (c m) -> p c m", c=8),
            in_=wk_i[:, 128 * pr: 128 * (pr + 1)].rearrange("(c p) m -> p c m", p=128)),
            waits=w_war)
        t3 = p.dma("pop", lambda e, pr=pr, sl=sl: e.dma_start(
            out=bqt[:, sl:sl + 1], in_=bqc_i[128 * pr: 128 * (pr + 1), :]),
            waits=w_war)
        t4 = p.dma("pop", lambda e, pr=pr, sl=sl: e.dma_start(
            out=bkt[:, sl:sl + 1], in_=bkc_i[128 * pr: 128 * (pr + 1), :]),
            waits=w_war)
        tok_w[pr] = [t1, t2, t3, t4]

    def w_convert(pr):
        sl = pr % 2
        w_war = [t_projmm_last[pr - 2]] if pr >= 2 else []
        ta = p.op("dve", lambda e, sl=sl: e.tensor_copy(
            wqb[:, CDIM * sl: CDIM * (sl + 1)], wqf[:, CDIM * sl: CDIM * (sl + 1)]),
            waits=[tok_w[pr][0]] + w_war)
        tb = p.op("dve", lambda e, sl=sl: e.tensor_copy(
            wkb[:, CDIM * sl: CDIM * (sl + 1)], wkf[:, CDIM * sl: CDIM * (sl + 1)]),
            waits=[tok_w[pr][1]] + w_war)
        tok_wcv[pr] = tb
        return [ta, tb]

    def make_proj_pieces(pr, wcv):
        pieces = []
        sl = pr % 2
        for gi, (which, n) in enumerate(
                [(w, n) for w in range(2) for n in range(NQT)]):
            wb, bcol, dst = ((wqb, bqt, qtb), (wkb, bkt, ktb))[which]
            pp = gi % 2

            def emit(wb=wb, bcol=bcol, dst=dst, sl=sl, n=n, pp=pp,
                     which=which, pr=pr, wcv=wcv):
                def mk_p(e):
                    last = None
                    for c in range(8):
                        last = e.matmul(
                            P4[:, 512 * pp: 512 * (pp + 1)],
                            wb[:, CDIM * sl + 128 * c: CDIM * sl + 128 * (c + 1)],
                            xnT[:, T * c + 512 * n: T * c + 512 * (n + 1)],
                            start=(c == 0), stop=(c == 7))
                    return last
                tmm = p.op("pe", mk_p, waits=t_xnT + [p4_free[pp]] + wcv)
                t_projmm_last[pr] = tmm
                tev = p.op("dve", lambda e: e.tensor_scalar_add(
                    dst[:, T * sl + 512 * n: T * sl + 512 * (n + 1)],
                    P4[:, 512 * pp: 512 * (pp + 1)], bcol[:, sl:sl + 1]),
                    waits=[tmm, t_cst, tok_w[pr][2 + which]])
                p4_free[pp] = tev
                t_proj_of[pr] = tev
            pieces.append(emit)
        return pieces

    w_loads(0)
    wcv0 = w_convert(0)
    for piece in make_proj_pieces(0, wcv0):
        piece()

    for pr in range(NPAIR):
        sl = pr % 2
        t_proj = t_proj_of[pr]
        pieces = []
        if pr + 1 < NPAIR:
            w_loads(pr + 1)
            wcvn = w_convert(pr + 1)
            pieces = list(make_proj_pieces(pr + 1, wcvn))
        n_chunks_total = sum(4 * qt + 4 for qt in range(NQT))
        stride = max(1, n_chunks_total // (len(pieces) + 1)) if pieces else 10 ** 9
        chunks_done = 0

        ysl = ytb[:, :]
        prev_tcp = []
        prev_rep = t_norm_last.get(pr - 1)
        pending = None

        def flush_pending():
            nonlocal pending, prev_rep
            if pending is None:
                return
            fqt, ftcp, ftrc = pending
            pending = None

            def mk_rep(e):
                e.matmul(P4[0:64, 0:512], ones_r[0:1, 0:64], rrow[:, 0:512],
                         start=True, stop=True)
                return e.matmul(P4[0:64, 512:1024], ones_r[0:1, 0:64],
                                rrow[:, 512:1024], start=True, stop=True)
            t_rep = p.op("pe", mk_rep,
                         waits=ftrc + [t_proj, p4_free[0], p4_free[1]])
            prev_rep = t_rep
            tn = []
            for hl in range(2):
                dst = ysl[64 * hl: 64 * (hl + 1), 512 * fqt: 512 * (fqt + 1)]
                w_n = [t_rep, ftcp[hl]]
                if pr >= 1 and fqt == 0:
                    w_n.append(tok_yt_sp[pr - 1])
                tn.append(p.op("dve", lambda e, dst=dst, hl=hl: e.tensor_tensor(
                    dst, yscr[0:64, 512 * hl: 512 * (hl + 1)],
                    P4[0:64, 512 * hl: 512 * (hl + 1)], ALU.mult), waits=w_n))
            p4_free[0] = tn[0]
            p4_free[1] = tn[1]
            t_norm_last[pr] = tn[-1]

        flat = [(qt, kc) for qt in range(NQT) for kc in range(4 * qt + 4)]
        LOOK = 2
        tok_pready = {}
        qt_avdone = {}
        n_after = {}

        def emit_front(i):
            qt, kc = flat[i]
            slot = i % 2
            pslot = i % NPSL
            off = max(0, 128 * kc - 512 * qt)

            def mk_s(e, kc=kc, qt=qt, slot=slot, off=off, sl=sl):
                last = None
                for hl in range(2):
                    last = e.matmul(
                        P1[:, 1024 * slot + 512 * hl + off:
                           1024 * slot + 512 * (hl + 1)],
                        ktb[64 * hl: 64 * (hl + 1),
                            T * sl + 128 * kc: T * sl + 128 * (kc + 1)],
                        qtb[64 * hl: 64 * (hl + 1),
                            T * sl + 512 * qt + off: T * sl + 512 * (qt + 1)],
                        start=True, stop=True)
                return last
            t_s = p.op("pe", mk_s, waits=[t_proj, slot_free[slot]])

            if off == 0:
                t_e = p.op("act", lambda e, slot=slot, pslot=pslot: e.activation(
                    pTb[:, 1024 * pslot: 1024 * (pslot + 1)],
                    P1[:, 1024 * slot: 1024 * (slot + 1)],
                    AF.Exp, bias=CST(C_ATTNB), scale=CST(C_ATTNS)),
                    waits=[t_s, pT_free[pslot]])
            else:
                def mk_e(e, slot=slot, pslot=pslot, off=off):
                    e.activation(pTb[:, 1024 * pslot + off: 1024 * pslot + 512],
                                 P1[:, 1024 * slot + off: 1024 * slot + 512],
                                 AF.Exp, bias=CST(C_ATTNB), scale=CST(C_ATTNS))
                    return e.activation(
                        pTb[:, 1024 * pslot + 512 + off: 1024 * (pslot + 1)],
                        P1[:, 1024 * slot + 512 + off: 1024 * (slot + 1)],
                        AF.Exp, bias=CST(C_ATTNB), scale=CST(C_ATTNS))
                t_e = p.op("act", mk_e, waits=[t_s, pT_free[pslot]])
            slot_free[slot] = t_e

            if kc >= 4 * qt:
                def mk_m(e, pslot=pslot, off=off):
                    e.tensor_mul(
                        pTb[:, 1024 * pslot + off: 1024 * pslot + off + 128],
                        pTb[:, 1024 * pslot + off: 1024 * pslot + off + 128],
                        tri[:, :])
                    return e.tensor_mul(
                        pTb[:, 1024 * pslot + 512 + off:
                            1024 * pslot + 512 + off + 128],
                        pTb[:, 1024 * pslot + 512 + off:
                            1024 * pslot + 512 + off + 128],
                        tri[:, :])
                tok_pready[i] = p.op("dve", mk_m, waits=[t_e, t_tri])
            else:
                tok_pready[i] = t_e

        def emit_back(j):
            nonlocal pending, prev_tcp
            qt, kc = flat[j]
            nkc = 4 * qt + 4
            pslot = j % NPSL

            def mk_av(e, kc=kc, pslot=pslot, pr=pr, nkc=nkc, qt=qt):
                off = max(0, 128 * kc - 512 * qt)
                last = None
                for hl, pyt in ((0, P2a), (1, P2b)):
                    last = e.matmul(
                        pyt[0:65, off:512],
                        Vp[:, 1040 * kc + 130 * pr + 65 * hl:
                           1040 * kc + 130 * pr + 65 * (hl + 1)],
                        pTb[:, 1024 * pslot + 512 * hl + off:
                            1024 * pslot + 512 * (hl + 1)],
                        start=(kc == 0), stop=(kc == nkc - 1))
                return last
            w_av = [tok_pready[j]]
            if kc == 0:
                w_av += prev_tcp if prev_tcp else [t_norm_last.get(pr - 1)]
            t_av = p.op("pe", mk_av, waits=w_av)
            pT_free[pslot] = t_av
            if kc == min(5, nkc - 1):
                flush_pending()
            if kc == nkc - 1:
                w_cp0 = [t_av, t_norm_last.get(pr)] + ([t_norm_last.get(pr - 1)] if qt == 0 else [])
                tcp = []
                for hl, pyt in ((0, P2a), (1, P2b)):
                    tcp.append(p.op("dve", lambda e, hl=hl, pyt=pyt: e.tensor_copy(
                        yscr[0:64, 512 * hl: 512 * (hl + 1)], pyt[0:64, :]),
                        waits=w_cp0))
                t_cps = p.op("dve", lambda e: e.tensor_copy(
                    yscr[64:65, 0:512], P2a[64:65, 0:512]), waits=w_cp0)
                t_cps2 = p.op("dve", lambda e: e.tensor_copy(
                    yscr[64:65, 512:1024], P2b[64:65, 0:512]), waits=w_cp0)
                trc = []

                def mk_rc(e, hl):
                    with nc.allow_low_precision(reason="bf16 softmax-"
                                                "denominator reciprocal"):
                        return e.reciprocal(rrow[:, 512 * hl: 512 * (hl + 1)],
                                            yscr[64:65, 512 * hl: 512 * (hl + 1)])
                trc.append(p.op("dve", lambda e: mk_rc(e, 0),
                                waits=[t_cps, prev_rep]))
                trc.append(p.op("dve", lambda e: mk_rc(e, 1),
                                waits=[t_cps2, prev_rep]))
                pending = (qt, tcp, trc)
                prev_tcp = [t_cps2]

        for i in range(len(flat) + LOOK):
            if i < len(flat):
                emit_front(i)
            j = i - LOOK
            if j >= 0:
                emit_back(j)
                chunks_done += 1
                if pieces and chunks_done % stride == 0:
                    pieces.pop(0)()
        flush_pending()
        for piece in pieces:
            piece()
        pieces = []

        tok_yt_sp[pr] = p.dma("sp", lambda e, pr=pr, ysl=ysl: e.dma_start(
            out=yt_o[128 * pr: 128 * (pr + 1), :], in_=ysl),
            waits=[t_norm_last[pr]])
    p.op("sp", lambda e: None,
         waits=p.qlast("sp") + p.qlast("act") + p.qlast("pop"), inc=0)

    p.emit()
    esP.close()
    esB.close()
    es.close()
    return nc


def build_l2():
    nc = bass.Bass("TRN2", target_bir_lowering=False)
    xtp_i = nc.dram_tensor("xtp", [8 * 256, CDIM], F32, kind="ExternalInput").ap()
    wex_i = nc.dram_tensor("wex", [8 * CDIM, CDIM], F32, kind="ExternalInput").ap()
    bex_i = nc.dram_tensor("bex", [1, 8 * CDIM], F32, kind="ExternalInput").ap()
    cst_i = nc.dram_tensor("consts", [128, NCONST], F32, kind="ExternalInput").ap()
    ytl_o = nc.dram_tensor("ytl", [256, CDIM], BF16, kind="ExternalOutput").ap()
    xtn_sp = nc.dram_tensor("xtn_sp", [256, CDIM], BF16).ap()

    p = Prog(nc)
    es = contextlib.ExitStack()
    cst = _alloc(es, nc, "cst", [128, NCONST], F32)
    ones_r = _alloc(es, nc, "ones_r", [1, 128], BF16)
    xts = _alloc(es, nc, "xts", [128, 16 * CDIM], F32)
    acc = _alloc(es, nc, "acc", [128, 2 * CDIM], F32)
    sqs = _alloc(es, nc, "sqs", [128, CDIM], F32)
    stt = _alloc(es, nc, "stt", [128, 2 * 12], F32)
    xtn = _alloc(es, nc, "xtn", [128, 2 * CDIM], BF16)
    xtnT = _alloc(es, nc, "xtnT", [128, 8 * 256], BF16)
    wxf = _alloc(es, nc, "wxf", [128, 2 * 8 * CDIM], F32)
    wxb = _alloc(es, nc, "wxb", [128, 2 * 8 * CDIM], BF16)
    bexb = _alloc(es, nc, "bexb", [1, 8 * CDIM], BF16)
    ystg = _alloc(es, nc, "ystg", [32, 2 * CDIM], BF16)
    esP = contextlib.ExitStack()
    PY = _psum(esP, nc, "PY", [128, 1024])

    def CST(j):
        return cst[:, j:j + 1]

    t_cst = p.dma("sp", lambda e: e.dma_start(out=cst[:, :], in_=cst_i))
    t_ones = p.op("dve", lambda e: e.memset(ones_r[:, :], 1.0))
    t_bex = p.dma("sp", lambda e: e.dma_start(out=wxf[0:1, 0:8 * CDIM], in_=bex_i))
    t_bexb = p.op("dve", lambda e: e.tensor_copy(bexb[:, :], wxf[0:1, 0:8 * CDIM]),
                  waits=[t_bex])

    t_ld = []
    for r in range(8):
        t_ld.append(p.dma("sp", lambda e, r=r: e.dma_start(
            out=xts[:, 2 * CDIM * r: 2 * CDIM * (r + 1)].rearrange(
                "p (c m) -> p c m", c=2),
            in_=xtp_i[256 * r: 256 * (r + 1), :].rearrange("(c p) m -> p c m", p=128))))

    tok_xtn = []
    for i in range(2):
        dst = acc[:, CDIM * i: CDIM * (i + 1)]
        tk = p.op("dve", lambda e, dst=dst, i=i: e.tensor_copy(
            dst, xts[:, CDIM * i: CDIM * (i + 1)]), waits=[t_ld[0]])
        for r in range(1, 8):
            src = xts[:, 2 * CDIM * r + CDIM * i: 2 * CDIM * r + CDIM * (i + 1)]
            tk = p.op("dve", lambda e, dst=dst, src=src: e.tensor_add(dst, dst, src),
                      waits=[t_ld[r], tk])
        tk = p.op("dve", lambda e, dst=dst: e.tensor_scalar(
            dst, dst, CST(C_CLIPHI), CST(C_CLIPLO), ALU.min, ALU.max),
            waits=[tk, t_cst])
        xdst = xtn[:, CDIM * i: CDIM * (i + 1)]
        w_ln = [tk] + ([tok_xtn[0]] if i == 1 else [])
        tok_xtn.append(_ln_chain(p, CST, dst, stt[:, 12 * i: 12 * (i + 1)],
                                 sqs[:, :], xdst, AF.Tanh, w_ln))

    t_sp = []
    for i in range(2):
        t_sp.append(p.dma("sp", lambda e, i=i: e.dma_start(
            out=xtn_sp[128 * i: 128 * (i + 1), :],
            in_=xtn[:, CDIM * i: CDIM * (i + 1)]), waits=[tok_xtn[i]]))
    t_T = []
    for dc in range(8):
        t_T.append(p.dma("act", lambda e, dc=dc: e.dma_start(
            out=xtnT[:, 256 * dc: 256 * (dc + 1)],
            in_=xtn_sp[:, 128 * dc: 128 * (dc + 1)], transpose=True),
            waits=[t_sp[0], t_sp[1]]))

    tok_wx = {}

    def wx_load(el, extra=()):
        w = wxf[:, 8 * CDIM * (el % 2): 8 * CDIM * (el % 2 + 1)]
        tl = p.dma("pop", lambda e, el=el, w=w: e.dma_start(
            out=w.rearrange("p (c m) -> p c m", c=8),
            in_=wex_i[CDIM * el: CDIM * (el + 1), :].rearrange(
                "(c p) m -> p c m", p=128)), waits=list(extra))
        tcv = p.op("dve", lambda e, el=el, w=w: e.tensor_copy(
            wxb[:, 8 * CDIM * (el % 2): 8 * CDIM * (el % 2 + 1)], w),
            waits=[tl] + list(extra))
        tok_wx[el] = (tl, tcv)

    wx_load(0, extra=[t_bexb])
    wx_load(1)
    tok_ev = {}
    tok_st = {}
    prev_mm = None
    for el in range(8):
        def mk_ex(e, el=el):
            last = None
            for hh in range(2):
                for dc in range(8):
                    last = e.matmul(
                        PY[0:32, 512 * hh: 512 * (hh + 1)],
                        xtnT[:, 256 * dc + 32 * el: 256 * dc + 32 * (el + 1)],
                        wxb[:, 8 * CDIM * (el % 2) + CDIM * dc + 512 * hh:
                            8 * CDIM * (el % 2) + CDIM * dc + 512 * (hh + 1)],
                        start=(dc == 0), stop=False)
                last = e.matmul(PY[0:32, 512 * hh: 512 * (hh + 1)],
                                ones_r[0:1, 0:32],
                                bexb[0:1, CDIM * el + 512 * hh: CDIM * el + 512 * (hh + 1)],
                                start=False, stop=True)
            return last
        w_ex = t_T + [tok_wx[el][1], t_bexb, t_ones,
                tok_ev.get(el - 1)]
        t_mm = p.op("pe", mk_ex, waits=w_ex)
        prev_mm = t_mm
        dst = ystg[:, CDIM * (el % 2): CDIM * (el % 2 + 1)]
        w_ev = [t_mm] + ([tok_st[el - 2]] if el >= 2 else [])
        tok_ev[el] = p.op("dve", lambda e, dst=dst: e.tensor_copy(dst, PY[0:32, :]),
                          waits=w_ev)
        tok_st[el] = p.dma("sp", lambda e, el=el, dst=dst: e.dma_start(
            out=ytl_o[32 * el: 32 * (el + 1), :], in_=dst), waits=[tok_ev[el]])
        if el + 2 < 8:
            wx_load(el + 2, extra=[tok_wx[el][1], t_mm])

    p.op("sp", lambda e: None,
         waits=p.qlast("sp") + p.qlast("act") + p.qlast("pop"), inc=0)
    p.emit()
    esP.close()
    es.close()
    return nc


def build_l3(T=4096):
    TC = T // 128
    NB = T // 512
    nc = bass.Bass("TRN2", target_bir_lowering=False)
    x_i = nc.dram_tensor("x", [T, CDIM], F32, kind="ExternalInput").ap()
    yt_i = nc.dram_tensor("yt", [CDIM, T], BF16, kind="ExternalInput").ap()
    csp_i = nc.dram_tensor("c_sp", [T, EDIM], BF16, kind="ExternalInput").ap()
    wo_i = nc.dram_tensor("wo", [CDIM, CDIM], F32, kind="ExternalInput").ap()
    bor_i = nc.dram_tensor("bor", [1, CDIM], F32, kind="ExternalInput").ap()
    ytl_i = nc.dram_tensor("ytl", [EDIM, CDIM], BF16, kind="ExternalInput").ap()
    out_o = nc.dram_tensor("out", [T, CDIM], F32, kind="ExternalOutput").ap()

    p = Prog(nc)
    es = contextlib.ExitStack()
    ones_r = _alloc(es, nc, "ones_r", [1, 128], BF16)
    wof = _alloc(es, nc, "wof", [128, 8 * CDIM], F32)
    wob = _alloc(es, nc, "wob", [128, 8 * CDIM], BF16)
    bof = _alloc(es, nc, "bof", [1, CDIM], F32)
    bob = _alloc(es, nc, "bob", [1, CDIM], BF16)
    ytl = _alloc(es, nc, "ytl_sb", [128, 16 * CDIM], BF16)
    ytsb = _alloc(es, nc, "ytsb", [128, 8 * T], BF16)
    ctb = _alloc(es, nc, "ctb", [128, 2 * 16 * 512], BF16)
    xb = _alloc(es, nc, "xb", [128, 2 * CDIM], F32)
    ob = _alloc(es, nc, "ob", [128, 2 * CDIM], F32)
    esP = contextlib.ExitStack()
    PO = _psum(esP, nc, "PO", [128, 2048])

    t_ones = p.op("dve", lambda e: e.memset(ones_r[:, :], 1.0))
    t_wo = p.dma("sp", lambda e: e.dma_start(
        out=wof[:, :].rearrange("p (c m) -> p c m", c=8),
        in_=wo_i.rearrange("(c p) m -> p c m", p=128)))
    t_wob = p.op("dve", lambda e: e.tensor_copy(wob[:, :], wof[:, :]), waits=[t_wo])
    t_bo = p.dma("sp", lambda e: e.dma_start(out=bof[:, :], in_=bor_i))
    t_bob = p.op("dve", lambda e: e.tensor_copy(bob[:, :], bof[:, :]), waits=[t_bo])
    t_ytl = p.dma("sp", lambda e: e.dma_start(
        out=ytl[:, :].rearrange("p (c m) -> p c m", c=16),
        in_=ytl_i.rearrange("(c p) m -> p c m", p=128)))
    t_yt = []
    for c in range(8):
        t_yt.append(p.dma("sp", lambda e, c=c: e.dma_start(
            out=ytsb[:, T * c: T * (c + 1)], in_=yt_i[128 * c: 128 * (c + 1), :])))

    tok_ct = [None] * NB
    tok_blk = [None] * NB
    for b in range(NB):
        sl = b % 2
        tks = []
        for ec in range(16):
            w_ct = [tok_blk[b - 2]] if b >= 2 else []
            tks.append(p.dma("act", lambda e, b=b, sl=sl, ec=ec: e.dma_start(
                out=ctb[:, 8192 * sl + 512 * ec: 8192 * sl + 512 * (ec + 1)],
                in_=csp_i[512 * b: 512 * (b + 1), 128 * ec: 128 * (ec + 1)],
                transpose=True), waits=w_ct))
        tok_ct[b] = tks

        for tt in range(4 * b, min(4 * b + 4, TC)):
            xsl = xb[:, CDIM * (tt % 2): CDIM * (tt % 2 + 1)]
            osl = ob[:, CDIM * (tt % 2): CDIM * (tt % 2 + 1)]
            w_x = [tok_ev[(tt - 2, 0)], tok_ev[(tt - 2, 1)]] if tt >= 2 else []
            t_x = p.dma("sp", lambda e, xsl=xsl, tt=tt: e.dma_start(
                out=xsl, in_=x_i[128 * tt: 128 * (tt + 1), :]), waits=w_x)
            if tt == 0:
                tok_ev = {}
                tok_stt = {}
            t_mm_last = None
            for hh in range(2):
                pb = PO[:, 1024 * (tt % 2) + 512 * hh:
                        1024 * (tt % 2) + 512 * (hh + 1)]

                def mk_o(e, tt=tt, hh=hh, pb=pb, sl=sl):
                    last = None
                    for c in range(8):
                        last = e.matmul(
                            pb, ytsb[:, T * c + 128 * tt: T * c + 128 * (tt + 1)],
                            wob[:, CDIM * c + 512 * hh: CDIM * c + 512 * (hh + 1)],
                            start=(c == 0), stop=False)
                    for ec in range(16):
                        last = e.matmul(
                            pb,
                            ctb[:, 8192 * sl + 512 * ec + 128 * (tt % 4):
                                8192 * sl + 512 * ec + 128 * (tt % 4) + 128],
                            ytl[:, CDIM * ec + 512 * hh: CDIM * ec + 512 * (hh + 1)],
                            start=False, stop=False)
                    return e.matmul(pb, ones_r[:, :], bob[:, 512 * hh: 512 * (hh + 1)],
                                    start=False, stop=True)
                w_o = [t_wob, t_bob, t_ytl, t_ones] + t_yt + tok_ct[b]
                if tt >= 2:
                    w_o.append(tok_ev[(tt - 2, hh)])
                t_mm = p.op("pe", mk_o, waits=w_o)
                t_mm_last = t_mm
                w_ev = [t_mm, t_x] + ([tok_stt[tt - 2]] if tt >= 2 else [])
                tok_ev[(tt, hh)] = p.op(
                    "dve", lambda e, osl=osl, pb=pb, hh=hh, xsl=xsl: e.tensor_add(
                        osl[:, 512 * hh: 512 * (hh + 1)], pb,
                        xsl[:, 512 * hh: 512 * (hh + 1)]), waits=w_ev)
            tok_stt[tt] = p.dma("sp", lambda e, osl=osl, tt=tt: e.dma_start(
                out=out_o[128 * tt: 128 * (tt + 1), :], in_=osl),
                waits=[tok_ev[(tt, 0)], tok_ev[(tt, 1)]])
            if tt % 4 == 3:
                tok_blk[tt // 4] = t_mm_last

    p.op("sp", lambda e: None, waits=p.qlast("sp") + p.qlast("act"), inc=0)
    p.emit()
    esP.close()
    es.close()
    return nc


_NC_CACHE = {}


def _get(builder, key, *args):
    if key not in _NC_CACHE:
        _NC_CACHE[key] = builder(*args)
    return _NC_CACHE[key]


def kernel(L0_attn_x, L0_attn_Wq, L0_attn_bq, L0_attn_Wk, L0_attn_bk,
           L0_attn_Wv, L0_attn_bv, L0_attn_Wo, L0_attn_bo,
           L0_ffn_phi_w, L0_ffn_phi_b, L0_ffn_expert_w, L0_ffn_expert_b):
    T = L0_attn_x.shape[1]
    f32 = np.float32
    cc = np.ascontiguousarray
    consts = np.tile(np.asarray(CONST_VALS, f32), (128, 1))
    tri = np.asarray((np.arange(128)[:, None] <= np.arange(128)[None, :]),
                     dtype=ml_dtypes.bfloat16)

    x = np.asarray(L0_attn_x, f32)
    phiw = cc(np.asarray(L0_ffn_phi_w, f32))
    phib = cc(np.asarray(L0_ffn_phi_b, f32).reshape(1, EDIM))
    wq = cc(np.asarray(L0_attn_Wq, f32))
    wk = cc(np.asarray(L0_attn_Wk, f32))
    wv = cc(np.asarray(L0_attn_Wv, f32))
    wo = cc(np.asarray(L0_attn_Wo, f32))
    bqc = cc(np.asarray(L0_attn_bq, f32).reshape(CDIM, 1))
    bkc = cc(np.asarray(L0_attn_bk, f32).reshape(CDIM, 1))
    bvr = cc(np.asarray(L0_attn_bv, f32).reshape(1, CDIM))
    bor = cc(np.asarray(L0_attn_bo, f32).reshape(1, CDIM))
    wex = np.asarray(L0_ffn_expert_w, f32)
    bex = np.asarray(L0_ffn_expert_b, f32)

    nc1 = _get(build_l1, ("l1", T), T)
    in1 = [dict(x=cc(x[b]), phiw=phiw, phib=phib, wq=wq, wk=wk, wv=wv,
                bqc=bqc, bkc=bkc, bvr=bvr, tri=tri, consts=consts)
           for b in range(NCORES)]
    r1 = run_bass_kernel_spmd(nc1, in1, core_ids=list(range(NCORES))).results

    nc2 = _get(build_l2, ("l2",))
    in2 = []
    for r in range(NCORES):
        xtp = cc(np.stack([np.asarray(r1[b]["xt_part"])[256 * r: 256 * (r + 1), :]
                           for b in range(NCORES)]).reshape(8 * 256, CDIM))
        in2.append(dict(xtp=xtp,
                        wex=cc(wex[8 * r: 8 * (r + 1)].reshape(8 * CDIM, CDIM)),
                        bex=cc(bex[8 * r: 8 * (r + 1)].reshape(1, 8 * CDIM)),
                        consts=consts))
    r2 = run_bass_kernel_spmd(nc2, in2, core_ids=list(range(NCORES))).results

    ytl_full = cc(np.concatenate([np.asarray(r2[r]["ytl"]) for r in range(NCORES)],
                                 axis=0))
    nc3 = _get(build_l3, ("l3", T), T)
    in3 = [dict(x=cc(x[b]), yt=cc(np.asarray(r1[b]["yt_sp"])),
                c_sp=cc(np.asarray(r1[b]["c_sp"])), wo=wo, bor=bor, ytl=ytl_full)
           for b in range(NCORES)]
    r3 = run_bass_kernel_spmd(nc3, in3, core_ids=list(range(NCORES))).results

    return np.stack([np.asarray(r3[b]["out"], f32) for b in range(NCORES)])
